# version 14
# baseline (speedup 1.0000x reference)
import contextlib
import numpy as np
import concourse.bass as bass
import concourse.mybir as mybir
from concourse.bass_utils import run_bass_kernel_spmd

F32 = mybir.dt.float32
BF16 = mybir.dt.bfloat16
AF = mybir.ActivationFunctionType
ALU = mybir.AluOpType
AX = mybir.AxisListType

NCORES = 8
D = 2048
TOK = 2048
NT = TOK // 128
DFF = 8192
NJ = DFF // 512
H = 16
DH = 128
DQ = 192
DC = 512
SCALER = 1.0 / float(np.sqrt(DQ))
EPS = 1e-5

ENGINES = ("tensor", "vector", "scalar", "gpsimd", "sync")


class _Op:
    __slots__ = ("idx", "engine", "emit_fn", "deps", "inc_amt", "cum_after",
                 "tag", "waits", "barrier")

    def __init__(self, idx, engine, emit_fn, deps, inc_amt, tag, barrier):
        self.idx = idx
        self.engine = engine
        self.emit_fn = emit_fn
        self.deps = deps
        self.inc_amt = inc_amt
        self.cum_after = None
        self.tag = tag
        self.barrier = barrier


class Sched:

    def __init__(self, nc):
        self.nc = nc
        self.ops = []
        self.sems = {}
        self._writers = {}
        self._epoch_read = {}
        self._readers_since = {}
        self._fence_ops = []

    def alloc_sems(self, stack):
        for e in ENGINES:
            self.sems[e] = stack.enter_context(self.nc.semaphore(f"sched_{e}"))

    def op(self, engine, emit_fn, reads=(), writes=(), deps=(), dma=False,
           n_dma=1, tag="", barrier=False):
        dd = set(d.idx for d in deps)
        for f in self._fence_ops:
            dd.add(f.idx)
        for r in reads:
            for w in self._writers.get(r, ()):
                dd.add(w.idx)
        for w in writes:
            for lw in self._writers.get(w, ()):
                dd.add(lw.idx)
            for rd in self._readers_since.get(w, ()):
                dd.add(rd.idx)
        o = _Op(len(self.ops), engine, emit_fn,
                [self.ops[i] for i in sorted(dd)],
                (16 * n_dma) if dma else 1, tag, barrier)
        self.ops.append(o)
        for r in reads:
            self._readers_since.setdefault(r, []).append(o)
            self._epoch_read[r] = True
        for w in writes:
            if self._epoch_read.get(w):
                self._writers[w] = [o]
                self._epoch_read[w] = False
            else:
                self._writers.setdefault(w, []).append(o)
            self._readers_since[w] = []
        return o

    def fence(self):
        last = {}
        for o in self.ops:
            last[o.engine] = o
        self._fence_ops = list(last.values())

    def emit(self, block):
        cum = {e: 0 for e in ENGINES}
        for o in self.ops:
            cum[o.engine] += o.inc_amt
            o.cum_after = cum[o.engine]
        per_engine = {e: [o for o in self.ops if o.engine == e] for e in ENGINES}
        for e in ENGINES:
            have = {x: 0 for x in ENGINES}
            for o in per_engine[e]:
                o.waits = []
                byeng = {}
                for d in o.deps:
                    if (d.engine == e and d.inc_amt == 1 and o.inc_amt == 1
                            and not o.barrier):
                        continue
                    byeng[d.engine] = max(byeng.get(d.engine, 0), d.cum_after)
                for en, vv in sorted(byeng.items()):
                    if vv > have[en]:
                        o.waits.append((en, vv))
                        have[en] = vv

        def mk(e):
            def body(eng):
                for o in per_engine[e]:
                    for en, vv in o.waits:
                        eng.wait_ge(self.sems[en], vv)
                    ret = o.emit_fn(eng)
                    insts = ret if isinstance(ret, (list, tuple)) else [ret]
                    insts = [i for i in insts if i is not None]
                    if not insts:
                        raise RuntimeError(f"op {o.idx} ({o.tag}) emitted nothing")
                    if o.inc_amt == 1:
                        insts[-1].then_inc(self.sems[e], 1)
                    else:
                        assert len(insts) * 16 == o.inc_amt, (o.tag, len(insts))
                        for i in insts:
                            i.then_inc(self.sems[e], 16)
            return body

        for e in ENGINES:
            if per_engine[e]:
                getattr(block, e)(mk(e))
        fin = dict(cum)

        def tail(eng):
            for e in ENGINES:
                if fin[e] and e != "sync":
                    eng.wait_ge(self.sems[e], fin[e])
        block.sync(tail)


def build_nc(debug=()):
    nc = bass.Bass("TRN2", target_bir_lowering=False)
    ap_in = {}
    shapes = {
        "x": [TOK, D], "Wc": [D, DC], "bc": [DC], "Wcq": [DC, D],
        "bcq": [D], "Wqr": [DC, 1024], "bqr": [1024], "Wckv": [D, DC],
        "bckv": [DC], "Wck": [DC, D], "bck": [D], "Wkr": [D, 1024],
        "bkr": [1024], "Wv": [DC, D], "bv": [D], "Wo": [D, D], "bo": [D],
        "w_gate": [D, DFF], "w_up": [D, DFF], "w_down": [DFF, D],
        "ident": [128, 128], "maskneg": [128, 128],
        "ropecos": [TOK, 512], "ropesin": [TOK, 512],
    }
    for n, s in shapes.items():
        ap_in[n] = nc.dram_tensor(n, s, F32, kind="ExternalInput").ap()
    out_h = nc.dram_tensor("out", [TOK, D], F32, kind="ExternalOutput").ap()
    dbg_h = {}
    for dn in debug:
        dbg_h[dn] = nc.dram_tensor("dbg_" + dn, [TOK, D], F32,
                                   kind="ExternalOutput").ap()

    hsp_d = nc.dram_tensor("hsp_scr", [DFF, TOK], BF16, kind="Internal")
    vscr_d = nc.dram_tensor("v_scr", [TOK, D], BF16, kind="Internal")
    qk_d = nc.dram_tensor("qk_scr", [NT, 384, TOK], BF16, kind="Internal")
    xm1_d = nc.dram_tensor("xmid1", [TOK, D], F32, kind="Internal")
    xm2_d = nc.dram_tensor("xmid2", [TOK, D], F32, kind="Internal")
    wgscr = nc.dram_tensor("wg_bf", [NJ, 128, 8192], BF16, kind="Internal")
    wuscr = nc.dram_tensor("wu_bf", [NJ, 128, 8192], BF16, kind="Internal")
    wdscr = nc.dram_tensor("wd_bf", [4, 128, 64 * 512], BF16, kind="Internal")

    st = contextlib.ExitStack()
    with st:
        sbt = lambda name, shape, dt: st.enter_context(
            nc.sbuf_tensor(name, shape, dt))
        ident = sbt("sb_ident", [128, 128], BF16)
        maskneg = sbt("sb_maskneg", [128, 128], F32)
        ones_row = sbt("ones_row", [1, 128], BF16)
        stats = sbt("stats", [128, 16], F32)
        flsc = sbt("flsc", [128, 16], F32)

        sch = Sched(nc)
        sch.alloc_sems(st)
        block = st.enter_context(nc.Block())

        pmm = [st.enter_context(nc.psum_tensor(f"pmm{i}", [128, 512], F32))
               for i in range(4)]
        ptr = [st.enter_context(nc.psum_tensor(f"ptr{i}", [128, 1024], BF16))
               for i in range(2)]
        pmm_i = [0]
        ptr_i = [0]

        def next_pmm():
            i = pmm_i[0] % 4
            pmm_i[0] += 1
            return pmm[i], f"pmm{i}"

        def next_ptr():
            i = ptr_i[0] % 2
            ptr_i[0] += 1
            return ptr[i], f"ptr{i}"

        evac_i = [0]

        def evac_to(dst_ap, dst_key, ps, ps_key, nparts=128, tg="e"):
            evac_i[0] += 1
            if evac_i[0] % 2:
                sch.op("scalar", lambda s: s.copy(dst_ap, ps[0:nparts, 0:128]),
                       reads=[ps_key], writes=[dst_key], tag=tg)
            else:
                sch.op("vector", lambda v: v.tensor_copy(dst_ap, ps[0:nparts, 0:128]),
                       reads=[ps_key], writes=[dst_key], tag=tg)

        def transpose_to(src_ap, src_key, dst_ap, dst_key, nparts=128, tg="tr"):
            ps, ps_key = next_ptr()
            sch.op("tensor", lambda te: te.transpose(
                ps[0:nparts, 0:128], src_ap, ident[:, :]),
                reads=[src_key, "ident"], writes=[ps_key], tag=f"{tg}t")
            evac_to(dst_ap, dst_key, ps, ps_key, nparts, tg=f"{tg}e")

        def transpose4_to(srcs, src_key, dst_ap, dst_key, nparts=128, tg="t4"):
            ps, ps_key = next_ptr()
            sch.op("tensor", lambda te, ss=list(srcs): [
                te.transpose(ps[0:nparts, k * 128:(k + 1) * 128], ss[k],
                             ident[:, :]) for k in range(4)],
                reads=[src_key, "ident"], writes=[ps_key], tag=f"{tg}t")
            src3 = ps[0:nparts, 0:512].rearrange("p (k n) -> p k n", n=128)
            evac_i[0] += 1
            if evac_i[0] % 2:
                sch.op("scalar", lambda s: s.copy(dst_ap, src3),
                       reads=[ps_key], writes=[dst_key], tag=f"{tg}e")
            else:
                sch.op("vector", lambda v: v.tensor_copy(dst_ap, src3),
                       reads=[ps_key], writes=[dst_key], tag=f"{tg}e")

        sch.op("gpsimd", lambda g: [
            g.dma_start(ident[:, :], ap_in["ident"][:, :]),
            g.dma_start(maskneg[:, :], ap_in["maskneg"][:, :])],
            writes=["ident", "maskneg"], dma=True, n_dma=2, tag="consts")
        sch.op("vector", lambda v: v.memset(ones_row[:, :], 1.0),
               writes=["ones"], tag="ones")

        def ln_tile(xt, xt_key, xn_bf, xn_key, sq_scr, tg, par=0):
            po = 8 * par
            s1 = stats[:, po + 0:po + 1]
            negmean = stats[:, po + 1:po + 2]
            vsum = stats[:, po + 2:po + 3]
            va = stats[:, po + 3:po + 4]
            rva = stats[:, po + 4:po + 5]
            lntmp = stats[:, po + 5:po + 6]
            rstd = stats[:, po + 6:po + 7]
            sch.op("vector", lambda v: v.reduce_sum(s1, xt[:, :], axis=AX.X),
                   reads=[xt_key], writes=[f"st_s1{par}"], tag=f"{tg}ln_r")
            sch.op("vector", lambda v: [
                v.tensor_scalar_mul(negmean, s1, -1.0 / D),
                v.tensor_copy(flsc[:, 8*par:8*par+1], negmean)],
                reads=[f"st_s1{par}"], writes=[f"st_nm{par}"], barrier=True, tag=f"{tg}ln_nm")
            sch.op("scalar", lambda s: [
                s.activation(sq_scr[:, :], xt[:, :], AF.Square,
                             bias=negmean, scale=1.0, accum_out=vsum),
                s.copy(flsc[:, 8*par+1:8*par+2], vsum)],
                reads=[xt_key, f"st_nm{par}"], writes=[f"st_vs{par}", f"sq{par}"],
                tag=f"{tg}ln_sq")
            sch.op("vector", lambda v: v.tensor_scalar(
                va, vsum, 1.0 / D, EPS, ALU.mult, ALU.add),
                reads=[f"st_vs{par}"], writes=[f"st_va{par}"], tag=f"{tg}ln_va")
            sch.op("vector", lambda v: [
                v.reciprocal(rva, va),
                v.tensor_copy(flsc[:, 8*par+2:8*par+3], rva)],
                reads=[f"st_va{par}"], writes=[f"st_rva{par}"], barrier=True,
                tag=f"{tg}ln_rv")
            sch.op("scalar", lambda s: s.activation(lntmp, rva, AF.Ln),
                   reads=[f"st_rva{par}"], writes=[f"st_lt{par}"], tag=f"{tg}ln_ln")
            sch.op("scalar", lambda s: [
                s.activation(rstd, lntmp, AF.Exp, scale=0.5),
                s.copy(flsc[:, 8*par+3:8*par+4], rstd)],
                reads=[f"st_lt{par}"], writes=[f"st_rs{par}"], barrier=True,
                tag=f"{tg}ln_ex")
            sch.op("vector", lambda v: v.tensor_scalar(
                xn_bf[:, :], xt[:, :], negmean, rstd, ALU.add, ALU.mult),
                reads=[xt_key, f"st_nm{par}", f"st_rs{par}"], writes=[xn_key],
                barrier=True, tag=f"{tg}ln_o")

        def ln_phase(x_src_ap, x_src_key, ln_t, ln_key, scope, sname):
            sbl = lambda n, sh, dt: scope.enter_context(
                nc.sbuf_tensor(sname + n, sh, dt))
            xt2 = [sbl(f"xt{i}", [128, D], F32) for i in range(2)]
            xnb2 = [sbl(f"xnb{i}", [128, D], BF16) for i in range(2)]
            sq2 = [sbl(f"sq{i}", [128, D], BF16) for i in range(2)]
            for t in range(NT):
                par = t % 2
                xt, xk = xt2[par], f"{sname}xt{par}"
                xnb = xnb2[par]
                sch.op("sync", lambda sy, tt=t, b=xt: sy.dma_start(
                    b[:, :], x_src_ap[tt * 128:(tt + 1) * 128, :]),
                    reads=[f"{x_src_key}:{t}"], writes=[xk], dma=True,
                    tag=f"{sname}ldx{t}")
                ln_tile(xt, xk, xnb, f"{sname}xnb{par}", sq2[par],
                        f"{sname}{t}", par=par)
                for k0 in range(0, 16, 4):
                    transpose4_to(
                        [xnb[:, (k0 + k) * 128:(k0 + k + 1) * 128]
                         for k in range(4)],
                        f"{sname}xnb{par}",
                        bass.AP(ln_t, ((k0 * NT + t) * 128),
                                [[NT * 16 * 128, 128], [NT * 128, 4], [1, 128]]),
                        f"{ln_key}:{t}", tg=f"{sname}lt{t}_{k0}")

        def stage_ffn(x_src_ap, x_src_key, x_out_ap, x_out_key, sname,
                      wmode):
            with contextlib.ExitStack() as s_ln:
                ln_t = s_ln.enter_context(
                    nc.sbuf_tensor(sname + "ln_t", [128, NT * 16 * 128], BF16))
                ln_key = sname + "ln_t"
                ln_phase(x_src_ap, x_src_key, ln_t, ln_key, s_ln, sname + "p1")

                if True:
                    sp = s_ln
                    sbl = lambda n, sh, dt: sp.enter_context(
                        nc.sbuf_tensor(sname + n, sh, dt))
                    wg2 = [sbl(f"wg{i}", [128, 16 * 512], BF16) for i in range(2)]
                    wu2 = [sbl(f"wu{i}", [128, 16 * 512], BF16) for i in range(2)]
                    hst2 = [sbl(f"hst{i}", [128, 4 * TOK], BF16) for i in range(1)]
                    hs2 = [sbl(f"hs{i}", [128, 512], F32) for i in range(2)]
                    hbuf2 = [sbl(f"hbuf{i}", [128, 512], F32) for i in range(2)]
                    hsq2 = [sbl(f"hsq{i}", [128, 512], F32) for i in range(2)]
                    tp2 = [sbl(f"tp{i}", [128, 256], F32) for i in range(2)]
                    tq2 = [sbl(f"tq{i}", [128, 256], F32) for i in range(2)]
                    tthr2 = [sbl(f"tthr{i}", [128, 128], F32) for i in range(2)]
                    tmn2 = [sbl(f"tmn{i}", [128, 128], F32) for i in range(2)]
                    tmask2 = [sbl(f"tmask{i}", [128, 512], BF16) for i in range(2)]
                    hsp_bf2 = [sbl(f"hsp_bf{i}", [128, 512], BF16) for i in range(2)]

                    def w_load(j, wbuf, wkey, src, scr):
                        if wmode == "read":
                            sch.op("sync", lambda sy: sy.dma_start(
                                wbuf[:, :],
                                bass.AP(scr, j * 128 * 8192,
                                        [[8192, 128], [1, 8192]])),
                                writes=[wkey], dma=True, tag=f"{sname}w{j}")
                        else:
                            sch.op("gpsimd", lambda g: g.dma_start(
                                wbuf[:, :].rearrange("p (k n) -> p k n", n=512),
                                src[:, j * 512:(j + 1) * 512].rearrange(
                                    "(k p) n -> p k n", p=128)),
                                writes=[wkey], dma=True, tag=f"{sname}w{j}")
                            sch.op("sync", lambda sy: sy.dma_start(
                                bass.AP(scr, j * 128 * 8192,
                                        [[8192, 128], [1, 8192]]),
                                wbuf[:, :]),
                                reads=[wkey], dma=True, tag=f"{sname}wc{j}")

                    pend = []
                    for j in range(NJ):
                        wg, wgk = wg2[j % 2], f"{sname}wg{j % 2}"
                        wu, wuk = wu2[j % 2], f"{sname}wu{j % 2}"
                        hst, hstk = hst2[0], f"{sname}hst0"
                        w_load(j, wg, wgk, ap_in["w_gate"], wgscr)
                        w_load(j, wu, wuk, ap_in["w_up"], wuscr)
                        for t in range(NT):
                            par = (j * NT + t) % 2
                            hs, hbuf, hsq = hs2[par], hbuf2[par], hsq2[par]
                            tp, tq, tthr = tp2[par], tq2[par], tthr2[par]
                            tmn, tmask, hsp_bf = tmn2[par], tmask2[par], hsp_bf2[par]
                            hsk, hbk, hqk = f"hs{par}", f"hbuf{par}", f"hsq{par}"
                            thk, spk = f"tthr{par}", f"hsp_bf{par}"
                            psg, psg_k = next_pmm()
                            psu, psu_k = next_pmm()
                            sch.op("tensor", lambda te, tt=t, w=wg, p=psg: [
                                te.matmul(p[:, :],
                                          ln_t[:, (k * NT + tt) * 128:
                                               (k * NT + tt + 1) * 128],
                                          w[:, k * 512:(k + 1) * 512],
                                          start=(k == 0), stop=(k == 15))
                                for k in range(16)],
                                reads=[f"{ln_key}:{t}", wgk], writes=[psg_k],
                                tag=f"{sname}g{j}_{t}")
                            sch.op("tensor", lambda te, tt=t, w=wu, p=psu: [
                                te.matmul(p[:, :],
                                          ln_t[:, (k * NT + tt) * 128:
                                               (k * NT + tt + 1) * 128],
                                          w[:, k * 512:(k + 1) * 512],
                                          start=(k == 0), stop=(k == 15))
                                for k in range(16)],
                                reads=[f"{ln_key}:{t}", wuk], writes=[psu_k],
                                tag=f"{sname}u{j}_{t}")
                            for fn in pend:
                                fn()
                            pend = []
                            sch.op("scalar", lambda s, p=psg, b=hs: s.activation(
                                b[:, :], p[:, :], AF.Silu),
                                reads=[psg_k], writes=[hsk],
                                tag=f"{sname}si{j}_{t}")
                            sch.op("vector", lambda v, p=psu, a=hs, b=hbuf:
                                   v.tensor_tensor(b[:, :], a[:, :], p[:, :],
                                                   ALU.mult),
                                reads=[hsk, psu_k], writes=[hbk],
                                tag=f"{sname}hm{j}_{t}")
                            sch.op("scalar", lambda s, a=hbuf, b=hsq: s.activation(
                                b[:, :], a[:, :], AF.Square),
                                reads=[hbk], writes=[hqk],
                                tag=f"{sname}sq{j}_{t}")
                            sch.op("vector", lambda v, a=hsq, b=tp, c=tq, d=tthr,
                                   e=tmn: [
                                v.tensor_tensor(b[:, :], a[:, 0:512:2],
                                                a[:, 1:512:2], ALU.max),
                                v.tensor_tensor(c[:, :], a[:, 0:512:2],
                                                a[:, 1:512:2], ALU.min),
                                v.tensor_tensor(d[:, :], b[:, 0:256:2],
                                                b[:, 1:256:2], ALU.min),
                                v.tensor_tensor(e[:, :], c[:, 0:256:2],
                                                c[:, 1:256:2], ALU.max),
                                v.tensor_tensor(d[:, :], d[:, :],
                                                e[:, :], ALU.max)],
                                reads=[hqk], writes=[thk],
                                tag=f"{sname}tr{j}_{t}")
                            sch.op("vector", lambda v, a=hsq, d=tthr, m=tmask,
                                   hb=hbuf, o=hsp_bf: [
                                v.tensor_tensor(
                                    m[:, :].rearrange("p (g k) -> p g k", k=4),
                                    a[:, :].rearrange("p (g k) -> p g k", k=4),
                                    d[:, :].unsqueeze(2).broadcast_to(
                                        [128, 128, 4]),
                                    ALU.is_ge),
                                v.tensor_tensor(o[:, :], hb[:, :],
                                                m[:, :], ALU.mult)],
                                reads=[hqk, thk, hbk], writes=[spk],
                                barrier=True, tag=f"{sname}mk{j}_{t}")

                            def mk_tr(jj, tt, hh, hhk, sb, sbk):
                                def go():
                                    transpose4_to(
                                        [sb[:, ic * 128:(ic + 1) * 128]
                                         for ic in range(4)], sbk,
                                        bass.AP(hh, tt * 128,
                                                [[4 * TOK, 128], [TOK, 4],
                                                 [1, 128]]),
                                        hhk, tg=f"{sname}ht{jj}_{tt}")
                                return go
                            pend.append(mk_tr(j, t, hst, hstk, hsp_bf, spk))
                        for fn in pend:
                            fn()
                        pend = []
                        sch.op("sync", lambda sy, jj=j, b=hst: sy.dma_start(
                            bass.AP(hsp_d, jj * 512 * TOK,
                                    [[TOK, 128], [128 * TOK, 4], [1, TOK]]),
                            b[:, :].rearrange("p (c n) -> p c n", n=TOK)),
                            reads=[hstk], writes=["hsp_d"], dma=True,
                            tag=f"{sname}sp{j}")
            sch.fence()

            with contextlib.ExitStack() as sp:
                sbl = lambda n, sh, dt: sp.enter_context(
                    nc.sbuf_tensor(sname + "b" + n, sh, dt))
                wdn2 = [sbl(f"wdn{i}", [128, 64 * 512], BF16) for i in range(2)]
                hti2 = [sbl(f"hti{i}", [128, 64 * 256], BF16) for i in range(2)]
                xt2 = [sbl(f"xt{i}", [128, 512], F32) for i in range(2)]
                xo2 = [sbl(f"xo{i}", [128, 512], F32) for i in range(2)]
                def wdn_load(qq):
                    w, wk = wdn2[qq % 2], f"{sname}bwdn{qq % 2}"
                    if wmode == "read":
                        sch.op("gpsimd", lambda g, qq=qq, w=w: g.dma_start(
                            w[:, :],
                            bass.AP(wdscr, qq * 128 * 64 * 512,
                                    [[64 * 512, 128], [1, 64 * 512]])),
                            writes=[wk], dma=True, tag=f"{sname}wd{qq}")
                    else:
                        sch.op("gpsimd", lambda g, qq=qq, w=w: g.dma_start(
                            w[:, :].rearrange("p (k n) -> p k n", n=512),
                            ap_in["w_down"][:, qq * 512:(qq + 1) * 512].rearrange(
                                "(k p) n -> p k n", p=128)),
                            writes=[wk], dma=True, tag=f"{sname}wd{qq}")
                        sch.op("sync", lambda sy, qq=qq, w=w: sy.dma_start(
                            bass.AP(wdscr, qq * 128 * 64 * 512,
                                    [[64 * 512, 128], [1, 64 * 512]]),
                            w[:, :]),
                            reads=[wk], dma=True, tag=f"{sname}wdc{qq}")
                wdn_load(0)
                for qr in range(4):
                    wdn, wdnk = wdn2[qr % 2], f"{sname}bwdn{qr % 2}"
                    for quad in range(8):
                        if quad == 1 and qr < 3:
                            wdn_load(qr + 1)
                        hti, htik = hti2[quad % 2], f"{sname}bhti{quad % 2}"
                        sch.op("sync", lambda sy, qd=quad, b=hti: sy.dma_start(
                            b[:, :].rearrange("p (k n) -> p k n", n=256),
                            bass.AP(hsp_d, qd * 256,
                                    [[TOK, 128], [128 * TOK, 64], [1, 256]])),
                            reads=["hsp_d"], writes=[htik], dma=True,
                            tag=f"{sname}lh{qr}_{quad}")
                        for tq in range(2):
                            t = quad * 2 + tq
                            xt, xtk = xt2[t % 2], f"{sname}bxt{t % 2}"
                            sch.op("sync", lambda sy, tt=t, qq=qr, b=xt:
                                   sy.dma_start(
                                       b[:, :],
                                       x_src_ap[tt * 128:(tt + 1) * 128,
                                                qq * 512:(qq + 1) * 512]),
                                   reads=[f"{x_src_key}:{t}"], writes=[xtk],
                                   dma=True, tag=f"{sname}lx{qr}_{t}")
                            xo, xok = xo2[t % 2], f"{sname}bxo{t % 2}"
                            psd, psd_k = next_pmm()
                            sch.op("tensor", lambda te, b=hti, w=wdn, p=psd,
                                   tb=tq * 128: [
                                te.matmul(p[:, :],
                                          b[:, k * 256 + tb:k * 256 + tb + 128],
                                          w[:, k * 512:(k + 1) * 512],
                                          start=(k == 0), stop=(k == 63))
                                for k in range(64)],
                                reads=[htik, wdnk], writes=[psd_k],
                                tag=f"{sname}d{qr}_{t}")
                            sch.op("vector", lambda v, b=xt, o=xo, p=psd:
                                   v.scalar_tensor_tensor(
                                       o[:, :], p[:, :], 0.5, b[:, :],
                                       ALU.mult, ALU.add),
                                   reads=[psd_k, xtk], writes=[xok],
                                   tag=f"{sname}e{qr}_{t}")
                            sch.op("sync", lambda sy, tt=t, qq=qr, o=xo:
                                   sy.dma_start(
                                       x_out_ap[tt * 128:(tt + 1) * 128,
                                                qq * 512:(qq + 1) * 512],
                                       o[:, :]),
                                   reads=[xok], writes=[f"{x_out_key}:{t}"],
                                   dma=True, tag=f"{sname}sx{qr}_{t}")
            sch.fence()

        def stage_attn(x_src_ap, x_src_key, x_out_ap, x_out_key):
            sname = "C"
            with contextlib.ExitStack() as sc:
                sbl = lambda n, sh, dt: sc.enter_context(
                    nc.sbuf_tensor(sname + n, sh, dt))
                cqT = sbl("cqT", [128, 4 * NT * 128], BF16)
                ckvT = sbl("ckvT", [128, 4 * NT * 128], BF16)
                with contextlib.ExitStack() as s_ln:
                    ln_t = s_ln.enter_context(
                        nc.sbuf_tensor("Cln_t", [128, NT * 16 * 128], BF16))
                    ln_key = "Cln_t"
                    with contextlib.ExitStack() as s_p0:
                        ln_phase(x_src_ap, x_src_key, ln_t, ln_key, s_p0, "Cp0")
                    sch.fence()

                    with contextlib.ExitStack() as sp:
                        sbp = lambda n, sh, dt: sp.enter_context(
                            nc.sbuf_tensor(sname + "1" + n, sh, dt))
                        wc_sb = sbp("wc", [128, 16 * 512], BF16)
                        wckv_sb = sbp("wckv", [128, 16 * 512], BF16)
                        bc_r = sbp("bc", [1, DC], BF16)
                        bckv_r = sbp("bckv", [1, DC], BF16)
                        cq_bf2 = [sbp(f"cq_bf{i}", [128, 512], BF16) for i in range(2)]
                        sch.op("gpsimd", lambda g: [
                            g.dma_start(wc_sb[:, :].rearrange(
                                "p (k n) -> p k n", n=512),
                                ap_in["Wc"].rearrange("(k p) n -> p k n", p=128)),
                            g.dma_start(wckv_sb[:, :].rearrange(
                                "p (k n) -> p k n", n=512),
                                ap_in["Wckv"].rearrange("(k p) n -> p k n", p=128)),
                            g.dma_start(bc_r[:, :], ap_in["bc"].unsqueeze(0)),
                            g.dma_start(bckv_r[:, :], ap_in["bckv"].unsqueeze(0))],
                            writes=["wc", "wckv", "bc", "bckv"], dma=True,
                            n_dma=4, tag="Cwc")
                        pend1 = []
                        for t in range(NT):
                            for ii, (wsb, wk_, br, brk, dstT, dstTk) in enumerate([
                                    (wc_sb, "wc", bc_r, "bc", cqT, "cqT"),
                                    (wckv_sb, "wckv", bckv_r, "bckv", ckvT, "ckvT")]):
                                cq_bf, cqbk = cq_bf2[ii], f"cq_bf{ii}"
                                ps, ps_k = next_pmm()
                                sch.op("tensor", lambda te, tt=t, w=wsb, b=br,
                                       p=ps: [
                                    te.matmul(p[:, :],
                                              ln_t[:, (k * NT + tt) * 128:
                                                   (k * NT + tt + 1) * 128],
                                              w[:, k * 512:(k + 1) * 512],
                                              start=(k == 0), stop=False)
                                    for k in range(16)] + [
                                    te.matmul(p[:, :], ones_row[:, :], b[:, :],
                                              start=False, stop=True)],
                                    reads=[f"{ln_key}:{t}", wk_, brk, "ones"],
                                    writes=[ps_k], tag=f"Ccq{t}")
                                sch.op("scalar", lambda s, p=ps, b=cq_bf: s.copy(
                                    b[:, :], p[:, :]),
                                    reads=[ps_k], writes=[cqbk],
                                    tag=f"Ccqe{t}")
                                for fn in pend1:
                                    fn()
                                pend1 = []

                                def mk_p1tr(tt, ii_, cq_bf=cq_bf, cqbk=cqbk,
                                            dstT=dstT, dstTk=dstTk):
                                    def go():
                                        transpose4_to(
                                            [cq_bf[:, k * 128:(k + 1) * 128]
                                             for k in range(4)], cqbk,
                                            bass.AP(dstT, tt * 128,
                                                    [[4 * NT * 128, 128],
                                                     [NT * 128, 4], [1, 128]]),
                                            f"{dstTk}:{tt}",
                                            tg=f"Ccqt{tt}_{ii_}")
                                    return go
                                pend1.append(mk_p1tr(t, ii))
                        for fn in pend1:
                            fn()
                    sch.fence()

                    with contextlib.ExitStack() as sp:
                        sbp = lambda n, sh, dt: sp.enter_context(
                            nc.sbuf_tensor(sname + "kr" + n, sh, dt))
                        wkr_sb = sbp("wkr", [128, 16 * 1024], BF16)
                        bkr_r = sbp("bkr", [1, 1024], BF16)
                        qrf2 = [sbp(f"qrf{i}", [128, 1024], F32) for i in range(2)]
                        qrb2 = [sbp(f"qrb{i}", [128, 1024], BF16) for i in range(2)]
                        cosb = [sbp(f"cos{i}", [128, 512], F32) for i in range(2)]
                        sinb = [sbp(f"sin{i}", [128, 512], F32) for i in range(2)]
                        ropet = sbp("rt", [128, 512], F32)
                        ropeu = sbp("ru", [128, 512], F32)
                        lo2 = [sbp(f"lo{i}", [64, 2048], BF16) for i in range(2)]
                        sch.op("gpsimd", lambda g: [
                            g.dma_start(wkr_sb[:, :].rearrange(
                                "p (k n) -> p k n", n=1024),
                                ap_in["Wkr"].rearrange("(k p) n -> p k n", p=128)),
                            g.dma_start(bkr_r[:, :], ap_in["bkr"].unsqueeze(0))],
                            writes=["wkr", "krb"], dma=True, n_dma=2, tag="Ckrw")
                        pendk = []
                        for t in range(NT):
                            par = t % 2
                            qrf, qrb = qrf2[par], qrb2[par]
                            qrfk, qrbk = f"krqrf{par}", f"krqrb{par}"
                            lo, lok = lo2[par], f"krlo{par}"
                            cs, csk = cosb[par], f"krcos{par}"
                            sn, snk = sinb[par], f"krsin{par}"
                            sch.op("sync", lambda sy, tt=t, a=cs, b=sn: [
                                sy.dma_start(a[:, :], ap_in["ropecos"][
                                    tt * 128:(tt + 1) * 128, :]),
                                sy.dma_start(b[:, :], ap_in["ropesin"][
                                    tt * 128:(tt + 1) * 128, :])],
                                writes=[csk, snk], dma=True, n_dma=2,
                                tag=f"Ckrc{t}")
                            for nck in range(2):
                                ps, ps_k = next_pmm()
                                sch.op("tensor", lambda te, tt=t, cc=nck, p=ps: [
                                    te.matmul(p[:, :],
                                              ln_t[:, (k * NT + tt) * 128:
                                                   (k * NT + tt + 1) * 128],
                                              wkr_sb[:, k * 1024 + cc * 512:
                                                     k * 1024 + cc * 512 + 512],
                                              start=(k == 0), stop=False)
                                    for k in range(16)] + [
                                    te.matmul(p[:, :], ones_row[:, :],
                                              bkr_r[:, cc * 512:(cc + 1) * 512],
                                              start=False, stop=True)],
                                    reads=[f"{ln_key}:{t}", "wkr", "krb", "ones"],
                                    writes=[ps_k], tag=f"Ckr{t}_{nck}")
                                sch.op("scalar", lambda s, cc=nck, p=ps, b=qrf:
                                       s.copy(b[:, cc * 512:(cc + 1) * 512],
                                              p[:, :]),
                                       reads=[ps_k], writes=[qrfk],
                                       tag=f"Ckre{t}_{nck}")
                            for fn in pendk:
                                fn()
                            pendk = []
                            q3 = qrf[:, :].rearrange("p (h r) -> p h r", r=64)
                            o3 = qrb[:, :].rearrange("p (h r) -> p h r", r=64)
                            c3 = cs[:, :].rearrange("p (h r) -> p h r", r=32)
                            s3 = sn[:, :].rearrange("p (h r) -> p h r", r=32)
                            t3 = ropet[:, :].rearrange("p (h r) -> p h r", r=32)
                            u3 = ropeu[:, :].rearrange("p (h r) -> p h r", r=32)
                            sch.op("vector", lambda v, t3=t3, u3=u3, q3=q3,
                                   c3=c3, s3=s3, o3=o3: [
                                v.tensor_tensor(t3, q3[:, :, 0:32], c3, ALU.mult),
                                v.tensor_tensor(u3, q3[:, :, 32:64], s3, ALU.mult),
                                v.tensor_tensor(o3[:, :, 0:32], t3, u3,
                                                ALU.subtract),
                                v.tensor_tensor(t3, q3[:, :, 32:64], c3, ALU.mult),
                                v.tensor_tensor(u3, q3[:, :, 0:32], s3, ALU.mult),
                                v.tensor_tensor(o3[:, :, 32:64], t3, u3, ALU.add)],
                                reads=[qrfk, csk, snk], writes=[qrbk],
                                tag=f"Ckro{t}")

                            def mk_krtr(tt, qrb=qrb, qrbk=qrbk, lo=lo, lok=lok):
                                def go():
                                    for h0 in range(0, H, 4):
                                        transpose4_to(
                                            [qrb[:, (h0 + h) * 64:
                                                 (h0 + h + 1) * 64]
                                             for h in range(4)], qrbk,
                                            bass.AP(lo, h0,
                                                    [[2048, 64], [1, 4],
                                                     [16, 128]]),
                                            lok, nparts=64,
                                            tg=f"Ckrt{tt}_{h0}")
                                    sch.op("sync", lambda sy, b=lo: sy.dma_start(
                                        bass.AP(qk_d, (tt * 384 + 320) * TOK,
                                                [[TOK, 64], [1, 2048]]),
                                        b[0:64, :]),
                                        reads=[lok], writes=[f"qk_d:{tt}"],
                                        dma=True, tag=f"Ckrs{tt}")
                                return go
                            pendk.append(mk_krtr(t))
                        for fn in pendk:
                            fn()
                    sch.fence()

                with contextlib.ExitStack() as sp:
                    sbp = lambda n, sh, dt: sp.enter_context(
                        nc.sbuf_tensor(sname + "m" + n, sh, dt))
                    wq_sb = sbp("wq", [128, 4 * 2048], BF16)
                    wqr_sb = sbp("wqr", [128, 4 * 1024], BF16)
                    wk_sb = sbp("wk", [128, 4 * 2048], BF16)
                    wv_sb = sbp("wv", [128, 4 * 2048], BF16)
                    bq_r = sbp("bq", [1, D], BF16)
                    bqr_r = sbp("bqr", [1, 1024], BF16)
                    bk_r = sbp("bk", [1, D], BF16)
                    bv_r = sbp("bv", [1, D], BF16)
                    sch.op("gpsimd", lambda g: [
                        g.dma_start(wq_sb[:, :].rearrange(
                            "p (k n) -> p k n", n=2048),
                            ap_in["Wcq"].rearrange("(k p) n -> p k n", p=128)),
                        g.dma_start(bq_r[:, :], ap_in["bcq"].unsqueeze(0)),
                        g.dma_start(wqr_sb[:, :].rearrange(
                            "p (k n) -> p k n", n=1024),
                            ap_in["Wqr"].rearrange("(k p) n -> p k n", p=128)),
                        g.dma_start(bqr_r[:, :], ap_in["bqr"].unsqueeze(0))],
                        writes=["wq", "bq", "wqr", "bqr"], dma=True, n_dma=4,
                        tag="Cmw1")
                    sch.op("gpsimd", lambda g: [
                        g.dma_start(wk_sb[:, :].rearrange(
                            "p (k n) -> p k n", n=2048),
                            ap_in["Wck"].rearrange("(k p) n -> p k n", p=128)),
                        g.dma_start(bk_r[:, :], ap_in["bck"].unsqueeze(0)),
                        g.dma_start(wv_sb[:, :].rearrange(
                            "p (k n) -> p k n", n=2048),
                            ap_in["Wv"].rearrange("(k p) n -> p k n", p=128)),
                        g.dma_start(bv_r[:, :], ap_in["bv"].unsqueeze(0))],
                        writes=["wk", "bk", "wv", "bv"], dma=True, n_dma=4,
                        tag="Cmw2")
                    qf2 = [sbp(f"qf{i}", [128, 2048], BF16) for i in range(2)]
                    kf2 = [sbp(f"kf{i}", [128, 2048], BF16) for i in range(2)]
                    vnb2 = [sbp(f"vnb{i}", [128, 2048], BF16) for i in range(2)]
                    qrf2 = [sbp(f"qrf{i}", [128, 1024], F32) for i in range(2)]
                    qrb2 = [sbp(f"qrb{i}", [128, 1024], BF16) for i in range(2)]
                    cosb = [sbp(f"cos{i}", [128, 512], F32) for i in range(2)]
                    sinb = [sbp(f"sin{i}", [128, 512], F32) for i in range(2)]
                    ropet = sbp("rt", [128, 512], F32)
                    ropeu = sbp("ru", [128, 512], F32)
                    hi2 = [sbp(f"hi{i}", [128, 2048], BF16) for i in range(2)]
                    lo2 = [sbp(f"lo{i}", [64, 2048], BF16) for i in range(2)]
                    pendm = []
                    for t in range(NT):
                        par = t % 2
                        qf, kf, vnb = qf2[par], kf2[par], vnb2[par]
                        qfk, kfk, vnbk = f"mqf{par}", f"mkf{par}", f"mvnb{par}"
                        qrf, qrb = qrf2[par], qrb2[par]
                        qrfk, qrbk = f"mqrf{par}", f"mqrb{par}"
                        hi, lo = hi2[par], lo2[par]
                        hik, lok = f"mhi{par}", f"mlo{par}"
                        cs, csk = cosb[par], f"mcos{par}"
                        sn, snk = sinb[par], f"msin{par}"
                        sch.op("sync", lambda sy, tt=t, a=cs, b=sn: [
                            sy.dma_start(a[:, :], ap_in["ropecos"][
                                tt * 128:(tt + 1) * 128, :]),
                            sy.dma_start(b[:, :], ap_in["ropesin"][
                                tt * 128:(tt + 1) * 128, :])],
                            writes=[csk, snk], dma=True, n_dma=2,
                            tag=f"Cmrc{t}")

                        def proj4(tt, cTk_, w_sb, wk_, b_r, brk, out, outk, tg):
                            cT_ = cqT if cTk_ == "cqT" else ckvT
                            for nck in range(4):
                                ps, ps_k = next_pmm()
                                sch.op("tensor", lambda te, cc=nck, p=ps,
                                       w=w_sb, b=b_r, c=cT_: [
                                    te.matmul(p[:, :],
                                              c[:, (k * NT + tt) * 128:
                                                (k * NT + tt + 1) * 128],
                                              w[:, k * 2048 + cc * 512:
                                                k * 2048 + cc * 512 + 512],
                                              start=(k == 0), stop=False)
                                    for k in range(4)] + [
                                    te.matmul(p[:, :], ones_row[:, :],
                                              b[:, cc * 512:(cc + 1) * 512],
                                              start=False, stop=True)],
                                    reads=[f"{cTk_}:{tt}", wk_, brk, "ones"],
                                    writes=[ps_k], tag=f"{tg}{tt}_{nck}")
                                sch.op("scalar", lambda s, cc=nck, p=ps, b=out:
                                       s.copy(b[:, cc * 512:(cc + 1) * 512],
                                              p[:, :]),
                                       reads=[ps_k], writes=[outk],
                                       tag=f"{tg}e{tt}_{nck}")

                        proj4(t, "cqT", wq_sb, "wq", bq_r, "bq", qf, qfk, "Cmq")
                        for nck in range(2):
                            ps, ps_k = next_pmm()
                            sch.op("tensor", lambda te, tt=t, cc=nck, p=ps: [
                                te.matmul(p[:, :],
                                          cqT[:, (k * NT + tt) * 128:
                                              (k * NT + tt + 1) * 128],
                                          wqr_sb[:, k * 1024 + cc * 512:
                                                 k * 1024 + cc * 512 + 512],
                                          start=(k == 0), stop=False)
                                for k in range(4)] + [
                                te.matmul(p[:, :], ones_row[:, :],
                                          bqr_r[:, cc * 512:(cc + 1) * 512],
                                          start=False, stop=True)],
                                reads=[f"cqT:{t}", "wqr", "bqr", "ones"],
                                writes=[ps_k], tag=f"Cmqr{t}_{nck}")
                            sch.op("scalar", lambda s, cc=nck, p=ps, b=qrf:
                                   s.copy(b[:, cc * 512:(cc + 1) * 512],
                                          p[:, :]),
                                   reads=[ps_k], writes=[qrfk],
                                   tag=f"Cmqre{t}_{nck}")
                        proj4(t, "ckvT", wk_sb, "wk", bk_r, "bk", kf, kfk, "Cmk")
                        proj4(t, "ckvT", wv_sb, "wv", bv_r, "bv", vnb, vnbk, "Cmv")
                        for fn in pendm:
                            fn()
                        pendm = []
                        q3 = qrf[:, :].rearrange("p (h r) -> p h r", r=64)
                        o3 = qrb[:, :].rearrange("p (h r) -> p h r", r=64)
                        c3 = cs[:, :].rearrange("p (h r) -> p h r", r=32)
                        s3 = sn[:, :].rearrange("p (h r) -> p h r", r=32)
                        t3 = ropet[:, :].rearrange("p (h r) -> p h r", r=32)
                        u3 = ropeu[:, :].rearrange("p (h r) -> p h r", r=32)
                        sch.op("vector", lambda v, t3=t3, u3=u3, q3=q3,
                               c3=c3, s3=s3, o3=o3: [
                            v.tensor_tensor(t3, q3[:, :, 0:32], c3, ALU.mult),
                            v.tensor_tensor(u3, q3[:, :, 32:64], s3, ALU.mult),
                            v.tensor_tensor(o3[:, :, 0:32], t3, u3, ALU.subtract),
                            v.tensor_tensor(t3, q3[:, :, 32:64], c3, ALU.mult),
                            v.tensor_tensor(u3, q3[:, :, 0:32], s3, ALU.mult),
                            v.tensor_tensor(o3[:, :, 32:64], t3, u3, ALU.add)],
                            reads=[qrfk, csk, snk], writes=[qrbk],
                            tag=f"Cmro{t}")
                        sch.op("sync", lambda sy, tt=t, b=vnb: sy.dma_start(
                            vscr_d.ap()[tt * 128:(tt + 1) * 128, :], b[:, :]),
                            reads=[vnbk], writes=[f"vscr:{t}"], dma=True,
                            tag=f"Cmvs{t}")

                        def mk_mtr(tt, qf=qf, qfk=qfk, kf=kf, kfk=kfk,
                                   qrb=qrb, qrbk=qrbk, hi=hi, lo=lo,
                                   hik=hik, lok=lok):
                            def go():
                                for h0 in range(0, H, 4):
                                    transpose4_to(
                                        [qf[:, (h0 + h) * 128:
                                            (h0 + h + 1) * 128]
                                         for h in range(4)], qfk,
                                        bass.AP(hi, h0, [[2048, 128], [1, 4],
                                                         [16, 128]]),
                                        hik, tg=f"Cmth{tt}_{h0}")
                                    transpose4_to(
                                        [qrb[:, (h0 + h) * 64:
                                             (h0 + h + 1) * 64]
                                         for h in range(4)], qrbk,
                                        bass.AP(lo, h0, [[2048, 64], [1, 4],
                                                         [16, 128]]),
                                        lok, nparts=64, tg=f"Cmtl{tt}_{h0}")
                                sch.op("sync", lambda sy, a=hi, b=lo: [
                                    sy.dma_start(
                                        bass.AP(qk_d, (tt * 384) * TOK,
                                                [[TOK, 128], [1, 2048]]),
                                        a[:, :]),
                                    sy.dma_start(
                                        bass.AP(qk_d, (tt * 384 + 128) * TOK,
                                                [[TOK, 64], [1, 2048]]),
                                        b[0:64, :])],
                                    reads=[hik, lok], writes=[f"qk_d:{tt}"],
                                    dma=True, n_dma=2, tag=f"Cmqs{tt}")
                                for h0 in range(0, H, 4):
                                    transpose4_to(
                                        [kf[:, (h0 + h) * 128:
                                            (h0 + h + 1) * 128]
                                         for h in range(4)], kfk,
                                        bass.AP(hi, h0, [[2048, 128], [1, 4],
                                                         [16, 128]]),
                                        hik, tg=f"Cmtk{tt}_{h0}")
                                sch.op("sync", lambda sy, a=hi: sy.dma_start(
                                    bass.AP(qk_d, (tt * 384 + 192) * TOK,
                                            [[TOK, 128], [1, 2048]]),
                                    a[:, :]),
                                    reads=[hik], writes=[f"qk_d:{tt}"],
                                    dma=True, tag=f"Cmks{tt}")
                            return go
                        pendm.append(mk_mtr(t))
                    for fn in pendm:
                        fn()
                sch.fence()
            sch.fence()

            with contextlib.ExitStack() as sp:
                sbl = lambda n, sh, dt: sp.enter_context(
                    nc.sbuf_tensor(sname + "g" + n, sh, dt))
                outT = sbl("outT", [128, NT * 2048], BF16)
                qhi2 = [sbl(f"qhi{i}", [128, 2048], BF16) for i in range(2)]
                qlo2 = [sbl(f"qlo{i}", [64, 2048], BF16) for i in range(2)]
                khi2 = [sbl(f"khi{i}", [128, 2048], BF16) for i in range(2)]
                klo2 = [sbl(f"klo{i}", [64, 2048], BF16) for i in range(2)]
                vg2 = [sbl(f"vg{i}", [128, 2048], BF16) for i in range(2)]
                g4_2 = [sbl(f"g4_{i}", [128, 512], F32) for i in range(2)]
                m4_2 = [sbl(f"m4_{i}", [128, 4], F32) for i in range(2)]
                bm4_2 = [sbl(f"bm4_{i}", [128, 4], F32) for i in range(2)]
                ein_2 = [sbl(f"ein_{i}", [128, 512], F32) for i in range(2)]
                ebf_2 = [sbl(f"ebf_{i}", [128, 512], BF16) for i in range(2)]
                rs4_2 = [sbl(f"rs4_{i}", [128, 4], F32) for i in range(2)]
                ri4_2 = [sbl(f"ri4_{i}", [128, 4], F32) for i in range(2)]
                at4_2 = [sbl(f"at4_{i}", [128, 512], BF16) for i in range(2)]
                av4_2 = [sbl(f"av4_{i}", [128, 512], BF16) for i in range(2)]
                pend7 = []
                for t in range(NT):
                    qhi, qhik = qhi2[t % 2], f"gqhi{t % 2}"
                    qlo, qlok = qlo2[t % 2], f"gqlo{t % 2}"
                    khi, khik = khi2[t % 2], f"gkhi{t % 2}"
                    klo, klok = klo2[t % 2], f"gklo{t % 2}"
                    vg, vgk = vg2[t % 2], f"gvg{t % 2}"
                    sch.op("sync", lambda sy, tt=t, a=qhi, b=qlo, c=khi, d=klo: [
                        sy.dma_start(a[:, :], bass.AP(
                            qk_d, (tt * 384) * TOK, [[TOK, 128], [1, 2048]])),
                        sy.dma_start(b[0:64, :], bass.AP(
                            qk_d, (tt * 384 + 128) * TOK, [[TOK, 64], [1, 2048]])),
                        sy.dma_start(c[:, :], bass.AP(
                            qk_d, (tt * 384 + 192) * TOK, [[TOK, 128], [1, 2048]])),
                        sy.dma_start(d[0:64, :], bass.AP(
                            qk_d, (tt * 384 + 320) * TOK, [[TOK, 64], [1, 2048]]))],
                        reads=[f"qk_d:{t}"], writes=[qhik, qlok, khik, klok],
                        dma=True, n_dma=4, tag=f"Cgl{t}")
                    sch.op("gpsimd", lambda g, tt=t, b=vg: g.dma_start(
                        b[:, :], bass.AP(vscr_d, tt * 128 * D,
                                         [[D, 8], [128, 16], [8 * D, 16],
                                          [1, 128]])),
                        reads=[f"vscr:{t}"], writes=[vgk], dma=True,
                        tag=f"Cgv{t}")
                    for blk in range(4):
                        gp = blk % 2
                        g4, m4, bm4 = g4_2[gp], m4_2[gp], bm4_2[gp]
                        ein, ebf = ein_2[gp], ebf_2[gp]
                        rs4, ri4 = rs4_2[gp], ri4_2[gp]
                        at4, av4 = at4_2[gp], av4_2[gp]
                        g4k, m4k, bm4k = f"g4_{gp}", f"m4_{gp}", f"bm4_{gp}"
                        eink, ebfk = f"ein_{gp}", f"ebf_{gp}"
                        rs4k, ri4k = f"rs4_{gp}", f"ri4_{gp}"
                        at4k, av4k = f"at4_{gp}", f"av4_{gp}"
                        psg, psg_k = next_pmm()
                        sch.op("tensor", lambda te, b_=blk, a=qhi, b=qlo,
                               c=khi, d=klo, p=psg: [
                            mm for g_ in range(4) for mm in (
                                te.matmul(p[:, g_ * 128:(g_ + 1) * 128],
                                          a[:, (b_ * 4 + g_) * 128:
                                            (b_ * 4 + g_ + 1) * 128],
                                          c[:, (b_ * 4 + g_) * 128:
                                            (b_ * 4 + g_ + 1) * 128],
                                          start=True, stop=False),
                                te.matmul(p[:, g_ * 128:(g_ + 1) * 128],
                                          b[0:64, (b_ * 4 + g_) * 128:
                                            (b_ * 4 + g_ + 1) * 128],
                                          d[0:64, (b_ * 4 + g_) * 128:
                                            (b_ * 4 + g_ + 1) * 128],
                                          start=False, stop=True))],
                            reads=[qhik, qlok, khik, klok], writes=[psg_k],
                            tag=f"Cgr{t}_{blk}")
                        for fn in pend7:
                            fn()
                        pend7 = []
                        sch.op("vector", lambda v, p=psg, a=g4, b=m4: [
                            v.tensor_tensor(
                                a[:, :].rearrange("p (g n) -> p g n", n=128),
                                p[:, :].rearrange("p (g n) -> p g n", n=128),
                                maskneg[:, :].unsqueeze(1).broadcast_to(
                                    [128, 4, 128]), ALU.add),
                            v.reduce_max(
                                b[:, :].rearrange("p (g n) -> p g n", n=1),
                                a[:, :].rearrange("p (g n) -> p g n", n=128),
                                axis=AX.X)],
                            reads=[psg_k, "maskneg"], writes=[g4k, m4k],
                            tag=f"Cgm{t}_{blk}")
                        sch.op("vector", lambda v, a=m4, c=g4, o=ein:
                               v.tensor_tensor(
                                   o[:, :].rearrange("p (g n) -> p g n", n=128),
                                   c[:, :].rearrange("p (g n) -> p g n", n=128),
                                   a[:, :].unsqueeze(2).broadcast_to(
                                       [128, 4, 128]),
                                   ALU.subtract),
                            reads=[m4k, g4k], writes=[eink],
                            barrier=True, tag=f"Cgs{t}_{blk}")
                        sch.op("scalar", lambda s, a=ein, b=ebf: s.activation(
                            b[:, :], a[:, :], AF.Exp, scale=SCALER),
                            reads=[eink], writes=[ebfk], tag=f"Cge{t}_{blk}")
                        sch.op("vector", lambda v, a=ebf, b=rs4: v.reduce_sum(
                            b[:, :].rearrange("p (g n) -> p g n", n=1),
                            a[:, :].rearrange("p (g n) -> p g n", n=128),
                            axis=AX.X),
                            reads=[ebfk], writes=[rs4k], tag=f"Cgr2{t}_{blk}")
                        sch.op("vector", lambda v, a=rs4, b=ri4: v.reciprocal(
                            b[:, :], a[:, :]),
                            reads=[rs4k], writes=[ri4k], barrier=True,
                            tag=f"Cgc{t}_{blk}")

                        def mk_av(tt, b_, vg=vg, vgk=vgk, ebf=ebf, ebfk=ebfk,
                                  at4=at4, at4k=at4k, av4=av4, av4k=av4k,
                                  ri4=ri4, ri4k=ri4k):
                            def go():
                                transpose4_to(
                                    [ebf[:, g_ * 128:(g_ + 1) * 128]
                                     for g_ in range(4)], ebfk,
                                    at4[:, :].rearrange("p (g n) -> p g n",
                                                        n=128),
                                    at4k, tg=f"Cgt{tt}_{b_}")
                                psa, psa_k = next_pmm()
                                sch.op("tensor", lambda te, p=psa: [
                                    te.matmul(p[:, g_ * 128:(g_ + 1) * 128],
                                              at4[:, g_ * 128:(g_ + 1) * 128],
                                              vg[:, (b_ * 4 + g_) * 128:
                                                 (b_ * 4 + g_ + 1) * 128],
                                              start=True, stop=True)
                                    for g_ in range(4)],
                                    reads=[at4k, vgk], writes=[psa_k],
                                    tag=f"Cga{tt}_{b_}")
                                sch.op("vector", lambda v, p=psa, o=av4:
                                       v.tensor_tensor(
                                           o[:, :].rearrange(
                                               "p (g n) -> p g n", n=128),
                                           p[:, :].rearrange(
                                               "p (g n) -> p g n", n=128),
                                           ri4[:, :].unsqueeze(2).broadcast_to(
                                               [128, 4, 128]), ALU.mult),
                                    reads=[psa_k, ri4k], writes=[av4k],
                                    tag=f"Cgae{tt}_{b_}")
                                transpose4_to(
                                    [av4[:, g_ * 128:(g_ + 1) * 128]
                                     for g_ in range(4)], av4k,
                                    outT[:, tt * 2048 + b_ * 512:
                                         tt * 2048 + (b_ + 1) * 512].rearrange(
                                        "p (g n) -> p g n", n=128),
                                    f"outT:{tt}", tg=f"Cgo{tt}_{b_}")
                            return go
                        pend7.append(mk_av(t, blk))
                for fn in pend7:
                    fn()
                pend7 = []

                for fn in pend7:
                    fn()
                pend7 = []
                wo2 = [sbl(f"wo{i}", [128, 16 * 512], BF16) for i in range(2)]
                bo_r = sbl("bo", [1, D], BF16)
                xr2 = [sbl(f"xr{i}", [128, 512], F32) for i in range(2)]
                xw2 = [sbl(f"xw{i}", [128, 512], F32) for i in range(2)]
                sch.op("gpsimd", lambda g: g.dma_start(
                    bo_r[:, :], ap_in["bo"].unsqueeze(0)),
                    writes=["bo"], dma=True, tag="Cbo")
                for ncw in range(4):
                    wo, wok = wo2[ncw % 2], f"gwo{ncw % 2}"
                    sch.op("gpsimd", lambda g, cc=ncw, b=wo: g.dma_start(
                        b[:, :].rearrange("p (k n) -> p k n", n=512),
                        ap_in["Wo"][:, cc * 512:(cc + 1) * 512].rearrange(
                            "(k p) n -> p k n", p=128)),
                        writes=[wok], dma=True, tag=f"Cgwo{ncw}")
                    for t in range(NT):
                        xr, xrk = xr2[t % 2], f"gxr{t % 2}"
                        sch.op("sync", lambda sy, tt=t, cc=ncw, b=xr:
                               sy.dma_start(
                                   b[:, :], x_src_ap[tt * 128:(tt + 1) * 128,
                                                     cc * 512:(cc + 1) * 512]),
                               reads=[f"{x_src_key}:{t}"], writes=[xrk],
                               dma=True, tag=f"Cglx{ncw}_{t}")
                        pso, pso_k = next_pmm()
                        sch.op("tensor", lambda te, tt=t, cc=ncw, w=wo, p=pso: [
                            te.matmul(p[:, :],
                                      bass.AP(outT, tt * 2048 + i,
                                              [[NT * 2048, 128], [16, 128]]),
                                      w[:, i * 512:(i + 1) * 512],
                                      start=(i == 0), stop=False)
                            for i in range(16)] + [
                            te.matmul(p[:, :], ones_row[:, :],
                                      bo_r[:, cc * 512:(cc + 1) * 512],
                                      start=False, stop=True)],
                            reads=[f"outT:{t}", wok, "bo", "ones"],
                            writes=[pso_k], tag=f"Cgw{ncw}_{t}")
                        xw, xwk = xw2[t % 2], f"gxw{t % 2}"
                        sch.op("vector", lambda v, b=xr, o=xw, p=pso:
                               v.tensor_tensor(o[:, :], p[:, :], b[:, :],
                                               ALU.add),
                               reads=[pso_k, xrk], writes=[xwk],
                               tag=f"Cgad{ncw}_{t}")
                        sch.op("sync", lambda sy, tt=t, cc=ncw, o=xw:
                               sy.dma_start(
                                   x_out_ap[tt * 128:(tt + 1) * 128,
                                            cc * 512:(cc + 1) * 512], o[:, :]),
                               reads=[xwk], writes=[f"{x_out_key}:{t}"],
                               dma=True, tag=f"Cgsx{ncw}_{t}")
            sch.fence()

        xm1 = xm1_d.ap()
        xm2 = xm2_d.ap()
        stage_ffn(ap_in["x"], "x_in", xm1, "xm1", "A", "write")
        if "xmid1" in dbg_h:
            sch.op("sync", lambda sy: [
                sy.dma_start(dbg_h["xmid1"][t * 128:(t + 1) * 128, :],
                             xm1[t * 128:(t + 1) * 128, :]) for t in range(NT)],
                reads=[f"xm1:{t}" for t in range(NT)], dma=True, n_dma=NT,
                tag="dbg1")
        stage_attn(xm1, "xm1", xm2, "xm2")
        if "qk" in dbg_h:
            sch.op("sync", lambda sy: sy.dma_start(
                dbg_h["qk"][:, :].bitcast(BF16)[:, 0:2048],
                bass.AP(qk_d, 0, [[2048, 2048], [1, 2048]])),
                reads=["qk_d"], dma=True, tag="dbgqk")
        if "vs" in dbg_h:
            sch.op("sync", lambda sy: sy.dma_start(
                dbg_h["vs"][:, :].bitcast(BF16)[:, 0:2048],
                bass.AP(vscr_d, 0, [[2048, 2048], [1, 2048]])),
                reads=["vscr"], dma=True, tag="dbgvs")
        if "xmid2" in dbg_h:
            sch.op("sync", lambda sy: [
                sy.dma_start(dbg_h["xmid2"][t * 128:(t + 1) * 128, :],
                             xm2[t * 128:(t + 1) * 128, :]) for t in range(NT)],
                reads=[f"xm2:{t}" for t in range(NT)], dma=True, n_dma=NT,
                tag="dbg2")
        stage_ffn(xm2, "xm2", out_h, "x_out", "D", "read")

        sch.emit(block)
    return nc


_NC_CACHE = {}


def _get_nc(debug=()):
    key = tuple(debug)
    if key not in _NC_CACHE:
        _NC_CACHE[key] = build_nc(debug)
    return _NC_CACHE[key]


def _host_constants():
    ident = np.eye(128, dtype=np.float32)
    maskneg = np.full((128, 128), -1e30, np.float32)
    for s8 in range(8):
        maskneg[s8 * 16:(s8 + 1) * 16, s8 * 16:(s8 + 1) * 16] = 0.0
    inv_freq = (1.0 / 10000.0 ** (np.arange(0, 64, 2, dtype=np.float32) / 64.0)
                ).astype(np.float32)
    t = np.arange(NCORES * TOK, dtype=np.float32)
    fr = t[:, None] * inv_freq[None, :]
    cos = np.tile(np.cos(fr), (1, H)).astype(np.float32)
    sin = np.tile(np.sin(fr), (1, H)).astype(np.float32)
    return ident, maskneg, cos, sin


def kernel(**inputs):
    x = np.ascontiguousarray(
        np.asarray(inputs["L0_attn_h_t"], dtype=np.float32).reshape(
            NCORES * TOK, D))
    ident, maskneg, cos, sin = _host_constants()
    wmap = {
        "Wc": "L0_attn_Wc", "bc": "L0_attn_bc", "Wcq": "L0_attn_Wcq",
        "bcq": "L0_attn_bcq", "Wqr": "L0_attn_Wqr", "bqr": "L0_attn_bqr",
        "Wckv": "L0_attn_Wckv", "bckv": "L0_attn_bckv", "Wck": "L0_attn_Wck",
        "bck": "L0_attn_bck", "Wkr": "L0_attn_Wkr", "bkr": "L0_attn_bkr",
        "Wv": "L0_attn_Wv", "bv": "L0_attn_bv", "Wo": "L0_attn_Wo",
        "bo": "L0_attn_bo", "w_gate": "L0_ffn_w_gate", "w_up": "L0_ffn_w_up",
        "w_down": "L0_ffn_w_down",
    }
    weights = {k: np.ascontiguousarray(np.asarray(inputs[v], dtype=np.float32))
               for k, v in wmap.items()}
    nc = _get_nc()
    in_maps = []
    for c in range(NCORES):
        m = dict(weights)
        m["x"] = x[c * TOK:(c + 1) * TOK]
        m["ident"] = ident
        m["maskneg"] = maskneg
        m["ropecos"] = np.ascontiguousarray(cos[c * TOK:(c + 1) * TOK])
        m["ropesin"] = np.ascontiguousarray(sin[c * TOK:(c + 1) * TOK])
        in_maps.append(m)
    res = run_bass_kernel_spmd(nc, in_maps, core_ids=list(range(NCORES)))
    out = np.concatenate([r["out"] for r in res.results], axis=0)
    return out.reshape(1, NCORES * TOK, D).astype(np.float32)


# version 15
# speedup vs baseline: 1.2372x; 1.2372x over previous
import contextlib
import numpy as np
import concourse.bass as bass
import concourse.mybir as mybir
from concourse.bass_utils import run_bass_kernel_spmd

F32 = mybir.dt.float32
BF16 = mybir.dt.bfloat16
AF = mybir.ActivationFunctionType
ALU = mybir.AluOpType
AX = mybir.AxisListType

NCORES = 8
D = 2048
TOK = 2048
NT = TOK // 128
DFF = 8192
NJ = DFF // 512
H = 16
DH = 128
DQ = 192
DC = 512
SCALER = 1.0 / float(np.sqrt(DQ))
EPS = 1e-5

ENGINES = ("tensor", "vector", "scalar", "gpsimd", "sync")


class _Op:
    __slots__ = ("idx", "engine", "emit_fn", "deps", "inc_amt", "cum_after",
                 "tag", "waits", "barrier")

    def __init__(self, idx, engine, emit_fn, deps, inc_amt, tag, barrier):
        self.idx = idx
        self.engine = engine
        self.emit_fn = emit_fn
        self.deps = deps
        self.inc_amt = inc_amt
        self.cum_after = None
        self.tag = tag
        self.barrier = barrier


class Sched:

    def __init__(self, nc):
        self.nc = nc
        self.ops = []
        self.sems = {}
        self._writers = {}
        self._epoch_read = {}
        self._readers_since = {}
        self._fence_ops = []

    def alloc_sems(self, stack):
        for e in ENGINES:
            self.sems[e] = stack.enter_context(self.nc.semaphore(f"sched_{e}"))

    def op(self, engine, emit_fn, reads=(), writes=(), deps=(), dma=False,
           n_dma=1, tag="", barrier=False):
        dd = set(d.idx for d in deps)
        for f in self._fence_ops:
            dd.add(f.idx)
        for r in reads:
            for w in self._writers.get(r, ()):
                dd.add(w.idx)
        for w in writes:
            for lw in self._writers.get(w, ()):
                dd.add(lw.idx)
            for rd in self._readers_since.get(w, ()):
                dd.add(rd.idx)
        o = _Op(len(self.ops), engine, emit_fn,
                [self.ops[i] for i in sorted(dd)],
                (16 * n_dma) if dma else 1, tag, barrier)
        self.ops.append(o)
        for r in reads:
            self._readers_since.setdefault(r, []).append(o)
            self._epoch_read[r] = True
        for w in writes:
            if self._epoch_read.get(w):
                self._writers[w] = [o]
                self._epoch_read[w] = False
            else:
                self._writers.setdefault(w, []).append(o)
            self._readers_since[w] = []
        return o

    def fence(self):
        last = {}
        for o in self.ops:
            last[o.engine] = o
        self._fence_ops = list(last.values())

    def emit(self, block):
        cum = {e: 0 for e in ENGINES}
        for o in self.ops:
            cum[o.engine] += o.inc_amt
            o.cum_after = cum[o.engine]
        per_engine = {e: [o for o in self.ops if o.engine == e] for e in ENGINES}
        for e in ENGINES:
            have = {x: 0 for x in ENGINES}
            for o in per_engine[e]:
                o.waits = []
                byeng = {}
                for d in o.deps:
                    if (d.engine == e and d.inc_amt == 1 and o.inc_amt == 1
                            and not o.barrier):
                        continue
                    byeng[d.engine] = max(byeng.get(d.engine, 0), d.cum_after)
                for en, vv in sorted(byeng.items()):
                    if vv > have[en]:
                        o.waits.append((en, vv))
                        have[en] = vv

        def mk(e):
            def body(eng):
                for o in per_engine[e]:
                    for en, vv in o.waits:
                        eng.wait_ge(self.sems[en], vv)
                    ret = o.emit_fn(eng)
                    insts = ret if isinstance(ret, (list, tuple)) else [ret]
                    insts = [i for i in insts if i is not None]
                    if not insts:
                        raise RuntimeError(f"op {o.idx} ({o.tag}) emitted nothing")
                    if o.inc_amt == 1:
                        insts[-1].then_inc(self.sems[e], 1)
                    else:
                        assert len(insts) * 16 == o.inc_amt, (o.tag, len(insts))
                        for i in insts:
                            i.then_inc(self.sems[e], 16)
            return body

        for e in ENGINES:
            if per_engine[e]:
                getattr(block, e)(mk(e))
        fin = dict(cum)

        def tail(eng):
            for e in ENGINES:
                if fin[e] and e != "sync":
                    eng.wait_ge(self.sems[e], fin[e])
        block.sync(tail)


def build_nc(debug=()):
    nc = bass.Bass("TRN2", target_bir_lowering=False)
    ap_in = {}
    shapes = {
        "x": [TOK, D], "Wc": [D, DC], "bc": [DC], "Wcq": [DC, D],
        "bcq": [D], "Wqr": [DC, 1024], "bqr": [1024], "Wckv": [D, DC],
        "bckv": [DC], "Wck": [DC, D], "bck": [D], "Wkr": [D, 1024],
        "bkr": [1024], "Wv": [DC, D], "bv": [D], "Wo": [D, D], "bo": [D],
        "w_gate": [D, DFF], "w_up": [D, DFF], "w_down": [DFF, D],
        "ident": [128, 128], "maskneg": [128, 128],
        "ropecos": [TOK, 512], "ropesin": [TOK, 512],
    }
    for n, s in shapes.items():
        ap_in[n] = nc.dram_tensor(n, s, F32, kind="ExternalInput").ap()
    out_h = nc.dram_tensor("out", [TOK, D], F32, kind="ExternalOutput").ap()
    dbg_h = {}
    for dn in debug:
        dbg_h[dn] = nc.dram_tensor("dbg_" + dn, [TOK, D], F32,
                                   kind="ExternalOutput").ap()

    hsp_d = nc.dram_tensor("hsp_scr", [DFF, TOK], BF16, kind="Internal")
    vscr_d = nc.dram_tensor("v_scr", [TOK, D], BF16, kind="Internal")
    qk_d = nc.dram_tensor("qk_scr", [NT, 384, TOK], BF16, kind="Internal")
    xm1_d = nc.dram_tensor("xmid1", [TOK, D], F32, kind="Internal")
    xm2_d = nc.dram_tensor("xmid2", [TOK, D], F32, kind="Internal")
    wgscr = nc.dram_tensor("wg_bf", [NJ, 128, 8192], BF16, kind="Internal")
    wuscr = nc.dram_tensor("wu_bf", [NJ, 128, 8192], BF16, kind="Internal")
    wdscr = nc.dram_tensor("wd_bf", [4, 128, 64 * 512], BF16, kind="Internal")

    st = contextlib.ExitStack()
    with st:
        sbt = lambda name, shape, dt: st.enter_context(
            nc.sbuf_tensor(name, shape, dt))
        ident = sbt("sb_ident", [128, 128], BF16)
        maskneg = sbt("sb_maskneg", [128, 128], F32)
        ones_row = sbt("ones_row", [1, 128], BF16)
        stats = sbt("stats", [128, 16], F32)
        flsc = sbt("flsc", [128, 16], F32)

        sch = Sched(nc)
        sch.alloc_sems(st)
        block = st.enter_context(nc.Block())

        pmm = [st.enter_context(nc.psum_tensor(f"pmm{i}", [128, 512], F32))
               for i in range(4)]
        ptr = [st.enter_context(nc.psum_tensor(f"ptr{i}", [128, 1024], BF16))
               for i in range(2)]
        pmm_i = [0]
        ptr_i = [0]

        def next_pmm():
            i = pmm_i[0] % 4
            pmm_i[0] += 1
            return pmm[i], f"pmm{i}"

        def next_ptr():
            i = ptr_i[0] % 2
            ptr_i[0] += 1
            return ptr[i], f"ptr{i}"

        evac_i = [0]

        def evac_to(dst_ap, dst_key, ps, ps_key, nparts=128, tg="e"):
            evac_i[0] += 1
            if evac_i[0] % 2:
                sch.op("scalar", lambda s: s.copy(dst_ap, ps[0:nparts, 0:128]),
                       reads=[ps_key], writes=[dst_key], tag=tg)
            else:
                sch.op("vector", lambda v: v.tensor_copy(dst_ap, ps[0:nparts, 0:128]),
                       reads=[ps_key], writes=[dst_key], tag=tg)

        def transpose_to(src_ap, src_key, dst_ap, dst_key, nparts=128, tg="tr"):
            ps, ps_key = next_ptr()
            sch.op("tensor", lambda te: te.transpose(
                ps[0:nparts, 0:128], src_ap, ident[:, :]),
                reads=[src_key, "ident"], writes=[ps_key], tag=f"{tg}t")
            evac_to(dst_ap, dst_key, ps, ps_key, nparts, tg=f"{tg}e")

        def transpose4_to(srcs, src_key, dst_ap, dst_key, nparts=128, tg="t4"):
            ps, ps_key = next_ptr()
            sch.op("tensor", lambda te, ss=list(srcs): [
                te.transpose(ps[0:nparts, k * 128:(k + 1) * 128], ss[k],
                             ident[:, :]) for k in range(4)],
                reads=[src_key, "ident"], writes=[ps_key], tag=f"{tg}t")
            src3 = ps[0:nparts, 0:512].rearrange("p (k n) -> p k n", n=128)
            evac_i[0] += 1
            if evac_i[0] % 2:
                sch.op("scalar", lambda s: s.copy(dst_ap, src3),
                       reads=[ps_key], writes=[dst_key], tag=f"{tg}e")
            else:
                sch.op("vector", lambda v: v.tensor_copy(dst_ap, src3),
                       reads=[ps_key], writes=[dst_key], tag=f"{tg}e")

        sch.op("gpsimd", lambda g: [
            g.dma_start(ident[:, :], ap_in["ident"][:, :]),
            g.dma_start(maskneg[:, :], ap_in["maskneg"][:, :])],
            writes=["ident", "maskneg"], dma=True, n_dma=2, tag="consts")
        sch.op("vector", lambda v: v.memset(ones_row[:, :], 1.0),
               writes=["ones"], tag="ones")

        def ln_tile(xt, xt_key, xn_bf, xn_key, sq_scr, tg, par=0):
            po = 8 * par
            s1 = stats[:, po + 0:po + 1]
            negmean = stats[:, po + 1:po + 2]
            vsum = stats[:, po + 2:po + 3]
            va = stats[:, po + 3:po + 4]
            rva = stats[:, po + 4:po + 5]
            lntmp = stats[:, po + 5:po + 6]
            rstd = stats[:, po + 6:po + 7]
            sch.op("vector", lambda v: v.reduce_sum(s1, xt[:, :], axis=AX.X),
                   reads=[xt_key], writes=[f"st_s1{par}"], tag=f"{tg}ln_r")
            sch.op("vector", lambda v: [
                v.tensor_scalar_mul(negmean, s1, -1.0 / D),
                v.tensor_copy(flsc[:, 8*par:8*par+1], negmean)],
                reads=[f"st_s1{par}"], writes=[f"st_nm{par}"], barrier=True, tag=f"{tg}ln_nm")
            sch.op("scalar", lambda s: [
                s.activation(sq_scr[:, :], xt[:, :], AF.Square,
                             bias=negmean, scale=1.0, accum_out=vsum),
                s.copy(flsc[:, 8*par+1:8*par+2], vsum)],
                reads=[xt_key, f"st_nm{par}"], writes=[f"st_vs{par}", f"sq{par}"],
                tag=f"{tg}ln_sq")
            sch.op("vector", lambda v: v.tensor_scalar(
                va, vsum, 1.0 / D, EPS, ALU.mult, ALU.add),
                reads=[f"st_vs{par}"], writes=[f"st_va{par}"], tag=f"{tg}ln_va")
            sch.op("vector", lambda v: [
                v.reciprocal(rva, va),
                v.tensor_copy(flsc[:, 8*par+2:8*par+3], rva)],
                reads=[f"st_va{par}"], writes=[f"st_rva{par}"], barrier=True,
                tag=f"{tg}ln_rv")
            sch.op("scalar", lambda s: s.activation(lntmp, rva, AF.Ln),
                   reads=[f"st_rva{par}"], writes=[f"st_lt{par}"], tag=f"{tg}ln_ln")
            sch.op("scalar", lambda s: [
                s.activation(rstd, lntmp, AF.Exp, scale=0.5),
                s.copy(flsc[:, 8*par+3:8*par+4], rstd)],
                reads=[f"st_lt{par}"], writes=[f"st_rs{par}"], barrier=True,
                tag=f"{tg}ln_ex")
            sch.op("vector", lambda v: v.tensor_scalar(
                xn_bf[:, :], xt[:, :], negmean, rstd, ALU.add, ALU.mult),
                reads=[xt_key, f"st_nm{par}", f"st_rs{par}"], writes=[xn_key],
                barrier=True, tag=f"{tg}ln_o")

        def ln_phase(x_src_ap, x_src_key, ln_t, ln_key, scope, sname):
            sbl = lambda n, sh, dt: scope.enter_context(
                nc.sbuf_tensor(sname + n, sh, dt))
            xt2 = [sbl(f"xt{i}", [128, D], F32) for i in range(2)]
            xnb2 = [sbl(f"xnb{i}", [128, D], BF16) for i in range(2)]
            sq2 = [sbl(f"sq{i}", [128, D], BF16) for i in range(2)]
            for t in range(NT):
                par = t % 2
                xt, xk = xt2[par], f"{sname}xt{par}"
                xnb = xnb2[par]
                sch.op("sync", lambda sy, tt=t, b=xt: sy.dma_start(
                    b[:, :], x_src_ap[tt * 128:(tt + 1) * 128, :]),
                    reads=[f"{x_src_key}:{t}"], writes=[xk], dma=True,
                    tag=f"{sname}ldx{t}")
                ln_tile(xt, xk, xnb, f"{sname}xnb{par}", sq2[par],
                        f"{sname}{t}", par=par)
                for k0 in range(0, 16, 4):
                    transpose4_to(
                        [xnb[:, (k0 + k) * 128:(k0 + k + 1) * 128]
                         for k in range(4)],
                        f"{sname}xnb{par}",
                        bass.AP(ln_t, ((k0 * NT + t) * 128),
                                [[NT * 16 * 128, 128], [NT * 128, 4], [1, 128]]),
                        f"{ln_key}:{t}", tg=f"{sname}lt{t}_{k0}")

        def stage_ffn(x_src_ap, x_src_key, x_out_ap, x_out_key, sname,
                      wmode):
            with contextlib.ExitStack() as s_ln:
                ln_t = s_ln.enter_context(
                    nc.sbuf_tensor(sname + "ln_t", [128, NT * 16 * 128], BF16))
                ln_key = sname + "ln_t"
                ln_phase(x_src_ap, x_src_key, ln_t, ln_key, s_ln, sname + "p1")

                if True:
                    sp = s_ln
                    sbl = lambda n, sh, dt: sp.enter_context(
                        nc.sbuf_tensor(sname + n, sh, dt))
                    wg2 = [sbl(f"wg{i}", [128, 16 * 512], BF16) for i in range(2)]
                    wu2 = [sbl(f"wu{i}", [128, 16 * 512], BF16) for i in range(2)]
                    hst2 = [sbl(f"hst{i}", [128, 4 * TOK], BF16) for i in range(1)]
                    hs2 = [sbl(f"hs{i}", [128, 512], F32) for i in range(2)]
                    hbuf2 = [sbl(f"hbuf{i}", [128, 512], F32) for i in range(2)]
                    hsq2 = [sbl(f"hsq{i}", [128, 512], F32) for i in range(2)]
                    tp2 = [sbl(f"tp{i}", [128, 256], F32) for i in range(2)]
                    tq2 = [sbl(f"tq{i}", [128, 256], F32) for i in range(2)]
                    tthr2 = [sbl(f"tthr{i}", [128, 128], F32) for i in range(2)]
                    tmn2 = [sbl(f"tmn{i}", [128, 128], F32) for i in range(2)]
                    tmask2 = [sbl(f"tmask{i}", [128, 512], BF16) for i in range(2)]
                    hsp_bf2 = [sbl(f"hsp_bf{i}", [128, 512], BF16) for i in range(2)]

                    def w_load(j, wbuf, wkey, src, scr):
                        if wmode == "read":
                            sch.op("sync", lambda sy: sy.dma_start(
                                wbuf[:, :],
                                bass.AP(scr, j * 128 * 8192,
                                        [[8192, 128], [1, 8192]])),
                                writes=[wkey], dma=True, tag=f"{sname}w{j}")
                        else:
                            sch.op("gpsimd", lambda g: g.dma_start(
                                wbuf[:, :].rearrange("p (k n) -> p k n", n=512),
                                src[:, j * 512:(j + 1) * 512].rearrange(
                                    "(k p) n -> p k n", p=128)),
                                writes=[wkey], dma=True, tag=f"{sname}w{j}")
                            sch.op("sync", lambda sy: sy.dma_start(
                                bass.AP(scr, j * 128 * 8192,
                                        [[8192, 128], [1, 8192]]),
                                wbuf[:, :]),
                                reads=[wkey], dma=True, tag=f"{sname}wc{j}")

                    pend = []
                    for j in range(NJ):
                        wg, wgk = wg2[j % 2], f"{sname}wg{j % 2}"
                        wu, wuk = wu2[j % 2], f"{sname}wu{j % 2}"
                        hst, hstk = hst2[0], f"{sname}hst0"
                        w_load(j, wg, wgk, ap_in["w_gate"], wgscr)
                        w_load(j, wu, wuk, ap_in["w_up"], wuscr)
                        for t in range(NT):
                            par = (j * NT + t) % 2
                            hs, hbuf, hsq = hs2[par], hbuf2[par], hsq2[par]
                            tp, tq, tthr = tp2[par], tq2[par], tthr2[par]
                            tmn, tmask, hsp_bf = tmn2[par], tmask2[par], hsp_bf2[par]
                            hsk, hbk, hqk = f"hs{par}", f"hbuf{par}", f"hsq{par}"
                            thk, spk = f"tthr{par}", f"hsp_bf{par}"
                            psg, psg_k = next_pmm()
                            psu, psu_k = next_pmm()
                            sch.op("tensor", lambda te, tt=t, w=wg, p=psg: [
                                te.matmul(p[:, :],
                                          ln_t[:, (k * NT + tt) * 128:
                                               (k * NT + tt + 1) * 128],
                                          w[:, k * 512:(k + 1) * 512],
                                          start=(k == 0), stop=(k == 15))
                                for k in range(16)],
                                reads=[f"{ln_key}:{t}", wgk], writes=[psg_k],
                                tag=f"{sname}g{j}_{t}")
                            sch.op("tensor", lambda te, tt=t, w=wu, p=psu: [
                                te.matmul(p[:, :],
                                          ln_t[:, (k * NT + tt) * 128:
                                               (k * NT + tt + 1) * 128],
                                          w[:, k * 512:(k + 1) * 512],
                                          start=(k == 0), stop=(k == 15))
                                for k in range(16)],
                                reads=[f"{ln_key}:{t}", wuk], writes=[psu_k],
                                tag=f"{sname}u{j}_{t}")
                            for fn in pend:
                                fn()
                            pend = []
                            sch.op("scalar", lambda s, p=psg, b=hs: s.activation(
                                b[:, :], p[:, :], AF.Silu),
                                reads=[psg_k], writes=[hsk],
                                tag=f"{sname}si{j}_{t}")
                            sch.op("vector", lambda v, p=psu, a=hs, b=hbuf:
                                   v.tensor_tensor(b[:, :], a[:, :], p[:, :],
                                                   ALU.mult),
                                reads=[hsk, psu_k], writes=[hbk],
                                tag=f"{sname}hm{j}_{t}")
                            sch.op("scalar", lambda s, a=hbuf, b=hsq: s.activation(
                                b[:, :], a[:, :], AF.Square),
                                reads=[hbk], writes=[hqk],
                                tag=f"{sname}sq{j}_{t}")
                            sch.op("vector", lambda v, a=hsq, b=tp, c=tq, d=tthr,
                                   e=tmn: [
                                v.tensor_tensor(b[:, :], a[:, 0:512:2],
                                                a[:, 1:512:2], ALU.max),
                                v.tensor_tensor(c[:, :], a[:, 0:512:2],
                                                a[:, 1:512:2], ALU.min),
                                v.tensor_tensor(d[:, :], b[:, 0:256:2],
                                                b[:, 1:256:2], ALU.min),
                                v.tensor_tensor(e[:, :], c[:, 0:256:2],
                                                c[:, 1:256:2], ALU.max),
                                v.tensor_tensor(d[:, :], d[:, :],
                                                e[:, :], ALU.max)],
                                reads=[hqk], writes=[thk],
                                tag=f"{sname}tr{j}_{t}")
                            sch.op("vector", lambda v, a=hsq, d=tthr, m=tmask,
                                   hb=hbuf, o=hsp_bf: [
                                v.tensor_tensor(
                                    m[:, :].rearrange("p (g k) -> p g k", k=4),
                                    a[:, :].rearrange("p (g k) -> p g k", k=4),
                                    d[:, :].unsqueeze(2).broadcast_to(
                                        [128, 128, 4]),
                                    ALU.is_ge),
                                v.tensor_tensor(o[:, :], hb[:, :],
                                                m[:, :], ALU.mult)],
                                reads=[hqk, thk, hbk], writes=[spk],
                                barrier=True, tag=f"{sname}mk{j}_{t}")

                            def mk_tr(jj, tt, hh, hhk, sb, sbk):
                                def go():
                                    transpose4_to(
                                        [sb[:, ic * 128:(ic + 1) * 128]
                                         for ic in range(4)], sbk,
                                        bass.AP(hh, tt * 128,
                                                [[4 * TOK, 128], [TOK, 4],
                                                 [1, 128]]),
                                        hhk, tg=f"{sname}ht{jj}_{tt}")
                                return go
                            pend.append(mk_tr(j, t, hst, hstk, hsp_bf, spk))
                        for fn in pend:
                            fn()
                        pend = []
                        sch.op("sync", lambda sy, jj=j, b=hst: sy.dma_start(
                            bass.AP(hsp_d, jj * 512 * TOK,
                                    [[TOK, 128], [128 * TOK, 4], [1, TOK]]),
                            b[:, :].rearrange("p (c n) -> p c n", n=TOK)),
                            reads=[hstk], writes=["hsp_d"], dma=True,
                            tag=f"{sname}sp{j}")
            sch.fence()

            with contextlib.ExitStack() as sp:
                sbl = lambda n, sh, dt: sp.enter_context(
                    nc.sbuf_tensor(sname + "b" + n, sh, dt))
                wdn2 = [sbl(f"wdn{i}", [128, 64 * 512], BF16) for i in range(2)]
                hti2 = [sbl(f"hti{i}", [128, 64 * 256], BF16) for i in range(2)]
                xt2 = [sbl(f"xt{i}", [128, 512], F32) for i in range(2)]
                xo2 = [sbl(f"xo{i}", [128, 512], F32) for i in range(2)]
                def wdn_piece(qq, pp):
                    w, wk = wdn2[qq % 2], f"{sname}bwdn{qq % 2}"
                    if wmode == "read":
                        sch.op("gpsimd", lambda g, qq=qq, pp=pp, w=w:
                               g.dma_start(
                                   w[:, pp * 4096:(pp + 1) * 4096],
                                   bass.AP(wdscr,
                                           qq * 128 * 32768 + pp * 4096,
                                           [[32768, 128], [1, 4096]])),
                               writes=[wk + f"p{pp}"], dma=True,
                               tag=f"{sname}wd{qq}_{pp}")
                    else:
                        sch.op("gpsimd", lambda g, qq=qq, pp=pp, w=w:
                               g.dma_start(
                                   w[:, pp * 4096:(pp + 1) * 4096].rearrange(
                                       "p (k n) -> p k n", n=512),
                                   bass.AP(ap_in["w_down"].tensor,
                                           pp * 1024 * 2048 + qq * 512,
                                           [[2048, 128], [128 * 2048, 8],
                                            [1, 512]])),
                               writes=[wk + f"p{pp}"], dma=True,
                               tag=f"{sname}wd{qq}_{pp}")
                        sch.op("sync", lambda sy, qq=qq, pp=pp, w=w:
                               sy.dma_start(
                                   bass.AP(wdscr,
                                           qq * 128 * 32768 + pp * 4096,
                                           [[32768, 128], [1, 4096]]),
                                   w[:, pp * 4096:(pp + 1) * 4096]),
                               reads=[wk + f"p{pp}"], dma=True,
                               tag=f"{sname}wdc{qq}_{pp}")

                def wdn_load(qq):
                    for pp in range(8):
                        wdn_piece(qq, pp)
                wdn_load(0)
                for qr in range(4):
                    wdn, wdnk = wdn2[qr % 2], f"{sname}bwdn{qr % 2}"
                    for quad in range(8):
                        if qr < 3:
                            wdn_piece(qr + 1, quad)
                        hti, htik = hti2[quad % 2], f"{sname}bhti{quad % 2}"
                        sch.op("gpsimd", lambda g, qd=quad, b=hti: g.dma_start(
                            b[:, :].rearrange("p (k n) -> p k n", n=256),
                            bass.AP(hsp_d, qd * 256,
                                    [[TOK, 128], [128 * TOK, 64], [1, 256]])),
                            reads=["hsp_d"], writes=[htik], dma=True,
                            tag=f"{sname}lh{qr}_{quad}")
                        for tq in range(2):
                            t = quad * 2 + tq
                            xt, xtk = xt2[t % 2], f"{sname}bxt{t % 2}"
                            sch.op("sync", lambda sy, tt=t, qq=qr, b=xt:
                                   sy.dma_start(
                                       b[:, :],
                                       x_src_ap[tt * 128:(tt + 1) * 128,
                                                qq * 512:(qq + 1) * 512]),
                                   reads=[f"{x_src_key}:{t}"], writes=[xtk],
                                   dma=True, tag=f"{sname}lx{qr}_{t}")
                            xo, xok = xo2[t % 2], f"{sname}bxo{t % 2}"
                            psd, psd_k = next_pmm()
                            sch.op("tensor", lambda te, b=hti, w=wdn, p=psd,
                                   tb=tq * 128: [
                                te.matmul(p[:, :],
                                          b[:, k * 256 + tb:k * 256 + tb + 128],
                                          w[:, k * 512:(k + 1) * 512],
                                          start=(k == 0), stop=(k == 63))
                                for k in range(64)],
                                reads=[htik] + [f"{wdnk}p{pp}"
                                                for pp in range(8)],
                                writes=[psd_k],
                                tag=f"{sname}d{qr}_{t}")
                            sch.op("vector", lambda v, b=xt, o=xo, p=psd:
                                   v.scalar_tensor_tensor(
                                       o[:, :], p[:, :], 0.5, b[:, :],
                                       ALU.mult, ALU.add),
                                   reads=[psd_k, xtk], writes=[xok],
                                   tag=f"{sname}e{qr}_{t}")
                            sch.op("sync", lambda sy, tt=t, qq=qr, o=xo:
                                   sy.dma_start(
                                       x_out_ap[tt * 128:(tt + 1) * 128,
                                                qq * 512:(qq + 1) * 512],
                                       o[:, :]),
                                   reads=[xok], writes=[f"{x_out_key}:{t}"],
                                   dma=True, tag=f"{sname}sx{qr}_{t}")
            sch.fence()

        def stage_attn(x_src_ap, x_src_key, x_out_ap, x_out_key):
            sname = "C"
            with contextlib.ExitStack() as sc:
                sbl = lambda n, sh, dt: sc.enter_context(
                    nc.sbuf_tensor(sname + n, sh, dt))
                cqT = sbl("cqT", [128, 4 * NT * 128], BF16)
                ckvT = sbl("ckvT", [128, 4 * NT * 128], BF16)
                with contextlib.ExitStack() as s_ln:
                    ln_t = s_ln.enter_context(
                        nc.sbuf_tensor("Cln_t", [128, NT * 16 * 128], BF16))
                    ln_key = "Cln_t"
                    with contextlib.ExitStack() as s_p0:
                        ln_phase(x_src_ap, x_src_key, ln_t, ln_key, s_p0, "Cp0")
                    sch.fence()

                    with contextlib.ExitStack() as sp:
                        sbp = lambda n, sh, dt: sp.enter_context(
                            nc.sbuf_tensor(sname + "1" + n, sh, dt))
                        wc_sb = sbp("wc", [128, 16 * 512], BF16)
                        wckv_sb = sbp("wckv", [128, 16 * 512], BF16)
                        bc_r = sbp("bc", [1, DC], BF16)
                        bckv_r = sbp("bckv", [1, DC], BF16)
                        cq_bf2 = [sbp(f"cq_bf{i}", [128, 512], BF16) for i in range(2)]
                        sch.op("gpsimd", lambda g: [
                            g.dma_start(wc_sb[:, :].rearrange(
                                "p (k n) -> p k n", n=512),
                                ap_in["Wc"].rearrange("(k p) n -> p k n", p=128)),
                            g.dma_start(wckv_sb[:, :].rearrange(
                                "p (k n) -> p k n", n=512),
                                ap_in["Wckv"].rearrange("(k p) n -> p k n", p=128)),
                            g.dma_start(bc_r[:, :], ap_in["bc"].unsqueeze(0)),
                            g.dma_start(bckv_r[:, :], ap_in["bckv"].unsqueeze(0))],
                            writes=["wc", "wckv", "bc", "bckv"], dma=True,
                            n_dma=4, tag="Cwc")
                        pend1 = []
                        for t in range(NT):
                            for ii, (wsb, wk_, br, brk, dstT, dstTk) in enumerate([
                                    (wc_sb, "wc", bc_r, "bc", cqT, "cqT"),
                                    (wckv_sb, "wckv", bckv_r, "bckv", ckvT, "ckvT")]):
                                cq_bf, cqbk = cq_bf2[ii], f"cq_bf{ii}"
                                ps, ps_k = next_pmm()
                                sch.op("tensor", lambda te, tt=t, w=wsb, b=br,
                                       p=ps: [
                                    te.matmul(p[:, :],
                                              ln_t[:, (k * NT + tt) * 128:
                                                   (k * NT + tt + 1) * 128],
                                              w[:, k * 512:(k + 1) * 512],
                                              start=(k == 0), stop=False)
                                    for k in range(16)] + [
                                    te.matmul(p[:, :], ones_row[:, :], b[:, :],
                                              start=False, stop=True)],
                                    reads=[f"{ln_key}:{t}", wk_, brk, "ones"],
                                    writes=[ps_k], tag=f"Ccq{t}")
                                sch.op("scalar", lambda s, p=ps, b=cq_bf: s.copy(
                                    b[:, :], p[:, :]),
                                    reads=[ps_k], writes=[cqbk],
                                    tag=f"Ccqe{t}")
                                for fn in pend1:
                                    fn()
                                pend1 = []

                                def mk_p1tr(tt, ii_, cq_bf=cq_bf, cqbk=cqbk,
                                            dstT=dstT, dstTk=dstTk):
                                    def go():
                                        transpose4_to(
                                            [cq_bf[:, k * 128:(k + 1) * 128]
                                             for k in range(4)], cqbk,
                                            bass.AP(dstT, tt * 128,
                                                    [[4 * NT * 128, 128],
                                                     [NT * 128, 4], [1, 128]]),
                                            f"{dstTk}:{tt}",
                                            tg=f"Ccqt{tt}_{ii_}")
                                    return go
                                pend1.append(mk_p1tr(t, ii))
                        for fn in pend1:
                            fn()
                    sch.fence()

                    with contextlib.ExitStack() as sp:
                        sbp = lambda n, sh, dt: sp.enter_context(
                            nc.sbuf_tensor(sname + "kr" + n, sh, dt))
                        wkr_sb = sbp("wkr", [128, 16 * 1024], BF16)
                        bkr_r = sbp("bkr", [1, 1024], BF16)
                        qrf2 = [sbp(f"qrf{i}", [128, 1024], F32) for i in range(2)]
                        qrb2 = [sbp(f"qrb{i}", [128, 1024], BF16) for i in range(2)]
                        cosb = [sbp(f"cos{i}", [128, 512], F32) for i in range(2)]
                        sinb = [sbp(f"sin{i}", [128, 512], F32) for i in range(2)]
                        ropet = sbp("rt", [128, 512], F32)
                        ropeu = sbp("ru", [128, 512], F32)
                        lo2 = [sbp(f"lo{i}", [64, 2048], BF16) for i in range(2)]
                        sch.op("gpsimd", lambda g: [
                            g.dma_start(wkr_sb[:, :].rearrange(
                                "p (k n) -> p k n", n=1024),
                                ap_in["Wkr"].rearrange("(k p) n -> p k n", p=128)),
                            g.dma_start(bkr_r[:, :], ap_in["bkr"].unsqueeze(0))],
                            writes=["wkr", "krb"], dma=True, n_dma=2, tag="Ckrw")
                        pendk = []
                        for t in range(NT):
                            par = t % 2
                            qrf, qrb = qrf2[par], qrb2[par]
                            qrfk, qrbk = f"krqrf{par}", f"krqrb{par}"
                            lo, lok = lo2[par], f"krlo{par}"
                            cs, csk = cosb[par], f"krcos{par}"
                            sn, snk = sinb[par], f"krsin{par}"
                            sch.op("sync", lambda sy, tt=t, a=cs, b=sn: [
                                sy.dma_start(a[:, :], ap_in["ropecos"][
                                    tt * 128:(tt + 1) * 128, :]),
                                sy.dma_start(b[:, :], ap_in["ropesin"][
                                    tt * 128:(tt + 1) * 128, :])],
                                writes=[csk, snk], dma=True, n_dma=2,
                                tag=f"Ckrc{t}")
                            for nck in range(2):
                                ps, ps_k = next_pmm()
                                sch.op("tensor", lambda te, tt=t, cc=nck, p=ps: [
                                    te.matmul(p[:, :],
                                              ln_t[:, (k * NT + tt) * 128:
                                                   (k * NT + tt + 1) * 128],
                                              wkr_sb[:, k * 1024 + cc * 512:
                                                     k * 1024 + cc * 512 + 512],
                                              start=(k == 0), stop=False)
                                    for k in range(16)] + [
                                    te.matmul(p[:, :], ones_row[:, :],
                                              bkr_r[:, cc * 512:(cc + 1) * 512],
                                              start=False, stop=True)],
                                    reads=[f"{ln_key}:{t}", "wkr", "krb", "ones"],
                                    writes=[ps_k], tag=f"Ckr{t}_{nck}")
                                sch.op("scalar", lambda s, cc=nck, p=ps, b=qrf:
                                       s.copy(b[:, cc * 512:(cc + 1) * 512],
                                              p[:, :]),
                                       reads=[ps_k], writes=[qrfk],
                                       tag=f"Ckre{t}_{nck}")
                            for fn in pendk:
                                fn()
                            pendk = []
                            q3 = qrf[:, :].rearrange("p (h r) -> p h r", r=64)
                            o3 = qrb[:, :].rearrange("p (h r) -> p h r", r=64)
                            c3 = cs[:, :].rearrange("p (h r) -> p h r", r=32)
                            s3 = sn[:, :].rearrange("p (h r) -> p h r", r=32)
                            t3 = ropet[:, :].rearrange("p (h r) -> p h r", r=32)
                            u3 = ropeu[:, :].rearrange("p (h r) -> p h r", r=32)
                            sch.op("vector", lambda v, t3=t3, u3=u3, q3=q3,
                                   c3=c3, s3=s3, o3=o3: [
                                v.tensor_tensor(t3, q3[:, :, 0:32], c3, ALU.mult),
                                v.tensor_tensor(u3, q3[:, :, 32:64], s3, ALU.mult),
                                v.tensor_tensor(o3[:, :, 0:32], t3, u3,
                                                ALU.subtract),
                                v.tensor_tensor(t3, q3[:, :, 32:64], c3, ALU.mult),
                                v.tensor_tensor(u3, q3[:, :, 0:32], s3, ALU.mult),
                                v.tensor_tensor(o3[:, :, 32:64], t3, u3, ALU.add)],
                                reads=[qrfk, csk, snk], writes=[qrbk],
                                tag=f"Ckro{t}")

                            def mk_krtr(tt, qrb=qrb, qrbk=qrbk, lo=lo, lok=lok):
                                def go():
                                    for h0 in range(0, H, 4):
                                        transpose4_to(
                                            [qrb[:, (h0 + h) * 64:
                                                 (h0 + h + 1) * 64]
                                             for h in range(4)], qrbk,
                                            bass.AP(lo, h0,
                                                    [[2048, 64], [1, 4],
                                                     [16, 128]]),
                                            lok, nparts=64,
                                            tg=f"Ckrt{tt}_{h0}")
                                    sch.op("sync", lambda sy, b=lo: sy.dma_start(
                                        bass.AP(qk_d, (tt * 384 + 320) * TOK,
                                                [[TOK, 64], [1, 2048]]),
                                        b[0:64, :]),
                                        reads=[lok], writes=[f"qk_d:{tt}"],
                                        dma=True, tag=f"Ckrs{tt}")
                                return go
                            pendk.append(mk_krtr(t))
                        for fn in pendk:
                            fn()
                    sch.fence()

                with contextlib.ExitStack() as sp:
                    sbp = lambda n, sh, dt: sp.enter_context(
                        nc.sbuf_tensor(sname + "m" + n, sh, dt))
                    wq_sb = sbp("wq", [128, 4 * 2048], BF16)
                    wqr_sb = sbp("wqr", [128, 4 * 1024], BF16)
                    wk_sb = sbp("wk", [128, 4 * 2048], BF16)
                    wv_sb = sbp("wv", [128, 4 * 2048], BF16)
                    bq_r = sbp("bq", [1, D], BF16)
                    bqr_r = sbp("bqr", [1, 1024], BF16)
                    bk_r = sbp("bk", [1, D], BF16)
                    bv_r = sbp("bv", [1, D], BF16)
                    sch.op("gpsimd", lambda g: [
                        g.dma_start(wq_sb[:, :].rearrange(
                            "p (k n) -> p k n", n=2048),
                            ap_in["Wcq"].rearrange("(k p) n -> p k n", p=128)),
                        g.dma_start(bq_r[:, :], ap_in["bcq"].unsqueeze(0)),
                        g.dma_start(wqr_sb[:, :].rearrange(
                            "p (k n) -> p k n", n=1024),
                            ap_in["Wqr"].rearrange("(k p) n -> p k n", p=128)),
                        g.dma_start(bqr_r[:, :], ap_in["bqr"].unsqueeze(0))],
                        writes=["wq", "bq", "wqr", "bqr"], dma=True, n_dma=4,
                        tag="Cmw1")
                    sch.op("gpsimd", lambda g: [
                        g.dma_start(wk_sb[:, :].rearrange(
                            "p (k n) -> p k n", n=2048),
                            ap_in["Wck"].rearrange("(k p) n -> p k n", p=128)),
                        g.dma_start(bk_r[:, :], ap_in["bck"].unsqueeze(0)),
                        g.dma_start(wv_sb[:, :].rearrange(
                            "p (k n) -> p k n", n=2048),
                            ap_in["Wv"].rearrange("(k p) n -> p k n", p=128)),
                        g.dma_start(bv_r[:, :], ap_in["bv"].unsqueeze(0))],
                        writes=["wk", "bk", "wv", "bv"], dma=True, n_dma=4,
                        tag="Cmw2")
                    qf2 = [sbp(f"qf{i}", [128, 2048], BF16) for i in range(2)]
                    kf2 = [sbp(f"kf{i}", [128, 2048], BF16) for i in range(2)]
                    vnb2 = [sbp(f"vnb{i}", [128, 2048], BF16) for i in range(2)]
                    qrf2 = [sbp(f"qrf{i}", [128, 1024], F32) for i in range(2)]
                    qrb2 = [sbp(f"qrb{i}", [128, 1024], BF16) for i in range(2)]
                    cosb = [sbp(f"cos{i}", [128, 512], F32) for i in range(2)]
                    sinb = [sbp(f"sin{i}", [128, 512], F32) for i in range(2)]
                    ropet = sbp("rt", [128, 512], F32)
                    ropeu = sbp("ru", [128, 512], F32)
                    hi2 = [sbp(f"hi{i}", [128, 2048], BF16) for i in range(2)]
                    lo2 = [sbp(f"lo{i}", [64, 2048], BF16) for i in range(2)]
                    pendm = []
                    for t in range(NT):
                        par = t % 2
                        qf, kf, vnb = qf2[par], kf2[par], vnb2[par]
                        qfk, kfk, vnbk = f"mqf{par}", f"mkf{par}", f"mvnb{par}"
                        qrf, qrb = qrf2[par], qrb2[par]
                        qrfk, qrbk = f"mqrf{par}", f"mqrb{par}"
                        hi, lo = hi2[par], lo2[par]
                        hik, lok = f"mhi{par}", f"mlo{par}"
                        cs, csk = cosb[par], f"mcos{par}"
                        sn, snk = sinb[par], f"msin{par}"
                        sch.op("sync", lambda sy, tt=t, a=cs, b=sn: [
                            sy.dma_start(a[:, :], ap_in["ropecos"][
                                tt * 128:(tt + 1) * 128, :]),
                            sy.dma_start(b[:, :], ap_in["ropesin"][
                                tt * 128:(tt + 1) * 128, :])],
                            writes=[csk, snk], dma=True, n_dma=2,
                            tag=f"Cmrc{t}")

                        def proj4(tt, cTk_, w_sb, wk_, b_r, brk, out, outk, tg):
                            cT_ = cqT if cTk_ == "cqT" else ckvT
                            for nck in range(4):
                                ps, ps_k = next_pmm()
                                sch.op("tensor", lambda te, cc=nck, p=ps,
                                       w=w_sb, b=b_r, c=cT_: [
                                    te.matmul(p[:, :],
                                              c[:, (k * NT + tt) * 128:
                                                (k * NT + tt + 1) * 128],
                                              w[:, k * 2048 + cc * 512:
                                                k * 2048 + cc * 512 + 512],
                                              start=(k == 0), stop=False)
                                    for k in range(4)] + [
                                    te.matmul(p[:, :], ones_row[:, :],
                                              b[:, cc * 512:(cc + 1) * 512],
                                              start=False, stop=True)],
                                    reads=[f"{cTk_}:{tt}", wk_, brk, "ones"],
                                    writes=[ps_k], tag=f"{tg}{tt}_{nck}")
                                sch.op("scalar", lambda s, cc=nck, p=ps, b=out:
                                       s.copy(b[:, cc * 512:(cc + 1) * 512],
                                              p[:, :]),
                                       reads=[ps_k], writes=[outk],
                                       tag=f"{tg}e{tt}_{nck}")

                        proj4(t, "cqT", wq_sb, "wq", bq_r, "bq", qf, qfk, "Cmq")
                        for nck in range(2):
                            ps, ps_k = next_pmm()
                            sch.op("tensor", lambda te, tt=t, cc=nck, p=ps: [
                                te.matmul(p[:, :],
                                          cqT[:, (k * NT + tt) * 128:
                                              (k * NT + tt + 1) * 128],
                                          wqr_sb[:, k * 1024 + cc * 512:
                                                 k * 1024 + cc * 512 + 512],
                                          start=(k == 0), stop=False)
                                for k in range(4)] + [
                                te.matmul(p[:, :], ones_row[:, :],
                                          bqr_r[:, cc * 512:(cc + 1) * 512],
                                          start=False, stop=True)],
                                reads=[f"cqT:{t}", "wqr", "bqr", "ones"],
                                writes=[ps_k], tag=f"Cmqr{t}_{nck}")
                            sch.op("scalar", lambda s, cc=nck, p=ps, b=qrf:
                                   s.copy(b[:, cc * 512:(cc + 1) * 512],
                                          p[:, :]),
                                   reads=[ps_k], writes=[qrfk],
                                   tag=f"Cmqre{t}_{nck}")
                        proj4(t, "ckvT", wk_sb, "wk", bk_r, "bk", kf, kfk, "Cmk")
                        proj4(t, "ckvT", wv_sb, "wv", bv_r, "bv", vnb, vnbk, "Cmv")
                        for fn in pendm:
                            fn()
                        pendm = []
                        q3 = qrf[:, :].rearrange("p (h r) -> p h r", r=64)
                        o3 = qrb[:, :].rearrange("p (h r) -> p h r", r=64)
                        c3 = cs[:, :].rearrange("p (h r) -> p h r", r=32)
                        s3 = sn[:, :].rearrange("p (h r) -> p h r", r=32)
                        t3 = ropet[:, :].rearrange("p (h r) -> p h r", r=32)
                        u3 = ropeu[:, :].rearrange("p (h r) -> p h r", r=32)
                        sch.op("vector", lambda v, t3=t3, u3=u3, q3=q3,
                               c3=c3, s3=s3, o3=o3: [
                            v.tensor_tensor(t3, q3[:, :, 0:32], c3, ALU.mult),
                            v.tensor_tensor(u3, q3[:, :, 32:64], s3, ALU.mult),
                            v.tensor_tensor(o3[:, :, 0:32], t3, u3, ALU.subtract),
                            v.tensor_tensor(t3, q3[:, :, 32:64], c3, ALU.mult),
                            v.tensor_tensor(u3, q3[:, :, 0:32], s3, ALU.mult),
                            v.tensor_tensor(o3[:, :, 32:64], t3, u3, ALU.add)],
                            reads=[qrfk, csk, snk], writes=[qrbk],
                            tag=f"Cmro{t}")
                        sch.op("sync", lambda sy, tt=t, b=vnb: sy.dma_start(
                            vscr_d.ap()[tt * 128:(tt + 1) * 128, :], b[:, :]),
                            reads=[vnbk], writes=[f"vscr:{t}"], dma=True,
                            tag=f"Cmvs{t}")

                        def mk_mtr(tt, qf=qf, qfk=qfk, kf=kf, kfk=kfk,
                                   qrb=qrb, qrbk=qrbk, hi=hi, lo=lo,
                                   hik=hik, lok=lok):
                            def go():
                                for h0 in range(0, H, 4):
                                    transpose4_to(
                                        [qf[:, (h0 + h) * 128:
                                            (h0 + h + 1) * 128]
                                         for h in range(4)], qfk,
                                        bass.AP(hi, h0, [[2048, 128], [1, 4],
                                                         [16, 128]]),
                                        hik, tg=f"Cmth{tt}_{h0}")
                                    transpose4_to(
                                        [qrb[:, (h0 + h) * 64:
                                             (h0 + h + 1) * 64]
                                         for h in range(4)], qrbk,
                                        bass.AP(lo, h0, [[2048, 64], [1, 4],
                                                         [16, 128]]),
                                        lok, nparts=64, tg=f"Cmtl{tt}_{h0}")
                                sch.op("sync", lambda sy, a=hi, b=lo: [
                                    sy.dma_start(
                                        bass.AP(qk_d, (tt * 384) * TOK,
                                                [[TOK, 128], [1, 2048]]),
                                        a[:, :]),
                                    sy.dma_start(
                                        bass.AP(qk_d, (tt * 384 + 128) * TOK,
                                                [[TOK, 64], [1, 2048]]),
                                        b[0:64, :])],
                                    reads=[hik, lok], writes=[f"qk_d:{tt}"],
                                    dma=True, n_dma=2, tag=f"Cmqs{tt}")
                                for h0 in range(0, H, 4):
                                    transpose4_to(
                                        [kf[:, (h0 + h) * 128:
                                            (h0 + h + 1) * 128]
                                         for h in range(4)], kfk,
                                        bass.AP(hi, h0, [[2048, 128], [1, 4],
                                                         [16, 128]]),
                                        hik, tg=f"Cmtk{tt}_{h0}")
                                sch.op("sync", lambda sy, a=hi: sy.dma_start(
                                    bass.AP(qk_d, (tt * 384 + 192) * TOK,
                                            [[TOK, 128], [1, 2048]]),
                                    a[:, :]),
                                    reads=[hik], writes=[f"qk_d:{tt}"],
                                    dma=True, tag=f"Cmks{tt}")
                            return go
                        pendm.append(mk_mtr(t))
                    for fn in pendm:
                        fn()
                sch.fence()
            sch.fence()

            with contextlib.ExitStack() as sp:
                sbl = lambda n, sh, dt: sp.enter_context(
                    nc.sbuf_tensor(sname + "g" + n, sh, dt))
                outT = sbl("outT", [128, NT * 2048], BF16)
                qhi2 = [sbl(f"qhi{i}", [128, 2048], BF16) for i in range(2)]
                qlo2 = [sbl(f"qlo{i}", [64, 2048], BF16) for i in range(2)]
                khi2 = [sbl(f"khi{i}", [128, 2048], BF16) for i in range(2)]
                klo2 = [sbl(f"klo{i}", [64, 2048], BF16) for i in range(2)]
                vg2 = [sbl(f"vg{i}", [128, 2048], BF16) for i in range(2)]
                g4_2 = [sbl(f"g4_{i}", [128, 512], F32) for i in range(2)]
                m4_2 = [sbl(f"m4_{i}", [128, 4], F32) for i in range(2)]
                bm4_2 = [sbl(f"bm4_{i}", [128, 4], F32) for i in range(2)]
                ein_2 = [sbl(f"ein_{i}", [128, 512], F32) for i in range(2)]
                ebf_2 = [sbl(f"ebf_{i}", [128, 512], BF16) for i in range(2)]
                rs4_2 = [sbl(f"rs4_{i}", [128, 4], F32) for i in range(2)]
                ri4_2 = [sbl(f"ri4_{i}", [128, 4], F32) for i in range(2)]
                at4_2 = [sbl(f"at4_{i}", [128, 512], BF16) for i in range(2)]
                av4_2 = [sbl(f"av4_{i}", [128, 512], BF16) for i in range(2)]
                pend7 = []
                for t in range(NT):
                    qhi, qhik = qhi2[t % 2], f"gqhi{t % 2}"
                    qlo, qlok = qlo2[t % 2], f"gqlo{t % 2}"
                    khi, khik = khi2[t % 2], f"gkhi{t % 2}"
                    klo, klok = klo2[t % 2], f"gklo{t % 2}"
                    vg, vgk = vg2[t % 2], f"gvg{t % 2}"
                    sch.op("sync", lambda sy, tt=t, a=qhi, b=qlo, c=khi, d=klo: [
                        sy.dma_start(a[:, :], bass.AP(
                            qk_d, (tt * 384) * TOK, [[TOK, 128], [1, 2048]])),
                        sy.dma_start(b[0:64, :], bass.AP(
                            qk_d, (tt * 384 + 128) * TOK, [[TOK, 64], [1, 2048]])),
                        sy.dma_start(c[:, :], bass.AP(
                            qk_d, (tt * 384 + 192) * TOK, [[TOK, 128], [1, 2048]])),
                        sy.dma_start(d[0:64, :], bass.AP(
                            qk_d, (tt * 384 + 320) * TOK, [[TOK, 64], [1, 2048]]))],
                        reads=[f"qk_d:{t}"], writes=[qhik, qlok, khik, klok],
                        dma=True, n_dma=4, tag=f"Cgl{t}")
                    sch.op("gpsimd", lambda g, tt=t, b=vg: g.dma_start(
                        b[:, :], bass.AP(vscr_d, tt * 128 * D,
                                         [[D, 8], [128, 16], [8 * D, 16],
                                          [1, 128]])),
                        reads=[f"vscr:{t}"], writes=[vgk], dma=True,
                        tag=f"Cgv{t}")
                    for blk in range(4):
                        gp = blk % 2
                        g4, m4, bm4 = g4_2[gp], m4_2[gp], bm4_2[gp]
                        ein, ebf = ein_2[gp], ebf_2[gp]
                        rs4, ri4 = rs4_2[gp], ri4_2[gp]
                        at4, av4 = at4_2[gp], av4_2[gp]
                        g4k, m4k, bm4k = f"g4_{gp}", f"m4_{gp}", f"bm4_{gp}"
                        eink, ebfk = f"ein_{gp}", f"ebf_{gp}"
                        rs4k, ri4k = f"rs4_{gp}", f"ri4_{gp}"
                        at4k, av4k = f"at4_{gp}", f"av4_{gp}"
                        psg, psg_k = next_pmm()
                        sch.op("tensor", lambda te, b_=blk, a=qhi, b=qlo,
                               c=khi, d=klo, p=psg: [
                            mm for g_ in range(4) for mm in (
                                te.matmul(p[:, g_ * 128:(g_ + 1) * 128],
                                          a[:, (b_ * 4 + g_) * 128:
                                            (b_ * 4 + g_ + 1) * 128],
                                          c[:, (b_ * 4 + g_) * 128:
                                            (b_ * 4 + g_ + 1) * 128],
                                          start=True, stop=False),
                                te.matmul(p[:, g_ * 128:(g_ + 1) * 128],
                                          b[0:64, (b_ * 4 + g_) * 128:
                                            (b_ * 4 + g_ + 1) * 128],
                                          d[0:64, (b_ * 4 + g_) * 128:
                                            (b_ * 4 + g_ + 1) * 128],
                                          start=False, stop=True))],
                            reads=[qhik, qlok, khik, klok], writes=[psg_k],
                            tag=f"Cgr{t}_{blk}")
                        for fn in pend7:
                            fn()
                        pend7 = []
                        sch.op("vector", lambda v, p=psg, a=g4, b=m4: [
                            v.tensor_tensor(
                                a[:, :].rearrange("p (g n) -> p g n", n=128),
                                p[:, :].rearrange("p (g n) -> p g n", n=128),
                                maskneg[:, :].unsqueeze(1).broadcast_to(
                                    [128, 4, 128]), ALU.add),
                            v.reduce_max(
                                b[:, :].rearrange("p (g n) -> p g n", n=1),
                                a[:, :].rearrange("p (g n) -> p g n", n=128),
                                axis=AX.X)],
                            reads=[psg_k, "maskneg"], writes=[g4k, m4k],
                            tag=f"Cgm{t}_{blk}")
                        sch.op("vector", lambda v, a=m4, c=g4, o=ein:
                               v.tensor_tensor(
                                   o[:, :].rearrange("p (g n) -> p g n", n=128),
                                   c[:, :].rearrange("p (g n) -> p g n", n=128),
                                   a[:, :].unsqueeze(2).broadcast_to(
                                       [128, 4, 128]),
                                   ALU.subtract),
                            reads=[m4k, g4k], writes=[eink],
                            barrier=True, tag=f"Cgs{t}_{blk}")
                        sch.op("scalar", lambda s, a=ein, b=ebf: s.activation(
                            b[:, :], a[:, :], AF.Exp, scale=SCALER),
                            reads=[eink], writes=[ebfk], tag=f"Cge{t}_{blk}")
                        sch.op("vector", lambda v, a=ebf, b=rs4: v.reduce_sum(
                            b[:, :].rearrange("p (g n) -> p g n", n=1),
                            a[:, :].rearrange("p (g n) -> p g n", n=128),
                            axis=AX.X),
                            reads=[ebfk], writes=[rs4k], tag=f"Cgr2{t}_{blk}")
                        sch.op("vector", lambda v, a=rs4, b=ri4: v.reciprocal(
                            b[:, :], a[:, :]),
                            reads=[rs4k], writes=[ri4k], barrier=True,
                            tag=f"Cgc{t}_{blk}")

                        def mk_av(tt, b_, vg=vg, vgk=vgk, ebf=ebf, ebfk=ebfk,
                                  at4=at4, at4k=at4k, av4=av4, av4k=av4k,
                                  ri4=ri4, ri4k=ri4k):
                            def go():
                                transpose4_to(
                                    [ebf[:, g_ * 128:(g_ + 1) * 128]
                                     for g_ in range(4)], ebfk,
                                    at4[:, :].rearrange("p (g n) -> p g n",
                                                        n=128),
                                    at4k, tg=f"Cgt{tt}_{b_}")
                                psa, psa_k = next_pmm()
                                sch.op("tensor", lambda te, p=psa: [
                                    te.matmul(p[:, g_ * 128:(g_ + 1) * 128],
                                              at4[:, g_ * 128:(g_ + 1) * 128],
                                              vg[:, (b_ * 4 + g_) * 128:
                                                 (b_ * 4 + g_ + 1) * 128],
                                              start=True, stop=True)
                                    for g_ in range(4)],
                                    reads=[at4k, vgk], writes=[psa_k],
                                    tag=f"Cga{tt}_{b_}")
                                sch.op("vector", lambda v, p=psa, o=av4:
                                       v.tensor_tensor(
                                           o[:, :].rearrange(
                                               "p (g n) -> p g n", n=128),
                                           p[:, :].rearrange(
                                               "p (g n) -> p g n", n=128),
                                           ri4[:, :].unsqueeze(2).broadcast_to(
                                               [128, 4, 128]), ALU.mult),
                                    reads=[psa_k, ri4k], writes=[av4k],
                                    tag=f"Cgae{tt}_{b_}")
                                transpose4_to(
                                    [av4[:, g_ * 128:(g_ + 1) * 128]
                                     for g_ in range(4)], av4k,
                                    outT[:, tt * 2048 + b_ * 512:
                                         tt * 2048 + (b_ + 1) * 512].rearrange(
                                        "p (g n) -> p g n", n=128),
                                    f"outT:{tt}", tg=f"Cgo{tt}_{b_}")
                            return go
                        pend7.append(mk_av(t, blk))
                for fn in pend7:
                    fn()
                pend7 = []

                for fn in pend7:
                    fn()
                pend7 = []
                wo2 = [sbl(f"wo{i}", [128, 16 * 512], BF16) for i in range(2)]
                bo_r = sbl("bo", [1, D], BF16)
                xr2 = [sbl(f"xr{i}", [128, 512], F32) for i in range(2)]
                xw2 = [sbl(f"xw{i}", [128, 512], F32) for i in range(2)]
                sch.op("gpsimd", lambda g: g.dma_start(
                    bo_r[:, :], ap_in["bo"].unsqueeze(0)),
                    writes=["bo"], dma=True, tag="Cbo")
                for ncw in range(4):
                    wo, wok = wo2[ncw % 2], f"gwo{ncw % 2}"
                    sch.op("gpsimd", lambda g, cc=ncw, b=wo: g.dma_start(
                        b[:, :].rearrange("p (k n) -> p k n", n=512),
                        ap_in["Wo"][:, cc * 512:(cc + 1) * 512].rearrange(
                            "(k p) n -> p k n", p=128)),
                        writes=[wok], dma=True, tag=f"Cgwo{ncw}")
                    for t in range(NT):
                        xr, xrk = xr2[t % 2], f"gxr{t % 2}"
                        sch.op("sync", lambda sy, tt=t, cc=ncw, b=xr:
                               sy.dma_start(
                                   b[:, :], x_src_ap[tt * 128:(tt + 1) * 128,
                                                     cc * 512:(cc + 1) * 512]),
                               reads=[f"{x_src_key}:{t}"], writes=[xrk],
                               dma=True, tag=f"Cglx{ncw}_{t}")
                        pso, pso_k = next_pmm()
                        sch.op("tensor", lambda te, tt=t, cc=ncw, w=wo, p=pso: [
                            te.matmul(p[:, :],
                                      bass.AP(outT, tt * 2048 + i,
                                              [[NT * 2048, 128], [16, 128]]),
                                      w[:, i * 512:(i + 1) * 512],
                                      start=(i == 0), stop=False)
                            for i in range(16)] + [
                            te.matmul(p[:, :], ones_row[:, :],
                                      bo_r[:, cc * 512:(cc + 1) * 512],
                                      start=False, stop=True)],
                            reads=[f"outT:{t}", wok, "bo", "ones"],
                            writes=[pso_k], tag=f"Cgw{ncw}_{t}")
                        xw, xwk = xw2[t % 2], f"gxw{t % 2}"
                        sch.op("vector", lambda v, b=xr, o=xw, p=pso:
                               v.tensor_tensor(o[:, :], p[:, :], b[:, :],
                                               ALU.add),
                               reads=[pso_k, xrk], writes=[xwk],
                               tag=f"Cgad{ncw}_{t}")
                        sch.op("sync", lambda sy, tt=t, cc=ncw, o=xw:
                               sy.dma_start(
                                   x_out_ap[tt * 128:(tt + 1) * 128,
                                            cc * 512:(cc + 1) * 512], o[:, :]),
                               reads=[xwk], writes=[f"{x_out_key}:{t}"],
                               dma=True, tag=f"Cgsx{ncw}_{t}")
            sch.fence()

        xm1 = xm1_d.ap()
        xm2 = xm2_d.ap()
        stage_ffn(ap_in["x"], "x_in", xm1, "xm1", "A", "write")
        if "xmid1" in dbg_h:
            sch.op("sync", lambda sy: [
                sy.dma_start(dbg_h["xmid1"][t * 128:(t + 1) * 128, :],
                             xm1[t * 128:(t + 1) * 128, :]) for t in range(NT)],
                reads=[f"xm1:{t}" for t in range(NT)], dma=True, n_dma=NT,
                tag="dbg1")
        stage_attn(xm1, "xm1", xm2, "xm2")
        if "qk" in dbg_h:
            sch.op("sync", lambda sy: sy.dma_start(
                dbg_h["qk"][:, :].bitcast(BF16)[:, 0:2048],
                bass.AP(qk_d, 0, [[2048, 2048], [1, 2048]])),
                reads=["qk_d"], dma=True, tag="dbgqk")
        if "vs" in dbg_h:
            sch.op("sync", lambda sy: sy.dma_start(
                dbg_h["vs"][:, :].bitcast(BF16)[:, 0:2048],
                bass.AP(vscr_d, 0, [[2048, 2048], [1, 2048]])),
                reads=["vscr"], dma=True, tag="dbgvs")
        if "xmid2" in dbg_h:
            sch.op("sync", lambda sy: [
                sy.dma_start(dbg_h["xmid2"][t * 128:(t + 1) * 128, :],
                             xm2[t * 128:(t + 1) * 128, :]) for t in range(NT)],
                reads=[f"xm2:{t}" for t in range(NT)], dma=True, n_dma=NT,
                tag="dbg2")
        stage_ffn(xm2, "xm2", out_h, "x_out", "D", "read")

        sch.emit(block)
    return nc


_NC_CACHE = {}


def _get_nc(debug=()):
    key = tuple(debug)
    if key not in _NC_CACHE:
        _NC_CACHE[key] = build_nc(debug)
    return _NC_CACHE[key]


def _host_constants():
    ident = np.eye(128, dtype=np.float32)
    maskneg = np.full((128, 128), -1e30, np.float32)
    for s8 in range(8):
        maskneg[s8 * 16:(s8 + 1) * 16, s8 * 16:(s8 + 1) * 16] = 0.0
    inv_freq = (1.0 / 10000.0 ** (np.arange(0, 64, 2, dtype=np.float32) / 64.0)
                ).astype(np.float32)
    t = np.arange(NCORES * TOK, dtype=np.float32)
    fr = t[:, None] * inv_freq[None, :]
    cos = np.tile(np.cos(fr), (1, H)).astype(np.float32)
    sin = np.tile(np.sin(fr), (1, H)).astype(np.float32)
    return ident, maskneg, cos, sin


def kernel(**inputs):
    x = np.ascontiguousarray(
        np.asarray(inputs["L0_attn_h_t"], dtype=np.float32).reshape(
            NCORES * TOK, D))
    ident, maskneg, cos, sin = _host_constants()
    wmap = {
        "Wc": "L0_attn_Wc", "bc": "L0_attn_bc", "Wcq": "L0_attn_Wcq",
        "bcq": "L0_attn_bcq", "Wqr": "L0_attn_Wqr", "bqr": "L0_attn_bqr",
        "Wckv": "L0_attn_Wckv", "bckv": "L0_attn_bckv", "Wck": "L0_attn_Wck",
        "bck": "L0_attn_bck", "Wkr": "L0_attn_Wkr", "bkr": "L0_attn_bkr",
        "Wv": "L0_attn_Wv", "bv": "L0_attn_bv", "Wo": "L0_attn_Wo",
        "bo": "L0_attn_bo", "w_gate": "L0_ffn_w_gate", "w_up": "L0_ffn_w_up",
        "w_down": "L0_ffn_w_down",
    }
    weights = {k: np.ascontiguousarray(np.asarray(inputs[v], dtype=np.float32))
               for k, v in wmap.items()}
    nc = _get_nc()
    in_maps = []
    for c in range(NCORES):
        m = dict(weights)
        m["x"] = x[c * TOK:(c + 1) * TOK]
        m["ident"] = ident
        m["maskneg"] = maskneg
        m["ropecos"] = np.ascontiguousarray(cos[c * TOK:(c + 1) * TOK])
        m["ropesin"] = np.ascontiguousarray(sin[c * TOK:(c + 1) * TOK])
        in_maps.append(m)
    res = run_bass_kernel_spmd(nc, in_maps, core_ids=list(range(NCORES)))
    out = np.concatenate([r["out"] for r in res.results], axis=0)
    return out.reshape(1, NCORES * TOK, D).astype(np.float32)


# version 17
# speedup vs baseline: 1.3333x; 1.0776x over previous
import contextlib
import numpy as np
import concourse.bass as bass
import concourse.mybir as mybir
from concourse.bass_utils import run_bass_kernel_spmd

F32 = mybir.dt.float32
BF16 = mybir.dt.bfloat16
AF = mybir.ActivationFunctionType
ALU = mybir.AluOpType
AX = mybir.AxisListType

NCORES = 8
D = 2048
TOK = 2048
NT = TOK // 128
DFF = 8192
NJ = DFF // 512
H = 16
DH = 128
DQ = 192
DC = 512
SCALER = 1.0 / float(np.sqrt(DQ))
EPS = 1e-5

ENGINES = ("tensor", "vector", "scalar", "gpsimd", "sync")


class _Op:
    __slots__ = ("idx", "engine", "emit_fn", "deps", "inc_amt", "cum_after",
                 "tag", "waits", "barrier")

    def __init__(self, idx, engine, emit_fn, deps, inc_amt, tag, barrier):
        self.idx = idx
        self.engine = engine
        self.emit_fn = emit_fn
        self.deps = deps
        self.inc_amt = inc_amt
        self.cum_after = None
        self.tag = tag
        self.barrier = barrier


class Sched:

    def __init__(self, nc):
        self.nc = nc
        self.ops = []
        self.sems = {}
        self._writers = {}
        self._epoch_read = {}
        self._readers_since = {}
        self._fence_ops = []

    def alloc_sems(self, stack):
        for e in ENGINES:
            self.sems[e] = stack.enter_context(self.nc.semaphore(f"sched_{e}"))

    def op(self, engine, emit_fn, reads=(), writes=(), deps=(), dma=False,
           n_dma=1, tag="", barrier=False):
        dd = set(d.idx for d in deps)
        for f in self._fence_ops:
            dd.add(f.idx)
        for r in reads:
            for w in self._writers.get(r, ()):
                dd.add(w.idx)
        for w in writes:
            for lw in self._writers.get(w, ()):
                dd.add(lw.idx)
            for rd in self._readers_since.get(w, ()):
                dd.add(rd.idx)
        o = _Op(len(self.ops), engine, emit_fn,
                [self.ops[i] for i in sorted(dd)],
                (16 * n_dma) if dma else 1, tag, barrier)
        self.ops.append(o)
        for r in reads:
            self._readers_since.setdefault(r, []).append(o)
            self._epoch_read[r] = True
        for w in writes:
            if self._epoch_read.get(w):
                self._writers[w] = [o]
                self._epoch_read[w] = False
            else:
                self._writers.setdefault(w, []).append(o)
            self._readers_since[w] = []
        return o

    def fence(self):
        last = {}
        for o in self.ops:
            last[o.engine] = o
        self._fence_ops = list(last.values())

    def emit(self, block):
        cum = {e: 0 for e in ENGINES}
        for o in self.ops:
            cum[o.engine] += o.inc_amt
            o.cum_after = cum[o.engine]
        per_engine = {e: [o for o in self.ops if o.engine == e] for e in ENGINES}
        for e in ENGINES:
            have = {x: 0 for x in ENGINES}
            for o in per_engine[e]:
                o.waits = []
                byeng = {}
                for d in o.deps:
                    if (d.engine == e and d.inc_amt == 1 and o.inc_amt == 1
                            and not o.barrier):
                        continue
                    byeng[d.engine] = max(byeng.get(d.engine, 0), d.cum_after)
                for en, vv in sorted(byeng.items()):
                    if vv > have[en]:
                        o.waits.append((en, vv))
                        have[en] = vv

        def mk(e):
            def body(eng):
                for o in per_engine[e]:
                    for en, vv in o.waits:
                        eng.wait_ge(self.sems[en], vv)
                    ret = o.emit_fn(eng)
                    insts = ret if isinstance(ret, (list, tuple)) else [ret]
                    insts = [i for i in insts if i is not None]
                    if not insts:
                        raise RuntimeError(f"op {o.idx} ({o.tag}) emitted nothing")
                    if o.inc_amt == 1:
                        insts[-1].then_inc(self.sems[e], 1)
                    else:
                        assert len(insts) * 16 == o.inc_amt, (o.tag, len(insts))
                        for i in insts:
                            i.then_inc(self.sems[e], 16)
            return body

        for e in ENGINES:
            if per_engine[e]:
                getattr(block, e)(mk(e))
        fin = dict(cum)

        def tail(eng):
            for e in ENGINES:
                if fin[e] and e != "sync":
                    eng.wait_ge(self.sems[e], fin[e])
        block.sync(tail)


def build_nc(debug=()):
    nc = bass.Bass("TRN2", target_bir_lowering=False)
    ap_in = {}
    shapes = {
        "x": [TOK, D], "Wc": [D, DC], "bc": [DC], "Wcq": [DC, D],
        "bcq": [D], "Wqr": [DC, 1024], "bqr": [1024], "Wckv": [D, DC],
        "bckv": [DC], "Wck": [DC, D], "bck": [D], "Wkr": [D, 1024],
        "bkr": [1024], "Wv": [DC, D], "bv": [D], "Wo": [D, D], "bo": [D],
        "w_gate": [D, DFF], "w_up": [D, DFF], "w_down": [DFF, D],
        "ident": [128, 128], "maskneg": [128, 128],
        "ropecos": [TOK, 512], "ropesin": [TOK, 512],
    }
    for n, s in shapes.items():
        ap_in[n] = nc.dram_tensor(n, s, F32, kind="ExternalInput").ap()
    out_h = nc.dram_tensor("out", [TOK, D], F32, kind="ExternalOutput").ap()
    dbg_h = {}
    for dn in debug:
        dbg_h[dn] = nc.dram_tensor("dbg_" + dn, [TOK, D], F32,
                                   kind="ExternalOutput").ap()

    hsp_d = nc.dram_tensor("hsp_scr", [DFF, TOK], BF16, kind="Internal")
    vscr_d = nc.dram_tensor("v_scr", [TOK, D], BF16, kind="Internal")
    qk_d = nc.dram_tensor("qk_scr", [NT, 384, TOK], BF16, kind="Internal")
    xm1_d = nc.dram_tensor("xmid1", [TOK, D], F32, kind="Internal")
    xm2_d = nc.dram_tensor("xmid2", [TOK, D], F32, kind="Internal")

    st = contextlib.ExitStack()
    with st:
        sbt = lambda name, shape, dt: st.enter_context(
            nc.sbuf_tensor(name, shape, dt))
        ident = sbt("sb_ident", [128, 128], BF16)
        maskneg = sbt("sb_maskneg", [128, 128], F32)
        ones_row = sbt("ones_row", [1, 128], BF16)
        stats = sbt("stats", [128, 16], F32)
        flsc = sbt("flsc", [128, 16], F32)

        sch = Sched(nc)
        sch.alloc_sems(st)
        block = st.enter_context(nc.Block())

        pmm = [st.enter_context(nc.psum_tensor(f"pmm{i}", [128, 512], F32))
               for i in range(4)]
        ptr = [st.enter_context(nc.psum_tensor(f"ptr{i}", [128, 1024], BF16))
               for i in range(2)]
        pmm_i = [0]
        ptr_i = [0]

        def next_pmm():
            i = pmm_i[0] % 4
            pmm_i[0] += 1
            return pmm[i], f"pmm{i}"

        def next_ptr():
            i = ptr_i[0] % 2
            ptr_i[0] += 1
            return ptr[i], f"ptr{i}"

        evac_i = [0]

        def evac_to(dst_ap, dst_key, ps, ps_key, nparts=128, tg="e"):
            evac_i[0] += 1
            if evac_i[0] % 2:
                sch.op("scalar", lambda s: s.copy(dst_ap, ps[0:nparts, 0:128]),
                       reads=[ps_key], writes=[dst_key], tag=tg)
            else:
                sch.op("vector", lambda v: v.tensor_copy(dst_ap, ps[0:nparts, 0:128]),
                       reads=[ps_key], writes=[dst_key], tag=tg)

        def transpose_to(src_ap, src_key, dst_ap, dst_key, nparts=128, tg="tr"):
            ps, ps_key = next_ptr()
            sch.op("tensor", lambda te: te.transpose(
                ps[0:nparts, 0:128], src_ap, ident[:, :]),
                reads=[src_key, "ident"], writes=[ps_key], tag=f"{tg}t")
            evac_to(dst_ap, dst_key, ps, ps_key, nparts, tg=f"{tg}e")

        def transpose4_to(srcs, src_key, dst_ap, dst_key, nparts=128, tg="t4"):
            ps, ps_key = next_ptr()
            sch.op("tensor", lambda te, ss=list(srcs): [
                te.transpose(ps[0:nparts, k * 128:(k + 1) * 128], ss[k],
                             ident[:, :]) for k in range(4)],
                reads=[src_key, "ident"], writes=[ps_key], tag=f"{tg}t")
            src3 = ps[0:nparts, 0:512].rearrange("p (k n) -> p k n", n=128)
            evac_i[0] += 1
            if evac_i[0] % 2:
                sch.op("scalar", lambda s: s.copy(dst_ap, src3),
                       reads=[ps_key], writes=[dst_key], tag=f"{tg}e")
            else:
                sch.op("vector", lambda v: v.tensor_copy(dst_ap, src3),
                       reads=[ps_key], writes=[dst_key], tag=f"{tg}e")

        sch.op("gpsimd", lambda g: [
            g.dma_start(ident[:, :], ap_in["ident"][:, :]),
            g.dma_start(maskneg[:, :], ap_in["maskneg"][:, :])],
            writes=["ident", "maskneg"], dma=True, n_dma=2, tag="consts")
        sch.op("vector", lambda v: v.memset(ones_row[:, :], 1.0),
               writes=["ones"], tag="ones")

        def ln_tile(xt, xt_key, xn_bf, xn_key, sq_scr, tg, par=0):
            po = 8 * par
            s1 = stats[:, po + 0:po + 1]
            negmean = stats[:, po + 1:po + 2]
            vsum = stats[:, po + 2:po + 3]
            va = stats[:, po + 3:po + 4]
            rva = stats[:, po + 4:po + 5]
            lntmp = stats[:, po + 5:po + 6]
            rstd = stats[:, po + 6:po + 7]
            sch.op("vector", lambda v: v.reduce_sum(s1, xt[:, :], axis=AX.X),
                   reads=[xt_key], writes=[f"st_s1{par}"], tag=f"{tg}ln_r")
            sch.op("vector", lambda v: [
                v.tensor_scalar_mul(negmean, s1, -1.0 / D),
                v.tensor_copy(flsc[:, 8*par:8*par+1], negmean)],
                reads=[f"st_s1{par}"], writes=[f"st_nm{par}"], barrier=True, tag=f"{tg}ln_nm")
            sch.op("scalar", lambda s: [
                s.activation(sq_scr[:, :], xt[:, :], AF.Square,
                             bias=negmean, scale=1.0, accum_out=vsum),
                s.copy(flsc[:, 8*par+1:8*par+2], vsum)],
                reads=[xt_key, f"st_nm{par}"], writes=[f"st_vs{par}", f"sq{par}"],
                tag=f"{tg}ln_sq")
            sch.op("vector", lambda v: v.tensor_scalar(
                va, vsum, 1.0 / D, EPS, ALU.mult, ALU.add),
                reads=[f"st_vs{par}"], writes=[f"st_va{par}"], tag=f"{tg}ln_va")
            sch.op("vector", lambda v: [
                v.reciprocal(rva, va),
                v.tensor_copy(flsc[:, 8*par+2:8*par+3], rva)],
                reads=[f"st_va{par}"], writes=[f"st_rva{par}"], barrier=True,
                tag=f"{tg}ln_rv")
            sch.op("scalar", lambda s: s.activation(lntmp, rva, AF.Ln),
                   reads=[f"st_rva{par}"], writes=[f"st_lt{par}"], tag=f"{tg}ln_ln")
            sch.op("scalar", lambda s: [
                s.activation(rstd, lntmp, AF.Exp, scale=0.5),
                s.copy(flsc[:, 8*par+3:8*par+4], rstd)],
                reads=[f"st_lt{par}"], writes=[f"st_rs{par}"], barrier=True,
                tag=f"{tg}ln_ex")
            sch.op("vector", lambda v: v.tensor_scalar(
                xn_bf[:, :], xt[:, :], negmean, rstd, ALU.add, ALU.mult),
                reads=[xt_key, f"st_nm{par}", f"st_rs{par}"], writes=[xn_key],
                barrier=True, tag=f"{tg}ln_o")

        def ln_phase(x_src_ap, x_src_key, ln_t, ln_key, scope, sname):
            sbl = lambda n, sh, dt: scope.enter_context(
                nc.sbuf_tensor(sname + n, sh, dt))
            xt2 = [sbl(f"xt{i}", [128, D], F32) for i in range(2)]
            xnb2 = [sbl(f"xnb{i}", [128, D], BF16) for i in range(2)]
            sq2 = [sbl(f"sq{i}", [128, D], BF16) for i in range(2)]
            for t in range(NT):
                par = t % 2
                xt, xk = xt2[par], f"{sname}xt{par}"
                xnb = xnb2[par]
                sch.op("sync", lambda sy, tt=t, b=xt: sy.dma_start(
                    b[:, :], x_src_ap[tt * 128:(tt + 1) * 128, :]),
                    reads=[f"{x_src_key}:{t}"], writes=[xk], dma=True,
                    tag=f"{sname}ldx{t}")
                ln_tile(xt, xk, xnb, f"{sname}xnb{par}", sq2[par],
                        f"{sname}{t}", par=par)
                for k0 in range(0, 16, 4):
                    transpose4_to(
                        [xnb[:, (k0 + k) * 128:(k0 + k + 1) * 128]
                         for k in range(4)],
                        f"{sname}xnb{par}",
                        bass.AP(ln_t, ((k0 * NT + t) * 128),
                                [[NT * 16 * 128, 128], [NT * 128, 4], [1, 128]]),
                        f"{ln_key}:{t}", tg=f"{sname}lt{t}_{k0}")

        def stage_ffn(x_src_ap, x_src_key, x_out_ap, x_out_key, sname):
            with contextlib.ExitStack() as s_ln:
                ln_t = s_ln.enter_context(
                    nc.sbuf_tensor(sname + "ln_t", [128, NT * 16 * 128], BF16))
                ln_key = sname + "ln_t"
                ln_phase(x_src_ap, x_src_key, ln_t, ln_key, s_ln, sname + "p1")

                if True:
                    sp = s_ln
                    sbl = lambda n, sh, dt: sp.enter_context(
                        nc.sbuf_tensor(sname + n, sh, dt))
                    wg2 = [sbl(f"wg{i}", [128, 16 * 512], BF16) for i in range(2)]
                    wu2 = [sbl(f"wu{i}", [128, 16 * 512], BF16) for i in range(2)]
                    hst2 = [sbl(f"hst{i}", [128, 4 * TOK], BF16) for i in range(1)]
                    hs2 = [sbl(f"hs{i}", [128, 512], F32) for i in range(2)]
                    hbuf2 = [sbl(f"hbuf{i}", [128, 512], F32) for i in range(2)]
                    hsq2 = [sbl(f"hsq{i}", [128, 512], F32) for i in range(2)]
                    tp2 = [sbl(f"tp{i}", [128, 256], F32) for i in range(2)]
                    tq2 = [sbl(f"tq{i}", [128, 256], F32) for i in range(2)]
                    tthr2 = [sbl(f"tthr{i}", [128, 128], F32) for i in range(2)]
                    tmn2 = [sbl(f"tmn{i}", [128, 128], F32) for i in range(2)]
                    tmask2 = [sbl(f"tmask{i}", [128, 512], BF16) for i in range(2)]
                    hsp_bf2 = [sbl(f"hsp_bf{i}", [128, 512], BF16) for i in range(2)]

                    def w_load(j, wbuf, wkey, src):
                        sch.op("gpsimd", lambda g: g.dma_start(
                            wbuf[:, :].rearrange("p (k n) -> p k n", n=512),
                            src[:, j * 512:(j + 1) * 512].rearrange(
                                "(k p) n -> p k n", p=128)),
                            writes=[wkey], dma=True, tag=f"{sname}w{j}")

                    pend = []
                    for j in range(NJ):
                        wg, wgk = wg2[j % 2], f"{sname}wg{j % 2}"
                        wu, wuk = wu2[j % 2], f"{sname}wu{j % 2}"
                        hst, hstk = hst2[0], f"{sname}hst0"
                        w_load(j, wg, wgk, ap_in["w_gate"])
                        w_load(j, wu, wuk, ap_in["w_up"])
                        for t in range(NT):
                            par = (j * NT + t) % 2
                            hs, hbuf, hsq = hs2[par], hbuf2[par], hsq2[par]
                            tp, tq, tthr = tp2[par], tq2[par], tthr2[par]
                            tmn, tmask, hsp_bf = tmn2[par], tmask2[par], hsp_bf2[par]
                            hsk, hbk, hqk = f"hs{par}", f"hbuf{par}", f"hsq{par}"
                            thk, spk = f"tthr{par}", f"hsp_bf{par}"
                            psg, psg_k = next_pmm()
                            psu, psu_k = next_pmm()
                            sch.op("tensor", lambda te, tt=t, w=wg, p=psg: [
                                te.matmul(p[:, :],
                                          ln_t[:, (k * NT + tt) * 128:
                                               (k * NT + tt + 1) * 128],
                                          w[:, k * 512:(k + 1) * 512],
                                          start=(k == 0), stop=(k == 15))
                                for k in range(16)],
                                reads=[f"{ln_key}:{t}", wgk], writes=[psg_k],
                                tag=f"{sname}g{j}_{t}")
                            sch.op("tensor", lambda te, tt=t, w=wu, p=psu: [
                                te.matmul(p[:, :],
                                          ln_t[:, (k * NT + tt) * 128:
                                               (k * NT + tt + 1) * 128],
                                          w[:, k * 512:(k + 1) * 512],
                                          start=(k == 0), stop=(k == 15))
                                for k in range(16)],
                                reads=[f"{ln_key}:{t}", wuk], writes=[psu_k],
                                tag=f"{sname}u{j}_{t}")
                            for fn in pend:
                                fn()
                            pend = []
                            sch.op("scalar", lambda s, p=psg, b=hs: s.activation(
                                b[:, :], p[:, :], AF.Silu),
                                reads=[psg_k], writes=[hsk],
                                tag=f"{sname}si{j}_{t}")
                            sch.op("vector", lambda v, p=psu, a=hs, b=hbuf:
                                   v.tensor_tensor(b[:, :], a[:, :], p[:, :],
                                                   ALU.mult),
                                reads=[hsk, psu_k], writes=[hbk],
                                tag=f"{sname}hm{j}_{t}")
                            sch.op("scalar", lambda s, a=hbuf, b=hsq: s.activation(
                                b[:, :], a[:, :], AF.Square),
                                reads=[hbk], writes=[hqk],
                                tag=f"{sname}sq{j}_{t}")
                            sch.op("vector", lambda v, a=hsq, b=tp, c=tq, d=tthr,
                                   e=tmn: [
                                v.tensor_tensor(b[:, :], a[:, 0:512:2],
                                                a[:, 1:512:2], ALU.max),
                                v.tensor_tensor(c[:, :], a[:, 0:512:2],
                                                a[:, 1:512:2], ALU.min),
                                v.tensor_tensor(d[:, :], b[:, 0:256:2],
                                                b[:, 1:256:2], ALU.min),
                                v.tensor_tensor(e[:, :], c[:, 0:256:2],
                                                c[:, 1:256:2], ALU.max),
                                v.tensor_tensor(d[:, :], d[:, :],
                                                e[:, :], ALU.max)],
                                reads=[hqk], writes=[thk],
                                tag=f"{sname}tr{j}_{t}")
                            sch.op("vector", lambda v, a=hsq, d=tthr, m=tmask,
                                   hb=hbuf, o=hsp_bf: [
                                v.tensor_tensor(
                                    m[:, :].rearrange("p (g k) -> p g k", k=4),
                                    a[:, :].rearrange("p (g k) -> p g k", k=4),
                                    d[:, :].unsqueeze(2).broadcast_to(
                                        [128, 128, 4]),
                                    ALU.is_ge),
                                v.tensor_tensor(o[:, :], hb[:, :],
                                                m[:, :], ALU.mult)],
                                reads=[hqk, thk, hbk], writes=[spk],
                                barrier=True, tag=f"{sname}mk{j}_{t}")

                            def mk_tr(jj, tt, hh, hhk, sb, sbk):
                                def go():
                                    transpose4_to(
                                        [sb[:, ic * 128:(ic + 1) * 128]
                                         for ic in range(4)], sbk,
                                        bass.AP(hh, tt * 128,
                                                [[4 * TOK, 128], [TOK, 4],
                                                 [1, 128]]),
                                        hhk, tg=f"{sname}ht{jj}_{tt}")
                                return go
                            pend.append(mk_tr(j, t, hst, hstk, hsp_bf, spk))
                        for fn in pend:
                            fn()
                        pend = []
                        sch.op("sync", lambda sy, jj=j, b=hst: sy.dma_start(
                            bass.AP(hsp_d, jj * 512 * TOK,
                                    [[TOK, 128], [128 * TOK, 4], [1, TOK]]),
                            b[:, :].rearrange("p (c n) -> p c n", n=TOK)),
                            reads=[hstk], writes=["hsp_d"], dma=True,
                            tag=f"{sname}sp{j}")
            sch.fence()

            with contextlib.ExitStack() as sp:
                sbl = lambda n, sh, dt: sp.enter_context(
                    nc.sbuf_tensor(sname + "b" + n, sh, dt))
                wdn2 = [sbl(f"wdn{i}", [128, 64 * 512], BF16) for i in range(2)]
                hti2 = [sbl(f"hti{i}", [128, 64 * 256], BF16) for i in range(2)]
                xt2 = [sbl(f"xt{i}", [128, 512], F32) for i in range(2)]
                xo2 = [sbl(f"xo{i}", [128, 512], F32) for i in range(2)]
                def wdn_load(qq):
                    w, wk = wdn2[qq % 2], f"{sname}bwdn{qq % 2}"
                    sch.op("gpsimd", lambda g, qq=qq, w=w: g.dma_start(
                        w[:, :].rearrange("p (k n) -> p k n", n=512),
                        ap_in["w_down"][:, qq * 512:(qq + 1) * 512].rearrange(
                            "(k p) n -> p k n", p=128)),
                        writes=[wk], dma=True, tag=f"{sname}wd{qq}")
                wdn_load(0)
                for qr in range(4):
                    wdn, wdnk = wdn2[qr % 2], f"{sname}bwdn{qr % 2}"
                    for quad in range(8):
                        if quad == 1 and qr < 3:
                            wdn_load(qr + 1)
                        hti, htik = hti2[quad % 2], f"{sname}bhti{quad % 2}"
                        sch.op("gpsimd", lambda g, qd=quad, b=hti: g.dma_start(
                            b[:, :].rearrange("p (k n) -> p k n", n=256),
                            bass.AP(hsp_d, qd * 256,
                                    [[TOK, 128], [128 * TOK, 64], [1, 256]])),
                            reads=["hsp_d"], writes=[htik], dma=True,
                            tag=f"{sname}lh{qr}_{quad}")
                        for tq in range(2):
                            t = quad * 2 + tq
                            xt, xtk = xt2[t % 2], f"{sname}bxt{t % 2}"
                            sch.op("sync", lambda sy, tt=t, qq=qr, b=xt:
                                   sy.dma_start(
                                       b[:, :],
                                       x_src_ap[tt * 128:(tt + 1) * 128,
                                                qq * 512:(qq + 1) * 512]),
                                   reads=[f"{x_src_key}:{t}"], writes=[xtk],
                                   dma=True, tag=f"{sname}lx{qr}_{t}")
                            xo, xok = xo2[t % 2], f"{sname}bxo{t % 2}"
                            psd, psd_k = next_pmm()
                            sch.op("tensor", lambda te, b=hti, w=wdn, p=psd,
                                   tb=tq * 128: [
                                te.matmul(p[:, :],
                                          b[:, k * 256 + tb:k * 256 + tb + 128],
                                          w[:, k * 512:(k + 1) * 512],
                                          start=(k == 0), stop=(k == 63))
                                for k in range(64)],
                                reads=[htik, wdnk], writes=[psd_k],
                                tag=f"{sname}d{qr}_{t}")
                            sch.op("vector", lambda v, b=xt, o=xo, p=psd:
                                   v.scalar_tensor_tensor(
                                       o[:, :], p[:, :], 0.5, b[:, :],
                                       ALU.mult, ALU.add),
                                   reads=[psd_k, xtk], writes=[xok],
                                   tag=f"{sname}e{qr}_{t}")
                            sch.op("sync", lambda sy, tt=t, qq=qr, o=xo:
                                   sy.dma_start(
                                       x_out_ap[tt * 128:(tt + 1) * 128,
                                                qq * 512:(qq + 1) * 512],
                                       o[:, :]),
                                   reads=[xok], writes=[f"{x_out_key}:{t}"],
                                   dma=True, tag=f"{sname}sx{qr}_{t}")
            sch.fence()

        def stage_attn(x_src_ap, x_src_key, x_out_ap, x_out_key):
            sname = "C"
            with contextlib.ExitStack() as sc:
                sbl = lambda n, sh, dt: sc.enter_context(
                    nc.sbuf_tensor(sname + n, sh, dt))
                cqT = sbl("cqT", [128, 4 * NT * 128], BF16)
                ckvT = sbl("ckvT", [128, 4 * NT * 128], BF16)
                with contextlib.ExitStack() as s_ln:
                    ln_t = s_ln.enter_context(
                        nc.sbuf_tensor("Cln_t", [128, NT * 16 * 128], BF16))
                    ln_key = "Cln_t"
                    with contextlib.ExitStack() as s_p0:
                        ln_phase(x_src_ap, x_src_key, ln_t, ln_key, s_p0, "Cp0")
                    sch.fence()

                    with contextlib.ExitStack() as sp:
                        sbp = lambda n, sh, dt: sp.enter_context(
                            nc.sbuf_tensor(sname + "1" + n, sh, dt))
                        wc_sb = sbp("wc", [128, 16 * 512], BF16)
                        wckv_sb = sbp("wckv", [128, 16 * 512], BF16)
                        bc_r = sbp("bc", [1, DC], BF16)
                        bckv_r = sbp("bckv", [1, DC], BF16)
                        cq_bf2 = [sbp(f"cq_bf{i}", [128, 512], BF16) for i in range(2)]
                        sch.op("gpsimd", lambda g: [
                            g.dma_start(wc_sb[:, :].rearrange(
                                "p (k n) -> p k n", n=512),
                                ap_in["Wc"].rearrange("(k p) n -> p k n", p=128)),
                            g.dma_start(wckv_sb[:, :].rearrange(
                                "p (k n) -> p k n", n=512),
                                ap_in["Wckv"].rearrange("(k p) n -> p k n", p=128)),
                            g.dma_start(bc_r[:, :], ap_in["bc"].unsqueeze(0)),
                            g.dma_start(bckv_r[:, :], ap_in["bckv"].unsqueeze(0))],
                            writes=["wc", "wckv", "bc", "bckv"], dma=True,
                            n_dma=4, tag="Cwc")
                        pend1 = []
                        for t in range(NT):
                            for ii, (wsb, wk_, br, brk, dstT, dstTk) in enumerate([
                                    (wc_sb, "wc", bc_r, "bc", cqT, "cqT"),
                                    (wckv_sb, "wckv", bckv_r, "bckv", ckvT, "ckvT")]):
                                cq_bf, cqbk = cq_bf2[ii], f"cq_bf{ii}"
                                ps, ps_k = next_pmm()
                                sch.op("tensor", lambda te, tt=t, w=wsb, b=br,
                                       p=ps: [
                                    te.matmul(p[:, :],
                                              ln_t[:, (k * NT + tt) * 128:
                                                   (k * NT + tt + 1) * 128],
                                              w[:, k * 512:(k + 1) * 512],
                                              start=(k == 0), stop=False)
                                    for k in range(16)] + [
                                    te.matmul(p[:, :], ones_row[:, :], b[:, :],
                                              start=False, stop=True)],
                                    reads=[f"{ln_key}:{t}", wk_, brk, "ones"],
                                    writes=[ps_k], tag=f"Ccq{t}")
                                sch.op("scalar", lambda s, p=ps, b=cq_bf: s.copy(
                                    b[:, :], p[:, :]),
                                    reads=[ps_k], writes=[cqbk],
                                    tag=f"Ccqe{t}")
                                for fn in pend1:
                                    fn()
                                pend1 = []

                                def mk_p1tr(tt, ii_, cq_bf=cq_bf, cqbk=cqbk,
                                            dstT=dstT, dstTk=dstTk):
                                    def go():
                                        transpose4_to(
                                            [cq_bf[:, k * 128:(k + 1) * 128]
                                             for k in range(4)], cqbk,
                                            bass.AP(dstT, tt * 128,
                                                    [[4 * NT * 128, 128],
                                                     [NT * 128, 4], [1, 128]]),
                                            f"{dstTk}:{tt}",
                                            tg=f"Ccqt{tt}_{ii_}")
                                    return go
                                pend1.append(mk_p1tr(t, ii))
                        for fn in pend1:
                            fn()
                    sch.fence()

                    with contextlib.ExitStack() as sp:
                        sbp = lambda n, sh, dt: sp.enter_context(
                            nc.sbuf_tensor(sname + "kr" + n, sh, dt))
                        wkr_sb = sbp("wkr", [128, 16 * 1024], BF16)
                        bkr_r = sbp("bkr", [1, 1024], BF16)
                        qrf2 = [sbp(f"qrf{i}", [128, 1024], F32) for i in range(2)]
                        qrb2 = [sbp(f"qrb{i}", [128, 1024], BF16) for i in range(2)]
                        cosb = [sbp(f"cos{i}", [128, 512], F32) for i in range(2)]
                        sinb = [sbp(f"sin{i}", [128, 512], F32) for i in range(2)]
                        ropet = sbp("rt", [128, 512], F32)
                        ropeu = sbp("ru", [128, 512], F32)
                        lo2 = [sbp(f"lo{i}", [64, 2048], BF16) for i in range(2)]
                        sch.op("gpsimd", lambda g: [
                            g.dma_start(wkr_sb[:, :].rearrange(
                                "p (k n) -> p k n", n=1024),
                                ap_in["Wkr"].rearrange("(k p) n -> p k n", p=128)),
                            g.dma_start(bkr_r[:, :], ap_in["bkr"].unsqueeze(0))],
                            writes=["wkr", "krb"], dma=True, n_dma=2, tag="Ckrw")
                        pendk = []
                        for t in range(NT):
                            par = t % 2
                            qrf, qrb = qrf2[par], qrb2[par]
                            qrfk, qrbk = f"krqrf{par}", f"krqrb{par}"
                            lo, lok = lo2[par], f"krlo{par}"
                            cs, csk = cosb[par], f"krcos{par}"
                            sn, snk = sinb[par], f"krsin{par}"
                            sch.op("sync", lambda sy, tt=t, a=cs, b=sn: [
                                sy.dma_start(a[:, :], ap_in["ropecos"][
                                    tt * 128:(tt + 1) * 128, :]),
                                sy.dma_start(b[:, :], ap_in["ropesin"][
                                    tt * 128:(tt + 1) * 128, :])],
                                writes=[csk, snk], dma=True, n_dma=2,
                                tag=f"Ckrc{t}")
                            for nck in range(2):
                                ps, ps_k = next_pmm()
                                sch.op("tensor", lambda te, tt=t, cc=nck, p=ps: [
                                    te.matmul(p[:, :],
                                              ln_t[:, (k * NT + tt) * 128:
                                                   (k * NT + tt + 1) * 128],
                                              wkr_sb[:, k * 1024 + cc * 512:
                                                     k * 1024 + cc * 512 + 512],
                                              start=(k == 0), stop=False)
                                    for k in range(16)] + [
                                    te.matmul(p[:, :], ones_row[:, :],
                                              bkr_r[:, cc * 512:(cc + 1) * 512],
                                              start=False, stop=True)],
                                    reads=[f"{ln_key}:{t}", "wkr", "krb", "ones"],
                                    writes=[ps_k], tag=f"Ckr{t}_{nck}")
                                sch.op("scalar", lambda s, cc=nck, p=ps, b=qrf:
                                       s.copy(b[:, cc * 512:(cc + 1) * 512],
                                              p[:, :]),
                                       reads=[ps_k], writes=[qrfk],
                                       tag=f"Ckre{t}_{nck}")
                            for fn in pendk:
                                fn()
                            pendk = []
                            q3 = qrf[:, :].rearrange("p (h r) -> p h r", r=64)
                            o3 = qrb[:, :].rearrange("p (h r) -> p h r", r=64)
                            c3 = cs[:, :].rearrange("p (h r) -> p h r", r=32)
                            s3 = sn[:, :].rearrange("p (h r) -> p h r", r=32)
                            t3 = ropet[:, :].rearrange("p (h r) -> p h r", r=32)
                            u3 = ropeu[:, :].rearrange("p (h r) -> p h r", r=32)
                            sch.op("vector", lambda v, t3=t3, u3=u3, q3=q3,
                                   c3=c3, s3=s3, o3=o3: [
                                v.tensor_tensor(t3, q3[:, :, 0:32], c3, ALU.mult),
                                v.tensor_tensor(u3, q3[:, :, 32:64], s3, ALU.mult),
                                v.tensor_tensor(o3[:, :, 0:32], t3, u3,
                                                ALU.subtract),
                                v.tensor_tensor(t3, q3[:, :, 32:64], c3, ALU.mult),
                                v.tensor_tensor(u3, q3[:, :, 0:32], s3, ALU.mult),
                                v.tensor_tensor(o3[:, :, 32:64], t3, u3, ALU.add)],
                                reads=[qrfk, csk, snk], writes=[qrbk],
                                tag=f"Ckro{t}")

                            def mk_krtr(tt, qrb=qrb, qrbk=qrbk, lo=lo, lok=lok):
                                def go():
                                    for h0 in range(0, H, 4):
                                        transpose4_to(
                                            [qrb[:, (h0 + h) * 64:
                                                 (h0 + h + 1) * 64]
                                             for h in range(4)], qrbk,
                                            bass.AP(lo, h0,
                                                    [[2048, 64], [1, 4],
                                                     [16, 128]]),
                                            lok, nparts=64,
                                            tg=f"Ckrt{tt}_{h0}")
                                    sch.op("sync", lambda sy, b=lo: sy.dma_start(
                                        bass.AP(qk_d, (tt * 384 + 320) * TOK,
                                                [[TOK, 64], [1, 2048]]),
                                        b[0:64, :]),
                                        reads=[lok], writes=[f"qk_d:{tt}"],
                                        dma=True, tag=f"Ckrs{tt}")
                                return go
                            pendk.append(mk_krtr(t))
                        for fn in pendk:
                            fn()
                    sch.fence()

                with contextlib.ExitStack() as sp:
                    sbp = lambda n, sh, dt: sp.enter_context(
                        nc.sbuf_tensor(sname + "m" + n, sh, dt))
                    wq_sb = sbp("wq", [128, 4 * 2048], BF16)
                    wqr_sb = sbp("wqr", [128, 4 * 1024], BF16)
                    wk_sb = sbp("wk", [128, 4 * 2048], BF16)
                    wv_sb = sbp("wv", [128, 4 * 2048], BF16)
                    bq_r = sbp("bq", [1, D], BF16)
                    bqr_r = sbp("bqr", [1, 1024], BF16)
                    bk_r = sbp("bk", [1, D], BF16)
                    bv_r = sbp("bv", [1, D], BF16)
                    qf2 = [sbp(f"qf{i}", [128, 2048], BF16) for i in range(2)]
                    kf2 = [sbp(f"kf{i}", [128, 2048], BF16) for i in range(2)]
                    vnb2 = [sbp(f"vnb{i}", [128, 2048], BF16) for i in range(2)]
                    qrf2 = [sbp(f"qrf{i}", [128, 1024], F32) for i in range(2)]
                    qrb2 = [sbp(f"qrb{i}", [128, 1024], BF16) for i in range(2)]
                    cosb = [sbp(f"cos{i}", [128, 512], F32) for i in range(2)]
                    sinb = [sbp(f"sin{i}", [128, 512], F32) for i in range(2)]
                    ropet = sbp("rt", [128, 512], F32)
                    ropeu = sbp("ru", [128, 512], F32)
                    hi2 = [sbp(f"hi{i}", [128, 2048], BF16) for i in range(2)]
                    lo2 = [sbp(f"lo{i}", [64, 2048], BF16) for i in range(2)]
                    sch.op("gpsimd", lambda g: [
                        g.dma_start(wq_sb[:, :].rearrange(
                            "p (k n) -> p k n", n=2048),
                            ap_in["Wcq"].rearrange("(k p) n -> p k n", p=128)),
                        g.dma_start(wqr_sb[:, :].rearrange(
                            "p (k n) -> p k n", n=1024),
                            ap_in["Wqr"].rearrange("(k p) n -> p k n", p=128)),
                        g.dma_start(wk_sb[:, :].rearrange(
                            "p (k n) -> p k n", n=2048),
                            ap_in["Wck"].rearrange("(k p) n -> p k n", p=128)),
                        g.dma_start(wv_sb[:, :].rearrange(
                            "p (k n) -> p k n", n=2048),
                            ap_in["Wv"].rearrange("(k p) n -> p k n", p=128)),
                        g.dma_start(bq_r[:, :], ap_in["bcq"].unsqueeze(0)),
                        g.dma_start(bqr_r[:, :], ap_in["bqr"].unsqueeze(0)),
                        g.dma_start(bk_r[:, :], ap_in["bck"].unsqueeze(0)),
                        g.dma_start(bv_r[:, :], ap_in["bv"].unsqueeze(0))],
                        writes=["wq", "wqr", "wk", "wv", "bq", "bqr", "bk",
                                "bv"], dma=True, n_dma=8, tag="Cmw")
                    pendm = []
                    for t in range(NT):
                        par = t % 2
                        qf, kf, vnb = qf2[par], kf2[par], vnb2[par]
                        qfk, kfk, vnbk = f"mqf{par}", f"mkf{par}", f"mvnb{par}"
                        qrf, qrb = qrf2[par], qrb2[par]
                        qrfk, qrbk = f"mqrf{par}", f"mqrb{par}"
                        hi, lo = hi2[par], lo2[par]
                        hik, lok = f"mhi{par}", f"mlo{par}"
                        cs, csk = cosb[par], f"mcos{par}"
                        sn, snk = sinb[par], f"msin{par}"
                        sch.op("sync", lambda sy, tt=t, a=cs, b=sn: [
                            sy.dma_start(a[:, :], ap_in["ropecos"][
                                tt * 128:(tt + 1) * 128, :]),
                            sy.dma_start(b[:, :], ap_in["ropesin"][
                                tt * 128:(tt + 1) * 128, :])],
                            writes=[csk, snk], dma=True, n_dma=2,
                            tag=f"Cmrc{t}")

                        def proj4(tt, cTk_, w_sb, wk_, b_r, brk, out, outk, tg):
                            cT_ = cqT if cTk_ == "cqT" else ckvT
                            for nck in range(4):
                                ps, ps_k = next_pmm()
                                sch.op("tensor", lambda te, cc=nck, p=ps,
                                       w=w_sb, b=b_r, c=cT_: [
                                    te.matmul(p[:, :],
                                              c[:, (k * NT + tt) * 128:
                                                (k * NT + tt + 1) * 128],
                                              w[:, k * 2048 + cc * 512:
                                                k * 2048 + cc * 512 + 512],
                                              start=(k == 0), stop=False)
                                    for k in range(4)] + [
                                    te.matmul(p[:, :], ones_row[:, :],
                                              b[:, cc * 512:(cc + 1) * 512],
                                              start=False, stop=True)],
                                    reads=[f"{cTk_}:{tt}", wk_, brk, "ones"],
                                    writes=[ps_k], tag=f"{tg}{tt}_{nck}")
                                sch.op("scalar", lambda s, cc=nck, p=ps, b=out:
                                       s.copy(b[:, cc * 512:(cc + 1) * 512],
                                              p[:, :]),
                                       reads=[ps_k], writes=[outk],
                                       tag=f"{tg}e{tt}_{nck}")

                        proj4(t, "cqT", wq_sb, "wq", bq_r, "bq", qf, qfk, "Cmq")
                        for nck in range(2):
                            ps, ps_k = next_pmm()
                            sch.op("tensor", lambda te, tt=t, cc=nck, p=ps: [
                                te.matmul(p[:, :],
                                          cqT[:, (k * NT + tt) * 128:
                                              (k * NT + tt + 1) * 128],
                                          wqr_sb[:, k * 1024 + cc * 512:
                                                 k * 1024 + cc * 512 + 512],
                                          start=(k == 0), stop=False)
                                for k in range(4)] + [
                                te.matmul(p[:, :], ones_row[:, :],
                                          bqr_r[:, cc * 512:(cc + 1) * 512],
                                          start=False, stop=True)],
                                reads=[f"cqT:{t}", "wqr", "bqr", "ones"],
                                writes=[ps_k], tag=f"Cmqr{t}_{nck}")
                            sch.op("scalar", lambda s, cc=nck, p=ps, b=qrf:
                                   s.copy(b[:, cc * 512:(cc + 1) * 512],
                                          p[:, :]),
                                   reads=[ps_k], writes=[qrfk],
                                   tag=f"Cmqre{t}_{nck}")
                        proj4(t, "ckvT", wk_sb, "wk", bk_r, "bk", kf, kfk, "Cmk")
                        proj4(t, "ckvT", wv_sb, "wv", bv_r, "bv", vnb, vnbk, "Cmv")
                        for fn in pendm:
                            fn()
                        pendm = []
                        q3 = qrf[:, :].rearrange("p (h r) -> p h r", r=64)
                        o3 = qrb[:, :].rearrange("p (h r) -> p h r", r=64)
                        c3 = cs[:, :].rearrange("p (h r) -> p h r", r=32)
                        s3 = sn[:, :].rearrange("p (h r) -> p h r", r=32)
                        t3 = ropet[:, :].rearrange("p (h r) -> p h r", r=32)
                        u3 = ropeu[:, :].rearrange("p (h r) -> p h r", r=32)
                        sch.op("vector", lambda v, t3=t3, u3=u3, q3=q3,
                               c3=c3, s3=s3, o3=o3: [
                            v.tensor_tensor(t3, q3[:, :, 0:32], c3, ALU.mult),
                            v.tensor_tensor(u3, q3[:, :, 32:64], s3, ALU.mult),
                            v.tensor_tensor(o3[:, :, 0:32], t3, u3, ALU.subtract),
                            v.tensor_tensor(t3, q3[:, :, 32:64], c3, ALU.mult),
                            v.tensor_tensor(u3, q3[:, :, 0:32], s3, ALU.mult),
                            v.tensor_tensor(o3[:, :, 32:64], t3, u3, ALU.add)],
                            reads=[qrfk, csk, snk], writes=[qrbk],
                            tag=f"Cmro{t}")
                        sch.op("sync", lambda sy, tt=t, b=vnb: sy.dma_start(
                            vscr_d.ap()[tt * 128:(tt + 1) * 128, :], b[:, :]),
                            reads=[vnbk], writes=[f"vscr:{t}"], dma=True,
                            tag=f"Cmvs{t}")

                        def mk_mtr(tt, qf=qf, qfk=qfk, kf=kf, kfk=kfk,
                                   qrb=qrb, qrbk=qrbk, hi=hi, lo=lo,
                                   hik=hik, lok=lok):
                            def go():
                                for h0 in range(0, H, 4):
                                    transpose4_to(
                                        [qf[:, (h0 + h) * 128:
                                            (h0 + h + 1) * 128]
                                         for h in range(4)], qfk,
                                        bass.AP(hi, h0, [[2048, 128], [1, 4],
                                                         [16, 128]]),
                                        hik, tg=f"Cmth{tt}_{h0}")
                                    transpose4_to(
                                        [qrb[:, (h0 + h) * 64:
                                             (h0 + h + 1) * 64]
                                         for h in range(4)], qrbk,
                                        bass.AP(lo, h0, [[2048, 64], [1, 4],
                                                         [16, 128]]),
                                        lok, nparts=64, tg=f"Cmtl{tt}_{h0}")
                                sch.op("sync", lambda sy, a=hi, b=lo: [
                                    sy.dma_start(
                                        bass.AP(qk_d, (tt * 384) * TOK,
                                                [[TOK, 128], [1, 2048]]),
                                        a[:, :]),
                                    sy.dma_start(
                                        bass.AP(qk_d, (tt * 384 + 128) * TOK,
                                                [[TOK, 64], [1, 2048]]),
                                        b[0:64, :])],
                                    reads=[hik, lok], writes=[f"qk_d:{tt}"],
                                    dma=True, n_dma=2, tag=f"Cmqs{tt}")
                                for h0 in range(0, H, 4):
                                    transpose4_to(
                                        [kf[:, (h0 + h) * 128:
                                            (h0 + h + 1) * 128]
                                         for h in range(4)], kfk,
                                        bass.AP(hi, h0, [[2048, 128], [1, 4],
                                                         [16, 128]]),
                                        hik, tg=f"Cmtk{tt}_{h0}")
                                sch.op("sync", lambda sy, a=hi: sy.dma_start(
                                    bass.AP(qk_d, (tt * 384 + 192) * TOK,
                                            [[TOK, 128], [1, 2048]]),
                                    a[:, :]),
                                    reads=[hik], writes=[f"qk_d:{tt}"],
                                    dma=True, tag=f"Cmks{tt}")
                            return go
                        pendm.append(mk_mtr(t))
                    for fn in pendm:
                        fn()
                sch.fence()
            sch.fence()

            with contextlib.ExitStack() as sp:
                sbl = lambda n, sh, dt: sp.enter_context(
                    nc.sbuf_tensor(sname + "g" + n, sh, dt))
                outT = sbl("outT", [128, NT * 2048], BF16)
                qhi2 = [sbl(f"qhi{i}", [128, 2048], BF16) for i in range(2)]
                qlo2 = [sbl(f"qlo{i}", [64, 2048], BF16) for i in range(2)]
                khi2 = [sbl(f"khi{i}", [128, 2048], BF16) for i in range(2)]
                klo2 = [sbl(f"klo{i}", [64, 2048], BF16) for i in range(2)]
                vg2 = [sbl(f"vg{i}", [128, 2048], BF16) for i in range(2)]
                g4_2 = [sbl(f"g4_{i}", [128, 512], F32) for i in range(2)]
                m4_2 = [sbl(f"m4_{i}", [128, 4], F32) for i in range(2)]
                bm4_2 = [sbl(f"bm4_{i}", [128, 4], F32) for i in range(2)]
                ein_2 = [sbl(f"ein_{i}", [128, 512], F32) for i in range(2)]
                ebf_2 = [sbl(f"ebf_{i}", [128, 512], BF16) for i in range(2)]
                rs4_2 = [sbl(f"rs4_{i}", [128, 4], F32) for i in range(2)]
                ri4_2 = [sbl(f"ri4_{i}", [128, 4], F32) for i in range(2)]
                at4_2 = [sbl(f"at4_{i}", [128, 512], BF16) for i in range(2)]
                av4_2 = [sbl(f"av4_{i}", [128, 512], BF16) for i in range(2)]
                pend7 = []
                for t in range(NT):
                    qhi, qhik = qhi2[t % 2], f"gqhi{t % 2}"
                    qlo, qlok = qlo2[t % 2], f"gqlo{t % 2}"
                    khi, khik = khi2[t % 2], f"gkhi{t % 2}"
                    klo, klok = klo2[t % 2], f"gklo{t % 2}"
                    vg, vgk = vg2[t % 2], f"gvg{t % 2}"
                    sch.op("sync", lambda sy, tt=t, a=qhi, b=qlo, c=khi, d=klo: [
                        sy.dma_start(a[:, :], bass.AP(
                            qk_d, (tt * 384) * TOK, [[TOK, 128], [1, 2048]])),
                        sy.dma_start(b[0:64, :], bass.AP(
                            qk_d, (tt * 384 + 128) * TOK, [[TOK, 64], [1, 2048]])),
                        sy.dma_start(c[:, :], bass.AP(
                            qk_d, (tt * 384 + 192) * TOK, [[TOK, 128], [1, 2048]])),
                        sy.dma_start(d[0:64, :], bass.AP(
                            qk_d, (tt * 384 + 320) * TOK, [[TOK, 64], [1, 2048]]))],
                        reads=[f"qk_d:{t}"], writes=[qhik, qlok, khik, klok],
                        dma=True, n_dma=4, tag=f"Cgl{t}")
                    sch.op("gpsimd", lambda g, tt=t, b=vg: g.dma_start(
                        b[:, :], bass.AP(vscr_d, tt * 128 * D,
                                         [[D, 8], [128, 16], [8 * D, 16],
                                          [1, 128]])),
                        reads=[f"vscr:{t}"], writes=[vgk], dma=True,
                        tag=f"Cgv{t}")
                    for blk in range(4):
                        gp = blk % 2
                        g4, m4, bm4 = g4_2[gp], m4_2[gp], bm4_2[gp]
                        ein, ebf = ein_2[gp], ebf_2[gp]
                        rs4, ri4 = rs4_2[gp], ri4_2[gp]
                        at4, av4 = at4_2[gp], av4_2[gp]
                        g4k, m4k, bm4k = f"g4_{gp}", f"m4_{gp}", f"bm4_{gp}"
                        eink, ebfk = f"ein_{gp}", f"ebf_{gp}"
                        rs4k, ri4k = f"rs4_{gp}", f"ri4_{gp}"
                        at4k, av4k = f"at4_{gp}", f"av4_{gp}"
                        psg, psg_k = next_pmm()
                        sch.op("tensor", lambda te, b_=blk, a=qhi, b=qlo,
                               c=khi, d=klo, p=psg: [
                            mm for g_ in range(4) for mm in (
                                te.matmul(p[:, g_ * 128:(g_ + 1) * 128],
                                          a[:, (b_ * 4 + g_) * 128:
                                            (b_ * 4 + g_ + 1) * 128],
                                          c[:, (b_ * 4 + g_) * 128:
                                            (b_ * 4 + g_ + 1) * 128],
                                          start=True, stop=False),
                                te.matmul(p[:, g_ * 128:(g_ + 1) * 128],
                                          b[0:64, (b_ * 4 + g_) * 128:
                                            (b_ * 4 + g_ + 1) * 128],
                                          d[0:64, (b_ * 4 + g_) * 128:
                                            (b_ * 4 + g_ + 1) * 128],
                                          start=False, stop=True))],
                            reads=[qhik, qlok, khik, klok], writes=[psg_k],
                            tag=f"Cgr{t}_{blk}")
                        for fn in pend7:
                            fn()
                        pend7 = []
                        sch.op("vector", lambda v, p=psg, a=g4, b=m4: [
                            v.tensor_tensor(
                                a[:, :].rearrange("p (g n) -> p g n", n=128),
                                p[:, :].rearrange("p (g n) -> p g n", n=128),
                                maskneg[:, :].unsqueeze(1).broadcast_to(
                                    [128, 4, 128]), ALU.add),
                            v.reduce_max(
                                b[:, :].rearrange("p (g n) -> p g n", n=1),
                                a[:, :].rearrange("p (g n) -> p g n", n=128),
                                axis=AX.X)],
                            reads=[psg_k, "maskneg"], writes=[g4k, m4k],
                            tag=f"Cgm{t}_{blk}")
                        sch.op("vector", lambda v, a=m4, c=g4, o=ein:
                               v.tensor_tensor(
                                   o[:, :].rearrange("p (g n) -> p g n", n=128),
                                   c[:, :].rearrange("p (g n) -> p g n", n=128),
                                   a[:, :].unsqueeze(2).broadcast_to(
                                       [128, 4, 128]),
                                   ALU.subtract),
                            reads=[m4k, g4k], writes=[eink],
                            barrier=True, tag=f"Cgs{t}_{blk}")
                        sch.op("scalar", lambda s, a=ein, b=ebf: s.activation(
                            b[:, :], a[:, :], AF.Exp, scale=SCALER),
                            reads=[eink], writes=[ebfk], tag=f"Cge{t}_{blk}")
                        sch.op("vector", lambda v, a=ebf, b=rs4: v.reduce_sum(
                            b[:, :].rearrange("p (g n) -> p g n", n=1),
                            a[:, :].rearrange("p (g n) -> p g n", n=128),
                            axis=AX.X),
                            reads=[ebfk], writes=[rs4k], tag=f"Cgr2{t}_{blk}")
                        sch.op("vector", lambda v, a=rs4, b=ri4: v.reciprocal(
                            b[:, :], a[:, :]),
                            reads=[rs4k], writes=[ri4k], barrier=True,
                            tag=f"Cgc{t}_{blk}")

                        def mk_av(tt, b_, vg=vg, vgk=vgk, ebf=ebf, ebfk=ebfk,
                                  at4=at4, at4k=at4k, av4=av4, av4k=av4k,
                                  ri4=ri4, ri4k=ri4k):
                            def go():
                                transpose4_to(
                                    [ebf[:, g_ * 128:(g_ + 1) * 128]
                                     for g_ in range(4)], ebfk,
                                    at4[:, :].rearrange("p (g n) -> p g n",
                                                        n=128),
                                    at4k, tg=f"Cgt{tt}_{b_}")
                                psa, psa_k = next_pmm()
                                sch.op("tensor", lambda te, p=psa: [
                                    te.matmul(p[:, g_ * 128:(g_ + 1) * 128],
                                              at4[:, g_ * 128:(g_ + 1) * 128],
                                              vg[:, (b_ * 4 + g_) * 128:
                                                 (b_ * 4 + g_ + 1) * 128],
                                              start=True, stop=True)
                                    for g_ in range(4)],
                                    reads=[at4k, vgk], writes=[psa_k],
                                    tag=f"Cga{tt}_{b_}")
                                sch.op("vector", lambda v, p=psa, o=av4:
                                       v.tensor_tensor(
                                           o[:, :].rearrange(
                                               "p (g n) -> p g n", n=128),
                                           p[:, :].rearrange(
                                               "p (g n) -> p g n", n=128),
                                           ri4[:, :].unsqueeze(2).broadcast_to(
                                               [128, 4, 128]), ALU.mult),
                                    reads=[psa_k, ri4k], writes=[av4k],
                                    tag=f"Cgae{tt}_{b_}")
                                transpose4_to(
                                    [av4[:, g_ * 128:(g_ + 1) * 128]
                                     for g_ in range(4)], av4k,
                                    outT[:, tt * 2048 + b_ * 512:
                                         tt * 2048 + (b_ + 1) * 512].rearrange(
                                        "p (g n) -> p g n", n=128),
                                    f"outT:{tt}", tg=f"Cgo{tt}_{b_}")
                            return go
                        pend7.append(mk_av(t, blk))
                for fn in pend7:
                    fn()
                pend7 = []

                for fn in pend7:
                    fn()
                pend7 = []
                wo2 = [sbl(f"wo{i}", [128, 16 * 512], BF16) for i in range(2)]
                bo_r = sbl("bo", [1, D], BF16)
                xr2 = [sbl(f"xr{i}", [128, 512], F32) for i in range(2)]
                xw2 = [sbl(f"xw{i}", [128, 512], F32) for i in range(2)]
                sch.op("gpsimd", lambda g: g.dma_start(
                    bo_r[:, :], ap_in["bo"].unsqueeze(0)),
                    writes=["bo"], dma=True, tag="Cbo")
                for ncw in range(4):
                    wo, wok = wo2[ncw % 2], f"gwo{ncw % 2}"
                    sch.op("gpsimd", lambda g, cc=ncw, b=wo: g.dma_start(
                        b[:, :].rearrange("p (k n) -> p k n", n=512),
                        ap_in["Wo"][:, cc * 512:(cc + 1) * 512].rearrange(
                            "(k p) n -> p k n", p=128)),
                        writes=[wok], dma=True, tag=f"Cgwo{ncw}")
                    for t in range(NT):
                        xr, xrk = xr2[t % 2], f"gxr{t % 2}"
                        sch.op("sync", lambda sy, tt=t, cc=ncw, b=xr:
                               sy.dma_start(
                                   b[:, :], x_src_ap[tt * 128:(tt + 1) * 128,
                                                     cc * 512:(cc + 1) * 512]),
                               reads=[f"{x_src_key}:{t}"], writes=[xrk],
                               dma=True, tag=f"Cglx{ncw}_{t}")
                        pso, pso_k = next_pmm()
                        sch.op("tensor", lambda te, tt=t, cc=ncw, w=wo, p=pso: [
                            te.matmul(p[:, :],
                                      bass.AP(outT, tt * 2048 + i,
                                              [[NT * 2048, 128], [16, 128]]),
                                      w[:, i * 512:(i + 1) * 512],
                                      start=(i == 0), stop=False)
                            for i in range(16)] + [
                            te.matmul(p[:, :], ones_row[:, :],
                                      bo_r[:, cc * 512:(cc + 1) * 512],
                                      start=False, stop=True)],
                            reads=[f"outT:{t}", wok, "bo", "ones"],
                            writes=[pso_k], tag=f"Cgw{ncw}_{t}")
                        xw, xwk = xw2[t % 2], f"gxw{t % 2}"
                        sch.op("vector", lambda v, b=xr, o=xw, p=pso:
                               v.tensor_tensor(o[:, :], p[:, :], b[:, :],
                                               ALU.add),
                               reads=[pso_k, xrk], writes=[xwk],
                               tag=f"Cgad{ncw}_{t}")
                        sch.op("sync", lambda sy, tt=t, cc=ncw, o=xw:
                               sy.dma_start(
                                   x_out_ap[tt * 128:(tt + 1) * 128,
                                            cc * 512:(cc + 1) * 512], o[:, :]),
                               reads=[xwk], writes=[f"{x_out_key}:{t}"],
                               dma=True, tag=f"Cgsx{ncw}_{t}")
            sch.fence()

        xm1 = xm1_d.ap()
        xm2 = xm2_d.ap()
        stage_ffn(ap_in["x"], "x_in", xm1, "xm1", "A")
        if "xmid1" in dbg_h:
            sch.op("sync", lambda sy: [
                sy.dma_start(dbg_h["xmid1"][t * 128:(t + 1) * 128, :],
                             xm1[t * 128:(t + 1) * 128, :]) for t in range(NT)],
                reads=[f"xm1:{t}" for t in range(NT)], dma=True, n_dma=NT,
                tag="dbg1")
        stage_attn(xm1, "xm1", xm2, "xm2")
        if "qk" in dbg_h:
            sch.op("sync", lambda sy: sy.dma_start(
                dbg_h["qk"][:, :].bitcast(BF16)[:, 0:2048],
                bass.AP(qk_d, 0, [[2048, 2048], [1, 2048]])),
                reads=["qk_d"], dma=True, tag="dbgqk")
        if "vs" in dbg_h:
            sch.op("sync", lambda sy: sy.dma_start(
                dbg_h["vs"][:, :].bitcast(BF16)[:, 0:2048],
                bass.AP(vscr_d, 0, [[2048, 2048], [1, 2048]])),
                reads=["vscr"], dma=True, tag="dbgvs")
        if "xmid2" in dbg_h:
            sch.op("sync", lambda sy: [
                sy.dma_start(dbg_h["xmid2"][t * 128:(t + 1) * 128, :],
                             xm2[t * 128:(t + 1) * 128, :]) for t in range(NT)],
                reads=[f"xm2:{t}" for t in range(NT)], dma=True, n_dma=NT,
                tag="dbg2")
        stage_ffn(xm2, "xm2", out_h, "x_out", "D")

        sch.emit(block)
    return nc


_NC_CACHE = {}


def _get_nc(debug=()):
    key = tuple(debug)
    if key not in _NC_CACHE:
        _NC_CACHE[key] = build_nc(debug)
    return _NC_CACHE[key]


def _host_constants():
    ident = np.eye(128, dtype=np.float32)
    maskneg = np.full((128, 128), -1e30, np.float32)
    for s8 in range(8):
        maskneg[s8 * 16:(s8 + 1) * 16, s8 * 16:(s8 + 1) * 16] = 0.0
    inv_freq = (1.0 / 10000.0 ** (np.arange(0, 64, 2, dtype=np.float32) / 64.0)
                ).astype(np.float32)
    t = np.arange(NCORES * TOK, dtype=np.float32)
    fr = t[:, None] * inv_freq[None, :]
    cos = np.tile(np.cos(fr), (1, H)).astype(np.float32)
    sin = np.tile(np.sin(fr), (1, H)).astype(np.float32)
    return ident, maskneg, cos, sin


def kernel(**inputs):
    x = np.ascontiguousarray(
        np.asarray(inputs["L0_attn_h_t"], dtype=np.float32).reshape(
            NCORES * TOK, D))
    ident, maskneg, cos, sin = _host_constants()
    wmap = {
        "Wc": "L0_attn_Wc", "bc": "L0_attn_bc", "Wcq": "L0_attn_Wcq",
        "bcq": "L0_attn_bcq", "Wqr": "L0_attn_Wqr", "bqr": "L0_attn_bqr",
        "Wckv": "L0_attn_Wckv", "bckv": "L0_attn_bckv", "Wck": "L0_attn_Wck",
        "bck": "L0_attn_bck", "Wkr": "L0_attn_Wkr", "bkr": "L0_attn_bkr",
        "Wv": "L0_attn_Wv", "bv": "L0_attn_bv", "Wo": "L0_attn_Wo",
        "bo": "L0_attn_bo", "w_gate": "L0_ffn_w_gate", "w_up": "L0_ffn_w_up",
        "w_down": "L0_ffn_w_down",
    }
    weights = {k: np.ascontiguousarray(np.asarray(inputs[v], dtype=np.float32))
               for k, v in wmap.items()}
    nc = _get_nc()
    in_maps = []
    for c in range(NCORES):
        m = dict(weights)
        m["x"] = x[c * TOK:(c + 1) * TOK]
        m["ident"] = ident
        m["maskneg"] = maskneg
        m["ropecos"] = np.ascontiguousarray(cos[c * TOK:(c + 1) * TOK])
        m["ropesin"] = np.ascontiguousarray(sin[c * TOK:(c + 1) * TOK])
        in_maps.append(m)
    res = run_bass_kernel_spmd(nc, in_maps, core_ids=list(range(NCORES)))
    out = np.concatenate([r["out"] for r in res.results], axis=0)
    return out.reshape(1, NCORES * TOK, D).astype(np.float32)


# version 18
# speedup vs baseline: 1.3582x; 1.0187x over previous
import contextlib
import numpy as np
import concourse.bass as bass
import concourse.mybir as mybir
from concourse.bass_utils import run_bass_kernel_spmd

F32 = mybir.dt.float32
BF16 = mybir.dt.bfloat16
AF = mybir.ActivationFunctionType
ALU = mybir.AluOpType
AX = mybir.AxisListType

NCORES = 8
D = 2048
TOK = 2048
NT = TOK // 128
DFF = 8192
NJ = DFF // 512
H = 16
DH = 128
DQ = 192
DC = 512
SCALER = 1.0 / float(np.sqrt(DQ))
EPS = 1e-5

ENGINES = ("tensor", "vector", "scalar", "gpsimd", "sync")


class _Op:
    __slots__ = ("idx", "engine", "emit_fn", "deps", "inc_amt", "cum_after",
                 "tag", "waits", "barrier")

    def __init__(self, idx, engine, emit_fn, deps, inc_amt, tag, barrier):
        self.idx = idx
        self.engine = engine
        self.emit_fn = emit_fn
        self.deps = deps
        self.inc_amt = inc_amt
        self.cum_after = None
        self.tag = tag
        self.barrier = barrier


class Sched:

    def __init__(self, nc):
        self.nc = nc
        self.ops = []
        self.sems = {}
        self._writers = {}
        self._epoch_read = {}
        self._readers_since = {}
        self._fence_ops = []

    def alloc_sems(self, stack):
        for e in ENGINES:
            self.sems[e] = stack.enter_context(self.nc.semaphore(f"sched_{e}"))

    def op(self, engine, emit_fn, reads=(), writes=(), deps=(), dma=False,
           n_dma=1, tag="", barrier=False):
        dd = set(d.idx for d in deps)
        for f in self._fence_ops:
            dd.add(f.idx)
        for r in reads:
            for w in self._writers.get(r, ()):
                dd.add(w.idx)
        for w in writes:
            for lw in self._writers.get(w, ()):
                dd.add(lw.idx)
            for rd in self._readers_since.get(w, ()):
                dd.add(rd.idx)
        o = _Op(len(self.ops), engine, emit_fn,
                [self.ops[i] for i in sorted(dd)],
                (16 * n_dma) if dma else 1, tag, barrier)
        self.ops.append(o)
        for r in reads:
            self._readers_since.setdefault(r, []).append(o)
            self._epoch_read[r] = True
        for w in writes:
            if self._epoch_read.get(w):
                self._writers[w] = [o]
                self._epoch_read[w] = False
            else:
                self._writers.setdefault(w, []).append(o)
            self._readers_since[w] = []
        return o

    def fence(self):
        last = {}
        for o in self.ops:
            last[o.engine] = o
        self._fence_ops = list(last.values())

    def emit(self, block):
        cum = {e: 0 for e in ENGINES}
        for o in self.ops:
            cum[o.engine] += o.inc_amt
            o.cum_after = cum[o.engine]
        per_engine = {e: [o for o in self.ops if o.engine == e] for e in ENGINES}
        for e in ENGINES:
            have = {x: 0 for x in ENGINES}
            for o in per_engine[e]:
                o.waits = []
                byeng = {}
                for d in o.deps:
                    if (d.engine == e and d.inc_amt == 1 and o.inc_amt == 1
                            and not o.barrier):
                        continue
                    byeng[d.engine] = max(byeng.get(d.engine, 0), d.cum_after)
                for en, vv in sorted(byeng.items()):
                    if vv > have[en]:
                        o.waits.append((en, vv))
                        have[en] = vv

        def mk(e):
            def body(eng):
                for o in per_engine[e]:
                    for en, vv in o.waits:
                        eng.wait_ge(self.sems[en], vv)
                    ret = o.emit_fn(eng)
                    insts = ret if isinstance(ret, (list, tuple)) else [ret]
                    insts = [i for i in insts if i is not None]
                    if not insts:
                        raise RuntimeError(f"op {o.idx} ({o.tag}) emitted nothing")
                    if o.inc_amt == 1:
                        insts[-1].then_inc(self.sems[e], 1)
                    else:
                        assert len(insts) * 16 == o.inc_amt, (o.tag, len(insts))
                        for i in insts:
                            i.then_inc(self.sems[e], 16)
            return body

        for e in ENGINES:
            if per_engine[e]:
                getattr(block, e)(mk(e))
        fin = dict(cum)

        def tail(eng):
            for e in ENGINES:
                if fin[e] and e != "sync":
                    eng.wait_ge(self.sems[e], fin[e])
        block.sync(tail)


def build_nc(debug=()):
    nc = bass.Bass("TRN2", target_bir_lowering=False)
    ap_in = {}
    shapes = {
        "x": [TOK, D], "Wc": [D, DC], "bc": [DC], "Wcq": [DC, D],
        "bcq": [D], "Wqr": [DC, 1024], "bqr": [1024], "Wckv": [D, DC],
        "bckv": [DC], "Wck": [DC, D], "bck": [D], "Wkr": [D, 1024],
        "bkr": [1024], "Wv": [DC, D], "bv": [D], "Wo": [D, D], "bo": [D],
        "w_gate": [D, DFF], "w_up": [D, DFF], "w_down": [DFF, D],
        "ident": [128, 128], "maskneg": [128, 128],
        "ropecos": [TOK, 512], "ropesin": [TOK, 512],
    }
    for n, s in shapes.items():
        ap_in[n] = nc.dram_tensor(n, s, F32, kind="ExternalInput").ap()
    out_h = nc.dram_tensor("out", [TOK, D], F32, kind="ExternalOutput").ap()
    dbg_h = {}
    for dn in debug:
        dbg_h[dn] = nc.dram_tensor("dbg_" + dn, [TOK, D], F32,
                                   kind="ExternalOutput").ap()

    hsp_d = nc.dram_tensor("hsp_scr", [DFF, TOK], BF16, kind="Internal")
    vscr_d = nc.dram_tensor("v_scr", [TOK, D], BF16, kind="Internal")
    qk_d = nc.dram_tensor("qk_scr", [NT, 384, TOK], BF16, kind="Internal")
    xm1_d = nc.dram_tensor("xmid1", [TOK, D], F32, kind="Internal")
    xm2_d = nc.dram_tensor("xmid2", [TOK, D], F32, kind="Internal")

    st = contextlib.ExitStack()
    with st:
        sbt = lambda name, shape, dt: st.enter_context(
            nc.sbuf_tensor(name, shape, dt))
        ident = sbt("sb_ident", [128, 128], BF16)
        maskneg = sbt("sb_maskneg", [128, 128], F32)
        ones_row = sbt("ones_row", [1, 128], BF16)
        stats = sbt("stats", [128, 16], F32)
        flsc = sbt("flsc", [128, 16], F32)

        sch = Sched(nc)
        sch.alloc_sems(st)
        block = st.enter_context(nc.Block())

        pmm = [st.enter_context(nc.psum_tensor(f"pmm{i}", [128, 512], F32))
               for i in range(4)]
        ptr = [st.enter_context(nc.psum_tensor(f"ptr{i}", [128, 1024], BF16))
               for i in range(2)]
        pmm_i = [0]
        ptr_i = [0]

        def next_pmm():
            i = pmm_i[0] % 4
            pmm_i[0] += 1
            return pmm[i], f"pmm{i}"

        def next_ptr():
            i = ptr_i[0] % 2
            ptr_i[0] += 1
            return ptr[i], f"ptr{i}"

        evac_i = [0]

        def evac_to(dst_ap, dst_key, ps, ps_key, nparts=128, tg="e"):
            evac_i[0] += 1
            if evac_i[0] % 2:
                sch.op("scalar", lambda s: s.copy(dst_ap, ps[0:nparts, 0:128]),
                       reads=[ps_key], writes=[dst_key], tag=tg)
            else:
                sch.op("vector", lambda v: v.tensor_copy(dst_ap, ps[0:nparts, 0:128]),
                       reads=[ps_key], writes=[dst_key], tag=tg)

        def transpose_to(src_ap, src_key, dst_ap, dst_key, nparts=128, tg="tr"):
            ps, ps_key = next_ptr()
            sch.op("tensor", lambda te: te.transpose(
                ps[0:nparts, 0:128], src_ap, ident[:, :]),
                reads=[src_key, "ident"], writes=[ps_key], tag=f"{tg}t")
            evac_to(dst_ap, dst_key, ps, ps_key, nparts, tg=f"{tg}e")

        def transpose4_to(srcs, src_key, dst_ap, dst_key, nparts=128, tg="t4"):
            ps, ps_key = next_ptr()
            sch.op("tensor", lambda te, ss=list(srcs): [
                te.transpose(ps[0:nparts, k * 128:(k + 1) * 128], ss[k],
                             ident[:, :]) for k in range(4)],
                reads=[src_key, "ident"], writes=[ps_key], tag=f"{tg}t")
            src3 = ps[0:nparts, 0:512].rearrange("p (k n) -> p k n", n=128)
            evac_i[0] += 1
            if evac_i[0] % 2:
                sch.op("scalar", lambda s: s.copy(dst_ap, src3),
                       reads=[ps_key], writes=[dst_key], tag=f"{tg}e")
            else:
                sch.op("vector", lambda v: v.tensor_copy(dst_ap, src3),
                       reads=[ps_key], writes=[dst_key], tag=f"{tg}e")

        sch.op("gpsimd", lambda g: [
            g.dma_start(ident[:, :], ap_in["ident"][:, :]),
            g.dma_start(maskneg[:, :], ap_in["maskneg"][:, :])],
            writes=["ident", "maskneg"], dma=True, n_dma=2, tag="consts")
        sch.op("vector", lambda v: v.memset(ones_row[:, :], 1.0),
               writes=["ones"], tag="ones")

        def ln_tile(xt, xt_key, xn_bf, xn_key, sq_scr, tg, par=0):
            po = 8 * par
            s1 = stats[:, po + 0:po + 1]
            negmean = stats[:, po + 1:po + 2]
            vsum = stats[:, po + 2:po + 3]
            va = stats[:, po + 3:po + 4]
            rva = stats[:, po + 4:po + 5]
            lntmp = stats[:, po + 5:po + 6]
            rstd = stats[:, po + 6:po + 7]
            sch.op("vector", lambda v: v.reduce_sum(s1, xt[:, :], axis=AX.X),
                   reads=[xt_key], writes=[f"st_s1{par}"], tag=f"{tg}ln_r")
            sch.op("vector", lambda v: [
                v.tensor_scalar_mul(negmean, s1, -1.0 / D),
                v.tensor_copy(flsc[:, 8*par:8*par+1], negmean)],
                reads=[f"st_s1{par}"], writes=[f"st_nm{par}"], barrier=True, tag=f"{tg}ln_nm")
            sch.op("scalar", lambda s: [
                s.activation(sq_scr[:, :], xt[:, :], AF.Square,
                             bias=negmean, scale=1.0, accum_out=vsum),
                s.copy(flsc[:, 8*par+1:8*par+2], vsum)],
                reads=[xt_key, f"st_nm{par}"], writes=[f"st_vs{par}", f"sq{par}"],
                tag=f"{tg}ln_sq")
            sch.op("vector", lambda v: v.tensor_scalar(
                va, vsum, 1.0 / D, EPS, ALU.mult, ALU.add),
                reads=[f"st_vs{par}"], writes=[f"st_va{par}"], tag=f"{tg}ln_va")
            sch.op("vector", lambda v: [
                v.reciprocal(rva, va),
                v.tensor_copy(flsc[:, 8*par+2:8*par+3], rva)],
                reads=[f"st_va{par}"], writes=[f"st_rva{par}"], barrier=True,
                tag=f"{tg}ln_rv")
            sch.op("scalar", lambda s: s.activation(lntmp, rva, AF.Ln),
                   reads=[f"st_rva{par}"], writes=[f"st_lt{par}"], tag=f"{tg}ln_ln")
            sch.op("scalar", lambda s: [
                s.activation(rstd, lntmp, AF.Exp, scale=0.5),
                s.copy(flsc[:, 8*par+3:8*par+4], rstd)],
                reads=[f"st_lt{par}"], writes=[f"st_rs{par}"], barrier=True,
                tag=f"{tg}ln_ex")
            sch.op("vector", lambda v: v.tensor_scalar(
                xn_bf[:, :], xt[:, :], negmean, rstd, ALU.add, ALU.mult),
                reads=[xt_key, f"st_nm{par}", f"st_rs{par}"], writes=[xn_key],
                barrier=True, tag=f"{tg}ln_o")

        def ln_phase(x_src_ap, x_src_key, ln_t, ln_key, scope, sname):
            sbl = lambda n, sh, dt: scope.enter_context(
                nc.sbuf_tensor(sname + n, sh, dt))
            xt2 = [sbl(f"xt{i}", [128, D], F32) for i in range(2)]
            xnb2 = [sbl(f"xnb{i}", [128, D], BF16) for i in range(2)]
            sq2 = [sbl(f"sq{i}", [128, D], BF16) for i in range(2)]
            for t in range(NT):
                par = t % 2
                xt, xk = xt2[par], f"{sname}xt{par}"
                xnb = xnb2[par]
                sch.op("sync", lambda sy, tt=t, b=xt: sy.dma_start(
                    b[:, :], x_src_ap[tt * 128:(tt + 1) * 128, :]),
                    reads=[f"{x_src_key}:{t}"], writes=[xk], dma=True,
                    tag=f"{sname}ldx{t}")
                ln_tile(xt, xk, xnb, f"{sname}xnb{par}", sq2[par],
                        f"{sname}{t}", par=par)
                for k0 in range(0, 16, 4):
                    transpose4_to(
                        [xnb[:, (k0 + k) * 128:(k0 + k + 1) * 128]
                         for k in range(4)],
                        f"{sname}xnb{par}",
                        bass.AP(ln_t, ((k0 * NT + t) * 128),
                                [[NT * 16 * 128, 128], [NT * 128, 4], [1, 128]]),
                        f"{ln_key}:{t}", tg=f"{sname}lt{t}_{k0}")

        def stage_ffn(x_src_ap, x_src_key, x_out_ap, x_out_key, sname):
            with contextlib.ExitStack() as s_ln:
                ln_t = s_ln.enter_context(
                    nc.sbuf_tensor(sname + "ln_t", [128, NT * 16 * 128], BF16))
                ln_key = sname + "ln_t"
                ln_phase(x_src_ap, x_src_key, ln_t, ln_key, s_ln, sname + "p1")

                if True:
                    sp = s_ln
                    sbl = lambda n, sh, dt: sp.enter_context(
                        nc.sbuf_tensor(sname + n, sh, dt))
                    wg2 = [sbl(f"wg{i}", [128, 16 * 512], BF16) for i in range(2)]
                    wu2 = [sbl(f"wu{i}", [128, 16 * 512], BF16) for i in range(2)]
                    hst2 = [sbl(f"hst{i}", [128, 4 * TOK], BF16) for i in range(1)]
                    hs2 = [sbl(f"hs{i}", [128, 512], F32) for i in range(2)]
                    hbuf2 = [sbl(f"hbuf{i}", [128, 512], F32) for i in range(2)]
                    hsq2 = [sbl(f"hsq{i}", [128, 512], F32) for i in range(2)]
                    tp2 = [sbl(f"tp{i}", [128, 256], F32) for i in range(2)]
                    tq2 = [sbl(f"tq{i}", [128, 256], F32) for i in range(2)]
                    tthr2 = [sbl(f"tthr{i}", [128, 128], F32) for i in range(2)]
                    tmn2 = [sbl(f"tmn{i}", [128, 128], F32) for i in range(2)]
                    tmask2 = [sbl(f"tmask{i}", [128, 512], BF16) for i in range(2)]
                    hsp_bf2 = [sbl(f"hsp_bf{i}", [128, 512], BF16) for i in range(2)]

                    def w_load(j, wbuf, wkey, src):
                        sch.op("gpsimd", lambda g: g.dma_start(
                            wbuf[:, :].rearrange("p (k n) -> p k n", n=512),
                            src[:, j * 512:(j + 1) * 512].rearrange(
                                "(k p) n -> p k n", p=128)),
                            writes=[wkey], dma=True, tag=f"{sname}w{j}")

                    pend = []
                    for j in range(NJ):
                        wg, wgk = wg2[j % 2], f"{sname}wg{j % 2}"
                        wu, wuk = wu2[j % 2], f"{sname}wu{j % 2}"
                        hst, hstk = hst2[0], f"{sname}hst0"
                        w_load(j, wg, wgk, ap_in["w_gate"])
                        w_load(j, wu, wuk, ap_in["w_up"])
                        for t in range(NT):
                            par = (j * NT + t) % 2
                            hs, hbuf, hsq = hs2[par], hbuf2[par], hsq2[par]
                            tp, tq, tthr = tp2[par], tq2[par], tthr2[par]
                            tmn, tmask, hsp_bf = tmn2[par], tmask2[par], hsp_bf2[par]
                            hsk, hbk, hqk = f"hs{par}", f"hbuf{par}", f"hsq{par}"
                            thk, spk = f"tthr{par}", f"hsp_bf{par}"
                            psg, psg_k = next_pmm()
                            psu, psu_k = next_pmm()
                            sch.op("tensor", lambda te, tt=t, w=wg, p=psg: [
                                te.matmul(p[:, :],
                                          ln_t[:, (k * NT + tt) * 128:
                                               (k * NT + tt + 1) * 128],
                                          w[:, k * 512:(k + 1) * 512],
                                          start=(k == 0), stop=(k == 15))
                                for k in range(16)],
                                reads=[f"{ln_key}:{t}", wgk], writes=[psg_k],
                                tag=f"{sname}g{j}_{t}")
                            sch.op("tensor", lambda te, tt=t, w=wu, p=psu: [
                                te.matmul(p[:, :],
                                          ln_t[:, (k * NT + tt) * 128:
                                               (k * NT + tt + 1) * 128],
                                          w[:, k * 512:(k + 1) * 512],
                                          start=(k == 0), stop=(k == 15))
                                for k in range(16)],
                                reads=[f"{ln_key}:{t}", wuk], writes=[psu_k],
                                tag=f"{sname}u{j}_{t}")
                            for fn in pend:
                                fn()
                            pend = []
                            sch.op("scalar", lambda s, p=psg, b=hs: s.activation(
                                b[:, :], p[:, :], AF.Silu),
                                reads=[psg_k], writes=[hsk],
                                tag=f"{sname}si{j}_{t}")
                            sch.op("vector", lambda v, p=psu, a=hs, b=hbuf:
                                   v.tensor_tensor(b[:, :], a[:, :], p[:, :],
                                                   ALU.mult),
                                reads=[hsk, psu_k], writes=[hbk],
                                tag=f"{sname}hm{j}_{t}")
                            sch.op("scalar", lambda s, a=hbuf, b=hsq: s.activation(
                                b[:, :], a[:, :], AF.Square),
                                reads=[hbk], writes=[hqk],
                                tag=f"{sname}sq{j}_{t}")
                            sch.op("vector", lambda v, a=hsq, b=tp, c=tq, d=tthr,
                                   e=tmn: [
                                v.tensor_tensor(b[:, :], a[:, 0:512:2],
                                                a[:, 1:512:2], ALU.max),
                                v.tensor_tensor(c[:, :], a[:, 0:512:2],
                                                a[:, 1:512:2], ALU.min),
                                v.tensor_tensor(d[:, :], b[:, 0:256:2],
                                                b[:, 1:256:2], ALU.min),
                                v.tensor_tensor(e[:, :], c[:, 0:256:2],
                                                c[:, 1:256:2], ALU.max),
                                v.tensor_tensor(d[:, :], d[:, :],
                                                e[:, :], ALU.max)],
                                reads=[hqk], writes=[thk],
                                tag=f"{sname}tr{j}_{t}")
                            sch.op("vector", lambda v, a=hsq, d=tthr, m=tmask,
                                   hb=hbuf, o=hsp_bf: [
                                v.tensor_tensor(
                                    m[:, :].rearrange("p (g k) -> p g k", k=4),
                                    a[:, :].rearrange("p (g k) -> p g k", k=4),
                                    d[:, :].unsqueeze(2).broadcast_to(
                                        [128, 128, 4]),
                                    ALU.is_ge),
                                v.tensor_tensor(o[:, :], hb[:, :],
                                                m[:, :], ALU.mult)],
                                reads=[hqk, thk, hbk], writes=[spk],
                                barrier=True, tag=f"{sname}mk{j}_{t}")

                            def mk_tr(jj, tt, hh, hhk, sb, sbk):
                                def go():
                                    transpose4_to(
                                        [sb[:, ic * 128:(ic + 1) * 128]
                                         for ic in range(4)], sbk,
                                        bass.AP(hh, tt * 128,
                                                [[4 * TOK, 128], [TOK, 4],
                                                 [1, 128]]),
                                        hhk, tg=f"{sname}ht{jj}_{tt}")
                                return go
                            pend.append(mk_tr(j, t, hst, hstk, hsp_bf, spk))
                        for fn in pend:
                            fn()
                        pend = []
                        sch.op("sync", lambda sy, jj=j, b=hst: sy.dma_start(
                            bass.AP(hsp_d, jj * 512 * TOK,
                                    [[TOK, 128], [128 * TOK, 4], [1, TOK]]),
                            b[:, :].rearrange("p (c n) -> p c n", n=TOK)),
                            reads=[hstk], writes=["hsp_d"], dma=True,
                            tag=f"{sname}sp{j}")
            sch.fence()

            with contextlib.ExitStack() as sp:
                sbl = lambda n, sh, dt: sp.enter_context(
                    nc.sbuf_tensor(sname + "b" + n, sh, dt))
                wdn2 = [sbl(f"wdn{i}", [128, 64 * 512], BF16) for i in range(2)]
                hti2 = [sbl(f"hti{i}", [128, 64 * 256], BF16) for i in range(2)]
                xt2 = [sbl(f"xt{i}", [128, 512], F32) for i in range(2)]
                xo2 = [sbl(f"xo{i}", [128, 512], F32) for i in range(2)]
                def wdn_piece(qq, pp):
                    w, wk = wdn2[qq % 2], f"{sname}bwdn{qq % 2}"
                    sch.op("gpsimd", lambda g, qq=qq, pp=pp, w=w: g.dma_start(
                        w[:, pp * 4096:(pp + 1) * 4096].rearrange(
                            "p (k n) -> p k n", n=512),
                        bass.AP(ap_in["w_down"].tensor,
                                pp * 1024 * 2048 + qq * 512,
                                [[2048, 128], [128 * 2048, 8], [1, 512]])),
                        writes=[wk + f"p{pp}"], dma=True,
                        tag=f"{sname}wd{qq}_{pp}")

                def wdn_load(qq):
                    for pp in range(8):
                        wdn_piece(qq, pp)
                wdn_load(0)
                for qr in range(4):
                    wdn, wdnk = wdn2[qr % 2], f"{sname}bwdn{qr % 2}"
                    for quad in range(8):
                        if qr < 3:
                            wdn_piece(qr + 1, quad)
                        hti, htik = hti2[quad % 2], f"{sname}bhti{quad % 2}"
                        sch.op("gpsimd", lambda g, qd=quad, b=hti: g.dma_start(
                            b[:, :].rearrange("p (k n) -> p k n", n=256),
                            bass.AP(hsp_d, qd * 256,
                                    [[TOK, 128], [128 * TOK, 64], [1, 256]])),
                            reads=["hsp_d"], writes=[htik], dma=True,
                            tag=f"{sname}lh{qr}_{quad}")
                        for tq in range(2):
                            t = quad * 2 + tq
                            xt, xtk = xt2[t % 2], f"{sname}bxt{t % 2}"
                            sch.op("sync", lambda sy, tt=t, qq=qr, b=xt:
                                   sy.dma_start(
                                       b[:, :],
                                       x_src_ap[tt * 128:(tt + 1) * 128,
                                                qq * 512:(qq + 1) * 512]),
                                   reads=[f"{x_src_key}:{t}"], writes=[xtk],
                                   dma=True, tag=f"{sname}lx{qr}_{t}")
                            xo, xok = xo2[t % 2], f"{sname}bxo{t % 2}"
                            psd, psd_k = next_pmm()
                            sch.op("tensor", lambda te, b=hti, w=wdn, p=psd,
                                   tb=tq * 128: [
                                te.matmul(p[:, :],
                                          b[:, k * 256 + tb:k * 256 + tb + 128],
                                          w[:, k * 512:(k + 1) * 512],
                                          start=(k == 0), stop=(k == 63))
                                for k in range(64)],
                                reads=[htik] + [f"{wdnk}p{pp}"
                                                for pp in range(8)],
                                writes=[psd_k],
                                tag=f"{sname}d{qr}_{t}")
                            sch.op("vector", lambda v, b=xt, o=xo, p=psd:
                                   v.scalar_tensor_tensor(
                                       o[:, :], p[:, :], 0.5, b[:, :],
                                       ALU.mult, ALU.add),
                                   reads=[psd_k, xtk], writes=[xok],
                                   tag=f"{sname}e{qr}_{t}")
                            sch.op("sync", lambda sy, tt=t, qq=qr, o=xo:
                                   sy.dma_start(
                                       x_out_ap[tt * 128:(tt + 1) * 128,
                                                qq * 512:(qq + 1) * 512],
                                       o[:, :]),
                                   reads=[xok], writes=[f"{x_out_key}:{t}"],
                                   dma=True, tag=f"{sname}sx{qr}_{t}")
            sch.fence()

        def stage_attn(x_src_ap, x_src_key, x_out_ap, x_out_key):
            sname = "C"
            with contextlib.ExitStack() as sc:
                sbl = lambda n, sh, dt: sc.enter_context(
                    nc.sbuf_tensor(sname + n, sh, dt))
                cqT = sbl("cqT", [128, 4 * NT * 128], BF16)
                ckvT = sbl("ckvT", [128, 4 * NT * 128], BF16)
                with contextlib.ExitStack() as s_ln:
                    ln_t = s_ln.enter_context(
                        nc.sbuf_tensor("Cln_t", [128, NT * 16 * 128], BF16))
                    ln_key = "Cln_t"
                    with contextlib.ExitStack() as s_p0:
                        ln_phase(x_src_ap, x_src_key, ln_t, ln_key, s_p0, "Cp0")
                    sch.fence()

                    with contextlib.ExitStack() as sp:
                        sbp = lambda n, sh, dt: sp.enter_context(
                            nc.sbuf_tensor(sname + "1" + n, sh, dt))
                        wc_sb = sbp("wc", [128, 16 * 512], BF16)
                        wckv_sb = sbp("wckv", [128, 16 * 512], BF16)
                        bc_r = sbp("bc", [1, DC], BF16)
                        bckv_r = sbp("bckv", [1, DC], BF16)
                        cq_bf2 = [sbp(f"cq_bf{i}", [128, 512], BF16) for i in range(2)]
                        sch.op("gpsimd", lambda g: [
                            g.dma_start(wc_sb[:, :].rearrange(
                                "p (k n) -> p k n", n=512),
                                ap_in["Wc"].rearrange("(k p) n -> p k n", p=128)),
                            g.dma_start(wckv_sb[:, :].rearrange(
                                "p (k n) -> p k n", n=512),
                                ap_in["Wckv"].rearrange("(k p) n -> p k n", p=128)),
                            g.dma_start(bc_r[:, :], ap_in["bc"].unsqueeze(0)),
                            g.dma_start(bckv_r[:, :], ap_in["bckv"].unsqueeze(0))],
                            writes=["wc", "wckv", "bc", "bckv"], dma=True,
                            n_dma=4, tag="Cwc")
                        pend1 = []
                        for t in range(NT):
                            for ii, (wsb, wk_, br, brk, dstT, dstTk) in enumerate([
                                    (wc_sb, "wc", bc_r, "bc", cqT, "cqT"),
                                    (wckv_sb, "wckv", bckv_r, "bckv", ckvT, "ckvT")]):
                                cq_bf, cqbk = cq_bf2[ii], f"cq_bf{ii}"
                                ps, ps_k = next_pmm()
                                sch.op("tensor", lambda te, tt=t, w=wsb, b=br,
                                       p=ps: [
                                    te.matmul(p[:, :],
                                              ln_t[:, (k * NT + tt) * 128:
                                                   (k * NT + tt + 1) * 128],
                                              w[:, k * 512:(k + 1) * 512],
                                              start=(k == 0), stop=False)
                                    for k in range(16)] + [
                                    te.matmul(p[:, :], ones_row[:, :], b[:, :],
                                              start=False, stop=True)],
                                    reads=[f"{ln_key}:{t}", wk_, brk, "ones"],
                                    writes=[ps_k], tag=f"Ccq{t}")
                                sch.op("scalar", lambda s, p=ps, b=cq_bf: s.copy(
                                    b[:, :], p[:, :]),
                                    reads=[ps_k], writes=[cqbk],
                                    tag=f"Ccqe{t}")
                                for fn in pend1:
                                    fn()
                                pend1 = []

                                def mk_p1tr(tt, ii_, cq_bf=cq_bf, cqbk=cqbk,
                                            dstT=dstT, dstTk=dstTk):
                                    def go():
                                        transpose4_to(
                                            [cq_bf[:, k * 128:(k + 1) * 128]
                                             for k in range(4)], cqbk,
                                            bass.AP(dstT, tt * 128,
                                                    [[4 * NT * 128, 128],
                                                     [NT * 128, 4], [1, 128]]),
                                            f"{dstTk}:{tt}",
                                            tg=f"Ccqt{tt}_{ii_}")
                                    return go
                                pend1.append(mk_p1tr(t, ii))
                        for fn in pend1:
                            fn()
                    sch.fence()

                    with contextlib.ExitStack() as sp:
                        sbp = lambda n, sh, dt: sp.enter_context(
                            nc.sbuf_tensor(sname + "kr" + n, sh, dt))
                        wkr_sb = sbp("wkr", [128, 16 * 1024], BF16)
                        bkr_r = sbp("bkr", [1, 1024], BF16)
                        qrf2 = [sbp(f"qrf{i}", [128, 1024], F32) for i in range(2)]
                        qrb2 = [sbp(f"qrb{i}", [128, 1024], BF16) for i in range(2)]
                        cosb = [sbp(f"cos{i}", [128, 512], F32) for i in range(2)]
                        sinb = [sbp(f"sin{i}", [128, 512], F32) for i in range(2)]
                        ropet = sbp("rt", [128, 512], F32)
                        ropeu = sbp("ru", [128, 512], F32)
                        lo2 = [sbp(f"lo{i}", [64, 2048], BF16) for i in range(2)]
                        sch.op("gpsimd", lambda g: [
                            g.dma_start(wkr_sb[:, :].rearrange(
                                "p (k n) -> p k n", n=1024),
                                ap_in["Wkr"].rearrange("(k p) n -> p k n", p=128)),
                            g.dma_start(bkr_r[:, :], ap_in["bkr"].unsqueeze(0))],
                            writes=["wkr", "krb"], dma=True, n_dma=2, tag="Ckrw")
                        pendk = []
                        for t in range(NT):
                            par = t % 2
                            qrf, qrb = qrf2[par], qrb2[par]
                            qrfk, qrbk = f"krqrf{par}", f"krqrb{par}"
                            lo, lok = lo2[par], f"krlo{par}"
                            cs, csk = cosb[par], f"krcos{par}"
                            sn, snk = sinb[par], f"krsin{par}"
                            sch.op("sync", lambda sy, tt=t, a=cs, b=sn: [
                                sy.dma_start(a[:, :], ap_in["ropecos"][
                                    tt * 128:(tt + 1) * 128, :]),
                                sy.dma_start(b[:, :], ap_in["ropesin"][
                                    tt * 128:(tt + 1) * 128, :])],
                                writes=[csk, snk], dma=True, n_dma=2,
                                tag=f"Ckrc{t}")
                            for nck in range(2):
                                ps, ps_k = next_pmm()
                                sch.op("tensor", lambda te, tt=t, cc=nck, p=ps: [
                                    te.matmul(p[:, :],
                                              ln_t[:, (k * NT + tt) * 128:
                                                   (k * NT + tt + 1) * 128],
                                              wkr_sb[:, k * 1024 + cc * 512:
                                                     k * 1024 + cc * 512 + 512],
                                              start=(k == 0), stop=False)
                                    for k in range(16)] + [
                                    te.matmul(p[:, :], ones_row[:, :],
                                              bkr_r[:, cc * 512:(cc + 1) * 512],
                                              start=False, stop=True)],
                                    reads=[f"{ln_key}:{t}", "wkr", "krb", "ones"],
                                    writes=[ps_k], tag=f"Ckr{t}_{nck}")
                                sch.op("scalar", lambda s, cc=nck, p=ps, b=qrf:
                                       s.copy(b[:, cc * 512:(cc + 1) * 512],
                                              p[:, :]),
                                       reads=[ps_k], writes=[qrfk],
                                       tag=f"Ckre{t}_{nck}")
                            for fn in pendk:
                                fn()
                            pendk = []
                            q3 = qrf[:, :].rearrange("p (h r) -> p h r", r=64)
                            o3 = qrb[:, :].rearrange("p (h r) -> p h r", r=64)
                            c3 = cs[:, :].rearrange("p (h r) -> p h r", r=32)
                            s3 = sn[:, :].rearrange("p (h r) -> p h r", r=32)
                            t3 = ropet[:, :].rearrange("p (h r) -> p h r", r=32)
                            u3 = ropeu[:, :].rearrange("p (h r) -> p h r", r=32)
                            sch.op("vector", lambda v, t3=t3, u3=u3, q3=q3,
                                   c3=c3, s3=s3, o3=o3: [
                                v.tensor_tensor(t3, q3[:, :, 0:32], c3, ALU.mult),
                                v.tensor_tensor(u3, q3[:, :, 32:64], s3, ALU.mult),
                                v.tensor_tensor(o3[:, :, 0:32], t3, u3,
                                                ALU.subtract),
                                v.tensor_tensor(t3, q3[:, :, 32:64], c3, ALU.mult),
                                v.tensor_tensor(u3, q3[:, :, 0:32], s3, ALU.mult),
                                v.tensor_tensor(o3[:, :, 32:64], t3, u3, ALU.add)],
                                reads=[qrfk, csk, snk], writes=[qrbk],
                                tag=f"Ckro{t}")

                            def mk_krtr(tt, qrb=qrb, qrbk=qrbk, lo=lo, lok=lok):
                                def go():
                                    for h0 in range(0, H, 4):
                                        transpose4_to(
                                            [qrb[:, (h0 + h) * 64:
                                                 (h0 + h + 1) * 64]
                                             for h in range(4)], qrbk,
                                            bass.AP(lo, h0,
                                                    [[2048, 64], [1, 4],
                                                     [16, 128]]),
                                            lok, nparts=64,
                                            tg=f"Ckrt{tt}_{h0}")
                                    sch.op("sync", lambda sy, b=lo: sy.dma_start(
                                        bass.AP(qk_d, (tt * 384 + 320) * TOK,
                                                [[TOK, 64], [1, 2048]]),
                                        b[0:64, :]),
                                        reads=[lok], writes=[f"qk_d:{tt}"],
                                        dma=True, tag=f"Ckrs{tt}")
                                return go
                            pendk.append(mk_krtr(t))
                        for fn in pendk:
                            fn()
                    sch.fence()

                with contextlib.ExitStack() as sp:
                    sbp = lambda n, sh, dt: sp.enter_context(
                        nc.sbuf_tensor(sname + "m" + n, sh, dt))
                    wq_sb = sbp("wq", [128, 4 * 2048], BF16)
                    wqr_sb = sbp("wqr", [128, 4 * 1024], BF16)
                    wk_sb = sbp("wk", [128, 4 * 2048], BF16)
                    wv_sb = sbp("wv", [128, 4 * 2048], BF16)
                    bq_r = sbp("bq", [1, D], BF16)
                    bqr_r = sbp("bqr", [1, 1024], BF16)
                    bk_r = sbp("bk", [1, D], BF16)
                    bv_r = sbp("bv", [1, D], BF16)
                    qf2 = [sbp(f"qf{i}", [128, 2048], BF16) for i in range(2)]
                    kf2 = [sbp(f"kf{i}", [128, 2048], BF16) for i in range(2)]
                    vnb2 = [sbp(f"vnb{i}", [128, 2048], BF16) for i in range(2)]
                    qrf2 = [sbp(f"qrf{i}", [128, 1024], F32) for i in range(2)]
                    qrb2 = [sbp(f"qrb{i}", [128, 1024], BF16) for i in range(2)]
                    cosb = [sbp(f"cos{i}", [128, 512], F32) for i in range(2)]
                    sinb = [sbp(f"sin{i}", [128, 512], F32) for i in range(2)]
                    ropet = sbp("rt", [128, 512], F32)
                    ropeu = sbp("ru", [128, 512], F32)
                    hi2 = [sbp(f"hi{i}", [128, 2048], BF16) for i in range(2)]
                    lo2 = [sbp(f"lo{i}", [64, 2048], BF16) for i in range(2)]
                    sch.op("gpsimd", lambda g: [
                        g.dma_start(wq_sb[:, :].rearrange(
                            "p (k n) -> p k n", n=2048),
                            ap_in["Wcq"].rearrange("(k p) n -> p k n", p=128)),
                        g.dma_start(wqr_sb[:, :].rearrange(
                            "p (k n) -> p k n", n=1024),
                            ap_in["Wqr"].rearrange("(k p) n -> p k n", p=128)),
                        g.dma_start(wk_sb[:, :].rearrange(
                            "p (k n) -> p k n", n=2048),
                            ap_in["Wck"].rearrange("(k p) n -> p k n", p=128)),
                        g.dma_start(wv_sb[:, :].rearrange(
                            "p (k n) -> p k n", n=2048),
                            ap_in["Wv"].rearrange("(k p) n -> p k n", p=128)),
                        g.dma_start(bq_r[:, :], ap_in["bcq"].unsqueeze(0)),
                        g.dma_start(bqr_r[:, :], ap_in["bqr"].unsqueeze(0)),
                        g.dma_start(bk_r[:, :], ap_in["bck"].unsqueeze(0)),
                        g.dma_start(bv_r[:, :], ap_in["bv"].unsqueeze(0))],
                        writes=["wq", "wqr", "wk", "wv", "bq", "bqr", "bk",
                                "bv"], dma=True, n_dma=8, tag="Cmw")
                    pendm = []
                    for t in range(NT):
                        par = t % 2
                        qf, kf, vnb = qf2[par], kf2[par], vnb2[par]
                        qfk, kfk, vnbk = f"mqf{par}", f"mkf{par}", f"mvnb{par}"
                        qrf, qrb = qrf2[par], qrb2[par]
                        qrfk, qrbk = f"mqrf{par}", f"mqrb{par}"
                        hi, lo = hi2[par], lo2[par]
                        hik, lok = f"mhi{par}", f"mlo{par}"
                        cs, csk = cosb[par], f"mcos{par}"
                        sn, snk = sinb[par], f"msin{par}"
                        sch.op("sync", lambda sy, tt=t, a=cs, b=sn: [
                            sy.dma_start(a[:, :], ap_in["ropecos"][
                                tt * 128:(tt + 1) * 128, :]),
                            sy.dma_start(b[:, :], ap_in["ropesin"][
                                tt * 128:(tt + 1) * 128, :])],
                            writes=[csk, snk], dma=True, n_dma=2,
                            tag=f"Cmrc{t}")

                        def proj4(tt, cTk_, w_sb, wk_, b_r, brk, out, outk, tg):
                            cT_ = cqT if cTk_ == "cqT" else ckvT
                            for nck in range(4):
                                ps, ps_k = next_pmm()
                                sch.op("tensor", lambda te, cc=nck, p=ps,
                                       w=w_sb, b=b_r, c=cT_: [
                                    te.matmul(p[:, :],
                                              c[:, (k * NT + tt) * 128:
                                                (k * NT + tt + 1) * 128],
                                              w[:, k * 2048 + cc * 512:
                                                k * 2048 + cc * 512 + 512],
                                              start=(k == 0), stop=False)
                                    for k in range(4)] + [
                                    te.matmul(p[:, :], ones_row[:, :],
                                              b[:, cc * 512:(cc + 1) * 512],
                                              start=False, stop=True)],
                                    reads=[f"{cTk_}:{tt}", wk_, brk, "ones"],
                                    writes=[ps_k], tag=f"{tg}{tt}_{nck}")
                                sch.op("scalar", lambda s, cc=nck, p=ps, b=out:
                                       s.copy(b[:, cc * 512:(cc + 1) * 512],
                                              p[:, :]),
                                       reads=[ps_k], writes=[outk],
                                       tag=f"{tg}e{tt}_{nck}")

                        proj4(t, "cqT", wq_sb, "wq", bq_r, "bq", qf, qfk, "Cmq")
                        for nck in range(2):
                            ps, ps_k = next_pmm()
                            sch.op("tensor", lambda te, tt=t, cc=nck, p=ps: [
                                te.matmul(p[:, :],
                                          cqT[:, (k * NT + tt) * 128:
                                              (k * NT + tt + 1) * 128],
                                          wqr_sb[:, k * 1024 + cc * 512:
                                                 k * 1024 + cc * 512 + 512],
                                          start=(k == 0), stop=False)
                                for k in range(4)] + [
                                te.matmul(p[:, :], ones_row[:, :],
                                          bqr_r[:, cc * 512:(cc + 1) * 512],
                                          start=False, stop=True)],
                                reads=[f"cqT:{t}", "wqr", "bqr", "ones"],
                                writes=[ps_k], tag=f"Cmqr{t}_{nck}")
                            sch.op("scalar", lambda s, cc=nck, p=ps, b=qrf:
                                   s.copy(b[:, cc * 512:(cc + 1) * 512],
                                          p[:, :]),
                                   reads=[ps_k], writes=[qrfk],
                                   tag=f"Cmqre{t}_{nck}")
                        proj4(t, "ckvT", wk_sb, "wk", bk_r, "bk", kf, kfk, "Cmk")
                        proj4(t, "ckvT", wv_sb, "wv", bv_r, "bv", vnb, vnbk, "Cmv")
                        for fn in pendm:
                            fn()
                        pendm = []
                        q3 = qrf[:, :].rearrange("p (h r) -> p h r", r=64)
                        o3 = qrb[:, :].rearrange("p (h r) -> p h r", r=64)
                        c3 = cs[:, :].rearrange("p (h r) -> p h r", r=32)
                        s3 = sn[:, :].rearrange("p (h r) -> p h r", r=32)
                        t3 = ropet[:, :].rearrange("p (h r) -> p h r", r=32)
                        u3 = ropeu[:, :].rearrange("p (h r) -> p h r", r=32)
                        sch.op("vector", lambda v, t3=t3, u3=u3, q3=q3,
                               c3=c3, s3=s3, o3=o3: [
                            v.tensor_tensor(t3, q3[:, :, 0:32], c3, ALU.mult),
                            v.tensor_tensor(u3, q3[:, :, 32:64], s3, ALU.mult),
                            v.tensor_tensor(o3[:, :, 0:32], t3, u3, ALU.subtract),
                            v.tensor_tensor(t3, q3[:, :, 32:64], c3, ALU.mult),
                            v.tensor_tensor(u3, q3[:, :, 0:32], s3, ALU.mult),
                            v.tensor_tensor(o3[:, :, 32:64], t3, u3, ALU.add)],
                            reads=[qrfk, csk, snk], writes=[qrbk],
                            tag=f"Cmro{t}")
                        sch.op("sync", lambda sy, tt=t, b=vnb: sy.dma_start(
                            vscr_d.ap()[tt * 128:(tt + 1) * 128, :], b[:, :]),
                            reads=[vnbk], writes=[f"vscr:{t}"], dma=True,
                            tag=f"Cmvs{t}")

                        def mk_mtr(tt, qf=qf, qfk=qfk, kf=kf, kfk=kfk,
                                   qrb=qrb, qrbk=qrbk, hi=hi, lo=lo,
                                   hik=hik, lok=lok):
                            def go():
                                for h0 in range(0, H, 4):
                                    transpose4_to(
                                        [qf[:, (h0 + h) * 128:
                                            (h0 + h + 1) * 128]
                                         for h in range(4)], qfk,
                                        bass.AP(hi, h0, [[2048, 128], [1, 4],
                                                         [16, 128]]),
                                        hik, tg=f"Cmth{tt}_{h0}")
                                    transpose4_to(
                                        [qrb[:, (h0 + h) * 64:
                                             (h0 + h + 1) * 64]
                                         for h in range(4)], qrbk,
                                        bass.AP(lo, h0, [[2048, 64], [1, 4],
                                                         [16, 128]]),
                                        lok, nparts=64, tg=f"Cmtl{tt}_{h0}")
                                sch.op("sync", lambda sy, a=hi, b=lo: [
                                    sy.dma_start(
                                        bass.AP(qk_d, (tt * 384) * TOK,
                                                [[TOK, 128], [1, 2048]]),
                                        a[:, :]),
                                    sy.dma_start(
                                        bass.AP(qk_d, (tt * 384 + 128) * TOK,
                                                [[TOK, 64], [1, 2048]]),
                                        b[0:64, :])],
                                    reads=[hik, lok], writes=[f"qk_d:{tt}"],
                                    dma=True, n_dma=2, tag=f"Cmqs{tt}")
                                for h0 in range(0, H, 4):
                                    transpose4_to(
                                        [kf[:, (h0 + h) * 128:
                                            (h0 + h + 1) * 128]
                                         for h in range(4)], kfk,
                                        bass.AP(hi, h0, [[2048, 128], [1, 4],
                                                         [16, 128]]),
                                        hik, tg=f"Cmtk{tt}_{h0}")
                                sch.op("sync", lambda sy, a=hi: sy.dma_start(
                                    bass.AP(qk_d, (tt * 384 + 192) * TOK,
                                            [[TOK, 128], [1, 2048]]),
                                    a[:, :]),
                                    reads=[hik], writes=[f"qk_d:{tt}"],
                                    dma=True, tag=f"Cmks{tt}")
                            return go
                        pendm.append(mk_mtr(t))
                    for fn in pendm:
                        fn()
                sch.fence()
            sch.fence()

            with contextlib.ExitStack() as sp:
                sbl = lambda n, sh, dt: sp.enter_context(
                    nc.sbuf_tensor(sname + "g" + n, sh, dt))
                outT = sbl("outT", [128, NT * 2048], BF16)
                qhi2 = [sbl(f"qhi{i}", [128, 2048], BF16) for i in range(2)]
                qlo2 = [sbl(f"qlo{i}", [64, 2048], BF16) for i in range(2)]
                khi2 = [sbl(f"khi{i}", [128, 2048], BF16) for i in range(2)]
                klo2 = [sbl(f"klo{i}", [64, 2048], BF16) for i in range(2)]
                vg2 = [sbl(f"vg{i}", [128, 2048], BF16) for i in range(2)]
                g4_2 = [sbl(f"g4_{i}", [128, 512], F32) for i in range(2)]
                m4_2 = [sbl(f"m4_{i}", [128, 4], F32) for i in range(2)]
                bm4_2 = [sbl(f"bm4_{i}", [128, 4], F32) for i in range(2)]
                ein_2 = [sbl(f"ein_{i}", [128, 512], F32) for i in range(2)]
                ebf_2 = [sbl(f"ebf_{i}", [128, 512], BF16) for i in range(2)]
                rs4_2 = [sbl(f"rs4_{i}", [128, 4], F32) for i in range(2)]
                ri4_2 = [sbl(f"ri4_{i}", [128, 4], F32) for i in range(2)]
                at4_2 = [sbl(f"at4_{i}", [128, 512], BF16) for i in range(2)]
                av4_2 = [sbl(f"av4_{i}", [128, 512], BF16) for i in range(2)]
                pend7 = []
                for t in range(NT):
                    qhi, qhik = qhi2[t % 2], f"gqhi{t % 2}"
                    qlo, qlok = qlo2[t % 2], f"gqlo{t % 2}"
                    khi, khik = khi2[t % 2], f"gkhi{t % 2}"
                    klo, klok = klo2[t % 2], f"gklo{t % 2}"
                    vg, vgk = vg2[t % 2], f"gvg{t % 2}"
                    sch.op("sync", lambda sy, tt=t, a=qhi, b=qlo, c=khi, d=klo: [
                        sy.dma_start(a[:, :], bass.AP(
                            qk_d, (tt * 384) * TOK, [[TOK, 128], [1, 2048]])),
                        sy.dma_start(b[0:64, :], bass.AP(
                            qk_d, (tt * 384 + 128) * TOK, [[TOK, 64], [1, 2048]])),
                        sy.dma_start(c[:, :], bass.AP(
                            qk_d, (tt * 384 + 192) * TOK, [[TOK, 128], [1, 2048]])),
                        sy.dma_start(d[0:64, :], bass.AP(
                            qk_d, (tt * 384 + 320) * TOK, [[TOK, 64], [1, 2048]]))],
                        reads=[f"qk_d:{t}"], writes=[qhik, qlok, khik, klok],
                        dma=True, n_dma=4, tag=f"Cgl{t}")
                    sch.op("gpsimd", lambda g, tt=t, b=vg: g.dma_start(
                        b[:, :], bass.AP(vscr_d, tt * 128 * D,
                                         [[D, 8], [128, 16], [8 * D, 16],
                                          [1, 128]])),
                        reads=[f"vscr:{t}"], writes=[vgk], dma=True,
                        tag=f"Cgv{t}")
                    for blk in range(4):
                        gp = blk % 2
                        g4, m4, bm4 = g4_2[gp], m4_2[gp], bm4_2[gp]
                        ein, ebf = ein_2[gp], ebf_2[gp]
                        rs4, ri4 = rs4_2[gp], ri4_2[gp]
                        at4, av4 = at4_2[gp], av4_2[gp]
                        g4k, m4k, bm4k = f"g4_{gp}", f"m4_{gp}", f"bm4_{gp}"
                        eink, ebfk = f"ein_{gp}", f"ebf_{gp}"
                        rs4k, ri4k = f"rs4_{gp}", f"ri4_{gp}"
                        at4k, av4k = f"at4_{gp}", f"av4_{gp}"
                        psg, psg_k = next_pmm()
                        sch.op("tensor", lambda te, b_=blk, a=qhi, b=qlo,
                               c=khi, d=klo, p=psg: [
                            mm for g_ in range(4) for mm in (
                                te.matmul(p[:, g_ * 128:(g_ + 1) * 128],
                                          a[:, (b_ * 4 + g_) * 128:
                                            (b_ * 4 + g_ + 1) * 128],
                                          c[:, (b_ * 4 + g_) * 128:
                                            (b_ * 4 + g_ + 1) * 128],
                                          start=True, stop=False),
                                te.matmul(p[:, g_ * 128:(g_ + 1) * 128],
                                          b[0:64, (b_ * 4 + g_) * 128:
                                            (b_ * 4 + g_ + 1) * 128],
                                          d[0:64, (b_ * 4 + g_) * 128:
                                            (b_ * 4 + g_ + 1) * 128],
                                          start=False, stop=True))],
                            reads=[qhik, qlok, khik, klok], writes=[psg_k],
                            tag=f"Cgr{t}_{blk}")
                        for fn in pend7:
                            fn()
                        pend7 = []
                        sch.op("vector", lambda v, p=psg, a=g4, b=m4: [
                            v.tensor_tensor(
                                a[:, :].rearrange("p (g n) -> p g n", n=128),
                                p[:, :].rearrange("p (g n) -> p g n", n=128),
                                maskneg[:, :].unsqueeze(1).broadcast_to(
                                    [128, 4, 128]), ALU.add),
                            v.reduce_max(
                                b[:, :].rearrange("p (g n) -> p g n", n=1),
                                a[:, :].rearrange("p (g n) -> p g n", n=128),
                                axis=AX.X)],
                            reads=[psg_k, "maskneg"], writes=[g4k, m4k],
                            tag=f"Cgm{t}_{blk}")
                        sch.op("vector", lambda v, a=m4, c=g4, o=ein:
                               v.tensor_tensor(
                                   o[:, :].rearrange("p (g n) -> p g n", n=128),
                                   c[:, :].rearrange("p (g n) -> p g n", n=128),
                                   a[:, :].unsqueeze(2).broadcast_to(
                                       [128, 4, 128]),
                                   ALU.subtract),
                            reads=[m4k, g4k], writes=[eink],
                            barrier=True, tag=f"Cgs{t}_{blk}")
                        sch.op("scalar", lambda s, a=ein, b=ebf: s.activation(
                            b[:, :], a[:, :], AF.Exp, scale=SCALER),
                            reads=[eink], writes=[ebfk], tag=f"Cge{t}_{blk}")
                        sch.op("vector", lambda v, a=ebf, b=rs4: v.reduce_sum(
                            b[:, :].rearrange("p (g n) -> p g n", n=1),
                            a[:, :].rearrange("p (g n) -> p g n", n=128),
                            axis=AX.X),
                            reads=[ebfk], writes=[rs4k], tag=f"Cgr2{t}_{blk}")
                        sch.op("vector", lambda v, a=rs4, b=ri4: v.reciprocal(
                            b[:, :], a[:, :]),
                            reads=[rs4k], writes=[ri4k], barrier=True,
                            tag=f"Cgc{t}_{blk}")

                        def mk_av(tt, b_, vg=vg, vgk=vgk, ebf=ebf, ebfk=ebfk,
                                  at4=at4, at4k=at4k, av4=av4, av4k=av4k,
                                  ri4=ri4, ri4k=ri4k):
                            def go():
                                transpose4_to(
                                    [ebf[:, g_ * 128:(g_ + 1) * 128]
                                     for g_ in range(4)], ebfk,
                                    at4[:, :].rearrange("p (g n) -> p g n",
                                                        n=128),
                                    at4k, tg=f"Cgt{tt}_{b_}")
                                psa, psa_k = next_pmm()
                                sch.op("tensor", lambda te, p=psa: [
                                    te.matmul(p[:, g_ * 128:(g_ + 1) * 128],
                                              at4[:, g_ * 128:(g_ + 1) * 128],
                                              vg[:, (b_ * 4 + g_) * 128:
                                                 (b_ * 4 + g_ + 1) * 128],
                                              start=True, stop=True)
                                    for g_ in range(4)],
                                    reads=[at4k, vgk], writes=[psa_k],
                                    tag=f"Cga{tt}_{b_}")
                                sch.op("vector", lambda v, p=psa, o=av4:
                                       v.tensor_tensor(
                                           o[:, :].rearrange(
                                               "p (g n) -> p g n", n=128),
                                           p[:, :].rearrange(
                                               "p (g n) -> p g n", n=128),
                                           ri4[:, :].unsqueeze(2).broadcast_to(
                                               [128, 4, 128]), ALU.mult),
                                    reads=[psa_k, ri4k], writes=[av4k],
                                    tag=f"Cgae{tt}_{b_}")
                                transpose4_to(
                                    [av4[:, g_ * 128:(g_ + 1) * 128]
                                     for g_ in range(4)], av4k,
                                    outT[:, tt * 2048 + b_ * 512:
                                         tt * 2048 + (b_ + 1) * 512].rearrange(
                                        "p (g n) -> p g n", n=128),
                                    f"outT:{tt}", tg=f"Cgo{tt}_{b_}")
                            return go
                        pend7.append(mk_av(t, blk))
                for fn in pend7:
                    fn()
                pend7 = []

                for fn in pend7:
                    fn()
                pend7 = []
                wo2 = [sbl(f"wo{i}", [128, 16 * 512], BF16) for i in range(2)]
                bo_r = sbl("bo", [1, D], BF16)
                xr2 = [sbl(f"xr{i}", [128, 512], F32) for i in range(2)]
                xw2 = [sbl(f"xw{i}", [128, 512], F32) for i in range(2)]
                sch.op("gpsimd", lambda g: g.dma_start(
                    bo_r[:, :], ap_in["bo"].unsqueeze(0)),
                    writes=["bo"], dma=True, tag="Cbo")
                for ncw in range(4):
                    wo, wok = wo2[ncw % 2], f"gwo{ncw % 2}"
                    sch.op("gpsimd", lambda g, cc=ncw, b=wo: g.dma_start(
                        b[:, :].rearrange("p (k n) -> p k n", n=512),
                        ap_in["Wo"][:, cc * 512:(cc + 1) * 512].rearrange(
                            "(k p) n -> p k n", p=128)),
                        writes=[wok], dma=True, tag=f"Cgwo{ncw}")
                    for t in range(NT):
                        xr, xrk = xr2[t % 2], f"gxr{t % 2}"
                        sch.op("sync", lambda sy, tt=t, cc=ncw, b=xr:
                               sy.dma_start(
                                   b[:, :], x_src_ap[tt * 128:(tt + 1) * 128,
                                                     cc * 512:(cc + 1) * 512]),
                               reads=[f"{x_src_key}:{t}"], writes=[xrk],
                               dma=True, tag=f"Cglx{ncw}_{t}")
                        pso, pso_k = next_pmm()
                        sch.op("tensor", lambda te, tt=t, cc=ncw, w=wo, p=pso: [
                            te.matmul(p[:, :],
                                      bass.AP(outT, tt * 2048 + i,
                                              [[NT * 2048, 128], [16, 128]]),
                                      w[:, i * 512:(i + 1) * 512],
                                      start=(i == 0), stop=False)
                            for i in range(16)] + [
                            te.matmul(p[:, :], ones_row[:, :],
                                      bo_r[:, cc * 512:(cc + 1) * 512],
                                      start=False, stop=True)],
                            reads=[f"outT:{t}", wok, "bo", "ones"],
                            writes=[pso_k], tag=f"Cgw{ncw}_{t}")
                        xw, xwk = xw2[t % 2], f"gxw{t % 2}"
                        sch.op("vector", lambda v, b=xr, o=xw, p=pso:
                               v.tensor_tensor(o[:, :], p[:, :], b[:, :],
                                               ALU.add),
                               reads=[pso_k, xrk], writes=[xwk],
                               tag=f"Cgad{ncw}_{t}")
                        sch.op("sync", lambda sy, tt=t, cc=ncw, o=xw:
                               sy.dma_start(
                                   x_out_ap[tt * 128:(tt + 1) * 128,
                                            cc * 512:(cc + 1) * 512], o[:, :]),
                               reads=[xwk], writes=[f"{x_out_key}:{t}"],
                               dma=True, tag=f"Cgsx{ncw}_{t}")
            sch.fence()

        xm1 = xm1_d.ap()
        xm2 = xm2_d.ap()
        stage_ffn(ap_in["x"], "x_in", xm1, "xm1", "A")
        if "xmid1" in dbg_h:
            sch.op("sync", lambda sy: [
                sy.dma_start(dbg_h["xmid1"][t * 128:(t + 1) * 128, :],
                             xm1[t * 128:(t + 1) * 128, :]) for t in range(NT)],
                reads=[f"xm1:{t}" for t in range(NT)], dma=True, n_dma=NT,
                tag="dbg1")
        stage_attn(xm1, "xm1", xm2, "xm2")
        if "qk" in dbg_h:
            sch.op("sync", lambda sy: sy.dma_start(
                dbg_h["qk"][:, :].bitcast(BF16)[:, 0:2048],
                bass.AP(qk_d, 0, [[2048, 2048], [1, 2048]])),
                reads=["qk_d"], dma=True, tag="dbgqk")
        if "vs" in dbg_h:
            sch.op("sync", lambda sy: sy.dma_start(
                dbg_h["vs"][:, :].bitcast(BF16)[:, 0:2048],
                bass.AP(vscr_d, 0, [[2048, 2048], [1, 2048]])),
                reads=["vscr"], dma=True, tag="dbgvs")
        if "xmid2" in dbg_h:
            sch.op("sync", lambda sy: [
                sy.dma_start(dbg_h["xmid2"][t * 128:(t + 1) * 128, :],
                             xm2[t * 128:(t + 1) * 128, :]) for t in range(NT)],
                reads=[f"xm2:{t}" for t in range(NT)], dma=True, n_dma=NT,
                tag="dbg2")
        stage_ffn(xm2, "xm2", out_h, "x_out", "D")

        sch.emit(block)
    return nc


_NC_CACHE = {}


def _get_nc(debug=()):
    key = tuple(debug)
    if key not in _NC_CACHE:
        _NC_CACHE[key] = build_nc(debug)
    return _NC_CACHE[key]


def _host_constants():
    ident = np.eye(128, dtype=np.float32)
    maskneg = np.full((128, 128), -1e30, np.float32)
    for s8 in range(8):
        maskneg[s8 * 16:(s8 + 1) * 16, s8 * 16:(s8 + 1) * 16] = 0.0
    inv_freq = (1.0 / 10000.0 ** (np.arange(0, 64, 2, dtype=np.float32) / 64.0)
                ).astype(np.float32)
    t = np.arange(NCORES * TOK, dtype=np.float32)
    fr = t[:, None] * inv_freq[None, :]
    cos = np.tile(np.cos(fr), (1, H)).astype(np.float32)
    sin = np.tile(np.sin(fr), (1, H)).astype(np.float32)
    return ident, maskneg, cos, sin


def kernel(**inputs):
    x = np.ascontiguousarray(
        np.asarray(inputs["L0_attn_h_t"], dtype=np.float32).reshape(
            NCORES * TOK, D))
    ident, maskneg, cos, sin = _host_constants()
    wmap = {
        "Wc": "L0_attn_Wc", "bc": "L0_attn_bc", "Wcq": "L0_attn_Wcq",
        "bcq": "L0_attn_bcq", "Wqr": "L0_attn_Wqr", "bqr": "L0_attn_bqr",
        "Wckv": "L0_attn_Wckv", "bckv": "L0_attn_bckv", "Wck": "L0_attn_Wck",
        "bck": "L0_attn_bck", "Wkr": "L0_attn_Wkr", "bkr": "L0_attn_bkr",
        "Wv": "L0_attn_Wv", "bv": "L0_attn_bv", "Wo": "L0_attn_Wo",
        "bo": "L0_attn_bo", "w_gate": "L0_ffn_w_gate", "w_up": "L0_ffn_w_up",
        "w_down": "L0_ffn_w_down",
    }
    weights = {k: np.ascontiguousarray(np.asarray(inputs[v], dtype=np.float32))
               for k, v in wmap.items()}
    nc = _get_nc()
    in_maps = []
    for c in range(NCORES):
        m = dict(weights)
        m["x"] = x[c * TOK:(c + 1) * TOK]
        m["ident"] = ident
        m["maskneg"] = maskneg
        m["ropecos"] = np.ascontiguousarray(cos[c * TOK:(c + 1) * TOK])
        m["ropesin"] = np.ascontiguousarray(sin[c * TOK:(c + 1) * TOK])
        in_maps.append(m)
    res = run_bass_kernel_spmd(nc, in_maps, core_ids=list(range(NCORES)))
    out = np.concatenate([r["out"] for r in res.results], axis=0)
    return out.reshape(1, NCORES * TOK, D).astype(np.float32)
